# version 03
# baseline (speedup 1.0000x reference)
import numpy as np
import ml_dtypes
import concourse.bass as bass
import concourse.mybir as mybir
from concourse.bass_utils import run_bass_kernel_spmd

F32 = mybir.dt.float32
BF16 = mybir.dt.bfloat16
I32 = mybir.dt.int32
U32 = mybir.dt.uint32
AF = mybir.ActivationFunctionType
ALU = mybir.AluOpType

SEQ = 8192
DM = 4096
NH = 4
HD = 1024
II = 11008
NC_I = II // 128
NC_D = DM // 128
TLOC = 2048
NG = 8
QT = 256
R = 32


class P:
    def __init__(self, sem):
        self.sem = sem
        self.n = 0

    def inc(self, inst, k=1):
        inst.then_inc(self.sem, k)
        self.n += k
        return self.n


class W:
    def __init__(self):
        self.seen = {}

    def wait(self, eng, p, v):
        if v <= 0:
            return
        key = (id(eng), id(p))
        if self.seen.get(key, -1) >= v:
            return
        eng.wait_ge(p.sem, v)
        self.seen[key] = v


def emit(nc, dbg=()):
    x_seq = nc.dram_tensor("x_seq", [SEQ, DM], F32, kind="ExternalInput")
    x_q = nc.dram_tensor("x_q", [TLOC, DM], F32, kind="ExternalInput")
    gq = nc.dram_tensor("gq", [II, DM], I32, kind="ExternalInput")
    uq = nc.dram_tensor("uq", [II, DM], I32, kind="ExternalInput")
    dq = nc.dram_tensor("dq", [DM, II], I32, kind="ExternalInput")
    gsc = nc.dram_tensor("gsc", [NC_I, 128], F32, kind="ExternalInput")
    usc = nc.dram_tensor("usc", [NC_I, 128], F32, kind="ExternalInput")
    dsc = nc.dram_tensor("dsc", [NC_D, 128], F32, kind="ExternalInput")
    ga = nc.dram_tensor("ga", [DM, R], F32, kind="ExternalInput")
    gb = nc.dram_tensor("gb", [R, II], F32, kind="ExternalInput")
    ua = nc.dram_tensor("ua", [DM, R], F32, kind="ExternalInput")
    ub = nc.dram_tensor("ub", [R, II], F32, kind="ExternalInput")
    da = nc.dram_tensor("da", [II, R], F32, kind="ExternalInput")
    db = nc.dram_tensor("db", [R, DM], F32, kind="ExternalInput")
    bvq = nc.dram_tensor("bvq", [NH, HD], F32, kind="ExternalInput")
    bvk = nc.dram_tensor("bvk", [NH, HD], F32, kind="ExternalInput")
    bvv = nc.dram_tensor("bvv", [NH, HD], F32, kind="ExternalInput")
    out_q = nc.dram_tensor("out_q", [TLOC, DM], F32, kind="ExternalOutput")

    wgT = nc.dram_tensor("wgT", [NC_I, 128, DM], BF16, kind="Internal")
    wuT = nc.dram_tensor("wuT", [NC_I, 128, DM], BF16, kind="Internal")
    wdT = nc.dram_tensor("wdT", [NC_D, 128, II], BF16, kind="Internal")
    xnfm = nc.dram_tensor("xnfm", [NC_D, 128, SEQ], BF16, kind="Internal")
    vs = nc.dram_tensor("vs", [SEQ, DM], BF16, kind="Internal")
    hs = nc.dram_tensor("hs", [NC_I, 128, TLOC], BF16, kind="Internal")
    ss = nc.dram_tensor("ss", [NC_D, 128, TLOC], BF16, kind="Internal")
    ps_ = nc.dram_tensor("ps", [DM, TLOC], BF16, kind="Internal")
    mvec = nc.dram_tensor("mvec", [NH, HD], F32, kind="Internal")
    vbvec = nc.dram_tensor("vbvec", [NH, HD], F32, kind="Internal")
    gb_bf = nc.dram_tensor("gb_bf", [R, II], BF16, kind="Internal")
    ub_bf = nc.dram_tensor("ub_bf", [R, II], BF16, kind="Internal")
    db_bf = nc.dram_tensor("db_bf", [R, DM], BF16, kind="Internal")

    sy, te, ve, se, ge = nc.sync, nc.tensor, nc.vector, nc.scalar, nc.gpsimd
    engines = [sy, te, ve, se, ge]

    from contextlib import ExitStack
    es0 = ExitStack()

    pSY = P(es0.enter_context(nc.semaphore("pSY")))
    pTE = P(es0.enter_context(nc.semaphore("pTE")))
    pVE = P(es0.enter_context(nc.semaphore("pVE")))
    pSE = P(es0.enter_context(nc.semaphore("pSE")))
    pGE = P(es0.enter_context(nc.semaphore("pGE")))
    wc = W()

    def vop(inst):
        v = pVE.inc(inst)
        wc.wait(ve, pVE, v)
        return v

    def barrier():
        for eng in engines:
            for p in (pSY, pTE, pVE, pSE, pGE):
                wc.wait(eng, p, p.n)

    ident_bf = es0.enter_context(nc.sbuf_tensor("ident_bf", [128, 128], BF16))
    dcmp = es0.enter_context(nc.sbuf_tensor("dcmp", [128, QT], F32))
    m_col = es0.enter_context(nc.sbuf_tensor("m_col", [128, R], F32))
    pid_sb = es0.enter_context(nc.sbuf_tensor("pid_sb", [1, 1], U32))
    ag_sb = es0.enter_context(nc.sbuf_tensor("ag_sb", [128, NC_D * R], BF16))
    ua_sb = es0.enter_context(nc.sbuf_tensor("ua_sb", [128, NC_D * R], BF16))
    da_sb = es0.enter_context(nc.sbuf_tensor("da_sb", [128, NC_I * R], BF16))
    esSX = ExitStack()
    vb_rep = esSX.enter_context(nc.sbuf_tensor("vb_rep", [128, DM], BF16))

    with ExitStack() as esS:
        ident_f = esS.enter_context(nc.sbuf_tensor("ident_f", [128, 128], F32))
        bv_sb = esS.enter_context(nc.sbuf_tensor("bv_sb", [NH, 3 * HD], F32))
        sgn = esS.enter_context(nc.sbuf_tensor("sgn", [NH, 3 * HD], F32))
        tmp1 = esS.enter_context(nc.sbuf_tensor("tmp1", [NH, HD], F32))
        alpha = esS.enter_context(nc.sbuf_tensor("alpha", [NH, 4], F32))
        mnat = esS.enter_context(nc.sbuf_tensor("mnat", [NH, HD], F32))
        pad = esS.enter_context(nc.sbuf_tensor("pad", [128, 128], F32))
        bcol = esS.enter_context(nc.sbuf_tensor("bcol", [R, II], F32))
        bcol_b = esS.enter_context(nc.sbuf_tensor("bcol_b", [R, II], BF16))
        acol = esS.enter_context(nc.sbuf_tensor("acol", [128, NC_I * R], F32))
        ones1 = esS.enter_context(nc.sbuf_tensor("ones1", [1, 128], F32))
        vbrow = esS.enter_context(nc.sbuf_tensor("vbrow", [1, DM], F32))
        pS1 = esS.enter_context(nc.psum_tensor("pS1", [128, 512], F32))
        pS2 = esS.enter_context(nc.psum_tensor("pS2", [128, 512], F32))

        v0 = pSY.inc(sy.dma_start(pid_sb[:], nc.partition_id_tensor.ap()), 16)
        vg = pGE.inc(ge.iota(dcmp[:, :], pattern=[[1, QT]], base=0, channel_multiplier=-1,
                             allow_small_or_imprecise_dtypes=True))
        wc.wait(ge, pGE, vg)
        vg = pGE.inc(ge.memset(ident_bf[:], 0.0))
        wc.wait(ge, pGE, vg)
        vg = pGE.inc(ge.affine_select(
            out=ident_bf[:], in_=ident_bf[:], compare_op=ALU.not_equal,
            fill=1.0, base=0, pattern=[[-1, 128]], channel_multiplier=1))
        vg2 = pGE.inc(ge.memset(ident_f[:], 0.0))
        wc.wait(ge, pGE, vg2)
        vg2 = pGE.inc(ge.affine_select(
            out=ident_f[:], in_=ident_f[:], compare_op=ALU.not_equal,
            fill=1.0, base=0, pattern=[[-1, 128]], channel_multiplier=1))
        vg3 = pGE.inc(ge.memset(ones1[:], 1.0))
        vg4 = pGE.inc(ge.memset(pad[:], 0.0))

        vq = pSY.inc(sy.dma_start(bv_sb[:, 0:HD], bvq.ap()), 16)
        vq = pSY.inc(sy.dma_start(bv_sb[:, HD:2 * HD], bvk.ap()), 16)
        vq = pSY.inc(sy.dma_start(bv_sb[:, 2 * HD:3 * HD], bvv.ap()), 16)
        wc.wait(se, pSY, vq)
        vs1 = pSE.inc(se.sign(sgn[:, :], bv_sb[:, :]))
        wc.wait(ve, pSY, vq)
        vv = vop(ve.reduce_sum(alpha[:, 0:1], bv_sb[:, 0:HD],
                                   axis=mybir.AxisListType.X, apply_absolute_value=True))
        vv = vop(ve.reduce_sum(alpha[:, 1:2], bv_sb[:, HD:2 * HD],
                                   axis=mybir.AxisListType.X, apply_absolute_value=True))
        vv = vop(ve.reduce_sum(alpha[:, 2:3], bv_sb[:, 2 * HD:3 * HD],
                                   axis=mybir.AxisListType.X, apply_absolute_value=True))
        wc.wait(ve, pSE, vs1)
        vv = vop(ve.tensor_tensor(tmp1[:, :], sgn[:, 0:HD], sgn[:, 0:HD], ALU.mult))
        vv = vop(ve.tensor_scalar(tmp1[:, :], tmp1[:, :], -1.0, 1.0, ALU.mult, ALU.add))
        vv = vop(ve.tensor_tensor(sgn[:, 0:HD], sgn[:, 0:HD], tmp1[:, :], ALU.add))
        vv = vop(ve.tensor_tensor(tmp1[:, :], sgn[:, HD:2 * HD], sgn[:, HD:2 * HD], ALU.mult))
        vv = vop(ve.tensor_scalar(tmp1[:, :], tmp1[:, :], -1.0, 1.0, ALU.mult, ALU.add))
        vv = vop(ve.tensor_tensor(sgn[:, HD:2 * HD], sgn[:, HD:2 * HD], tmp1[:, :], ALU.add))
        vv = vop(ve.tensor_tensor(tmp1[:, :], sgn[:, 2 * HD:3 * HD], sgn[:, 2 * HD:3 * HD], ALU.mult))
        vv = vop(ve.tensor_scalar(tmp1[:, :], tmp1[:, :], -1.0, 1.0, ALU.mult, ALU.add))
        vv = vop(ve.tensor_tensor(sgn[:, 2 * HD:3 * HD], sgn[:, 2 * HD:3 * HD], tmp1[:, :], ALU.add))
        vv = vop(ve.tensor_tensor(mnat[:, :], sgn[:, 0:HD], sgn[:, HD:2 * HD], ALU.mult))
        vv = vop(ve.tensor_tensor(alpha[:, 3:4], alpha[:, 0:1], alpha[:, 1:2], ALU.mult))
        vv = vop(ve.tensor_scalar(alpha[:, 3:4], alpha[:, 3:4], 1.0 / (HD * HD), None, ALU.mult))
        vv = vop(ve.tensor_scalar(mnat[:, :], mnat[:, :], alpha[:, 3:4], None, ALU.mult))
        vv = vop(ve.tensor_scalar(alpha[:, 2:3], alpha[:, 2:3], 1.0 / HD, None, ALU.mult))
        vv = vop(ve.tensor_scalar(sgn[:, 2 * HD:3 * HD], sgn[:, 2 * HD:3 * HD],
                                      alpha[:, 2:3], None, ALU.mult))
        wc.wait(ge, pVE, vv)
        vg5 = pGE.inc(ge.dma_start(mvec.ap(), mnat[:, :]), 16)
        vg5 = pGE.inc(ge.dma_start(vbvec.ap(), sgn[:, 2 * HD:3 * HD]), 16)
        wc.wait(sy, pGE, vg5)
        wc.wait(sy, pGE, vg4)
        v1 = pSY.inc(sy.dma_start(pad[0:R, :], bass.AP(mvec, 0, [[128, R], [1, 128]])), 16)
        wc.wait(te, pSY, v1)
        wc.wait(te, pGE, vg4)
        wc.wait(te, pGE, vg2)
        vt = pTE.inc(te.transpose(pS1[:, 0:128], pad[:, :], ident_f[:]))
        wc.wait(ve, pTE, vt)
        vv = pVE.inc(ve.tensor_copy(m_col[:, :], pS1[:, 0:R]))
        v2 = pSY.inc(sy.dma_start(vbrow[0:1, :], bass.AP(vbvec, 0, [[DM, 1], [1, DM]])), 16)
        wc.wait(te, pSY, v2)
        wc.wait(te, pGE, vg3)
        vt = pTE.inc(te.matmul(pS1[:, :], ones1[0:1, :], vbrow[0:1, 0:512], start=True, stop=True))
        vt = pTE.inc(te.matmul(pS2[:, :], ones1[0:1, :], vbrow[0:1, 512:1024], start=True, stop=True))
        wc.wait(ve, pTE, vt)
        vv = pVE.inc(ve.tensor_copy(vb_rep[:, 0:512], pS1[:, :]))
        vv = pVE.inc(ve.tensor_copy(vb_rep[:, 512:1024], pS2[:, :]))
        for half in range(1, 4):
            wc.wait(te, pVE, vv)
            vt = pTE.inc(te.matmul(pS1[:, :], ones1[0:1, :],
                                   vbrow[0:1, half * 1024:half * 1024 + 512], start=True, stop=True))
            vt = pTE.inc(te.matmul(pS2[:, :], ones1[0:1, :],
                                   vbrow[0:1, half * 1024 + 512:half * 1024 + 1024], start=True, stop=True))
            wc.wait(ve, pTE, vt)
            vv = pVE.inc(ve.tensor_copy(vb_rep[:, half * 1024:half * 1024 + 512], pS1[:, :]))
            vv = pVE.inc(ve.tensor_copy(vb_rep[:, half * 1024 + 512:half * 1024 + 1024], pS2[:, :]))

        for src, dst in ((gb, gb_bf), (ub, ub_bf), (db, db_bf)):
            n = src.shape[1]
            wc.wait(sy, pVE, vv)
            v3 = pSY.inc(sy.dma_start(bcol[0:R, 0:n], src.ap()), 16)
            wc.wait(ve, pSY, v3)
            wc.wait(ve, pGE, pGE.n)
            vv = pVE.inc(ve.tensor_copy(bcol_b[0:R, 0:n], bcol[0:R, 0:n]))
            wc.wait(ge, pVE, vv)
            pGE.inc(ge.dma_start(dst.ap(), bcol_b[0:R, 0:n]), 16)
        for src, dst, nch in ((ga, ag_sb, NC_D), (ua, ua_sb, NC_D), (da, da_sb, NC_I)):
            wc.wait(sy, pVE, vv)
            v4 = pSY.inc(sy.dma_start(
                acol[:, 0:nch * R],
                bass.AP(src, 0, [[R, 128], [128 * R, nch], [1, R]])), 16)
            wc.wait(ve, pSY, v4)
            vv = pVE.inc(ve.tensor_copy(dst[:, 0:nch * R], acol[:, 0:nch * R]))

        ji = esS.enter_context(nc.sbuf_tensor("ji", [1, 2], U32))
        jf = esS.enter_context(nc.sbuf_tensor("jf", [1, 2], F32))
        jcol = esS.enter_context(nc.sbuf_tensor("jcol", [128, 1], F32))
        wc.wait(ve, pSY, v0)
        vv = pVE.inc(ve.tensor_scalar(ji[0:1, 0:1], pid_sb[0:1, 0:1], 3, None, ALU.bitwise_and))
        wc.wait(ve, pVE, vv)
        vv = pVE.inc(ve.tensor_copy(jf[0:1, 0:1], ji[0:1, 0:1]))
        wc.wait(ve, pVE, vv)
        vv = pVE.inc(ve.tensor_scalar(jf[0:1, 0:1], jf[0:1, 0:1], float(QT), None, ALU.mult))
        wc.wait(te, pVE, vv)
        wc.wait(te, pGE, vg3)
        vt = pTE.inc(te.matmul(pS1[:, 0:1], ones1[0:1, :], jf[0:1, 0:1], start=True, stop=True))
        wc.wait(ve, pTE, vt)
        vv = pVE.inc(ve.tensor_copy(jcol[:, 0:1], pS1[:, 0:1]))
        wc.wait(ge, pVE, vv)
        vg = pGE.inc(ge.tensor_scalar(dcmp[:, :], dcmp[:, :], jcol[:, 0:1], None, ALU.add))
        wc.wait(sy, pSY, v0)
        rj_sy = sy.alloc_register("rj_sy")
        sy.reg_load(rj_sy, pid_sb[0:1, 0:1])
        sy.reg_alu(rj_sy, rj_sy, 3, ALU.bitwise_and)
        sy.reg_alu(rj_sy, rj_sy, QT, ALU.mult)
        rg_sy = sy.alloc_register("rg_sy")
        c1024 = sy.alloc_register("c1024")
        sy.reg_mov(c1024, 1024)
        barrier()

    esWX = ExitStack()

    with ExitStack() as esW:
        qraw = [esW.enter_context(nc.sbuf_tensor(f"qraw{i}", [128, II], I32)) for i in range(2)]
        wcvt = [esW.enter_context(nc.sbuf_tensor(f"wcvt{i}", [128, II], BF16)) for i in range(2)]
        wstage = [esW.enter_context(nc.sbuf_tensor(f"wstage{i}", [128, II], BF16)) for i in range(2)]
        scol = esW.enter_context(nc.sbuf_tensor("scol", [128, 2 * NC_I + NC_D], F32))
        bcl = esW.enter_context(nc.sbuf_tensor("bcl", [128, 2 * NC_I + NC_D], F32))
        spad = esW.enter_context(nc.sbuf_tensor("spad", [128, 128], F32))
        sident = esW.enter_context(nc.sbuf_tensor("sident", [128, 128], F32))

        vg = pGE.inc(ge.memset(sident[:], 0.0))
        wc.wait(ge, pGE, vg)
        vg = pGE.inc(ge.affine_select(out=sident[:], in_=sident[:], compare_op=ALU.not_equal,
                                      fill=1.0, base=0, pattern=[[-1, 128]], channel_multiplier=1))
        vg_id = vg
        vg = pGE.inc(ge.memset(spad[:], 0.0))
        with nc.psum_tensor("pSc", [128, 512], F32) as pSc:
            for (scsrc, nrows, off) in ((gsc, NC_I, 0), (usc, NC_I, NC_I), (dsc, NC_D, 2 * NC_I)):
                wc.wait(sy, pGE, vg)
                v1 = pSY.inc(sy.dma_start(spad[0:nrows, :], scsrc.ap()), 16)
                wc.wait(te, pSY, v1)
                wc.wait(te, pGE, vg_id)
                wc.wait(te, pGE, vg)
                wc.wait(te, pVE, pVE.n)
                vt = pTE.inc(te.transpose(pSc[:, 0:128], spad[:, :], sident[:]))
                wc.wait(ve, pTE, vt)
                vv = pVE.inc(ve.tensor_copy(scol[:, off:off + nrows], pSc[:, 0:nrows]))
                wc.wait(sy, pTE, vt)
            wc.wait(ve, pVE, pVE.n)
            vv = pVE.inc(ve.tensor_scalar(scol[:, :], scol[:, :], 0.005, None, ALU.mult))
            wc.wait(ve, pVE, vv)
            vv_sc = pVE.inc(ve.tensor_scalar(bcl[:, :], scol[:, :], -7.5, None, ALU.mult))
        pW = [esWX.enter_context(nc.psum_tensor(f"pW{i}", [128, DM], BF16)) for i in range(2)]

        jobs = [(gq, NC_I, DM, wgT, 0), (uq, NC_I, DM, wuT, NC_I), (dq, NC_D, II, wdT, 2 * NC_I)]
        ld_v = []
        cv_v = []
        tr_v = []
        ev_v = []
        st_v = []
        tix = 0
        pcount = 0
        grp_ev = {}
        grp_tr = {}
        for (src, ntile, cols, scratch, soff) in jobs:
            nch = cols // 128
            for oc in range(ntile):
                par = tix % 2
                if tix >= 2:
                    wc.wait(sy, pSE, cv_v[tix - 2])
                v = pSY.inc(sy.dma_start(
                    qraw[par][:, 0:cols],
                    bass.AP(src, oc * 128 * cols, [[cols, 128], [1, cols]])), 16)
                ld_v.append(v)
                wc.wait(se, pSY, ld_v[tix])
                wc.wait(se, pVE, vv_sc)
                if tix >= 2:
                    wc.wait(se, pTE, tr_v[tix - 2])
                v = pSE.inc(se.activation(
                    wcvt[par][:, 0:cols], qraw[par][:, 0:cols], AF.Identity,
                    bias=bcl[:, soff + oc:soff + oc + 1],
                    scale=scol[:, soff + oc:soff + oc + 1]))
                cv_v.append(v)
                n_ev = 0
                wc.wait(te, pSE, cv_v[tix])
                for g0 in range(0, nch, 32):
                    gn = min(32, nch - g0)
                    ppar = pcount % 2
                    if pcount >= 2:
                        wc.wait(te, pVE, grp_ev[pcount - 2])
                    for ci in range(gn):
                        inst = te.transpose(pW[ppar][:, ci * 128:(ci + 1) * 128],
                                            wcvt[par][:, (g0 + ci) * 128:(g0 + ci + 1) * 128],
                                            ident_bf[:])
                    vt = pTE.inc(inst)
                    grp_tr[pcount] = vt
                    wc.wait(ve, pTE, vt)
                    if tix >= 2:
                        wc.wait(ve, pGE, st_v[tix - 2])
                    vv = pVE.inc(ve.tensor_copy(
                        wstage[par][:, g0 * 128:(g0 + gn) * 128],
                        pW[ppar][:, 0:gn * 128]))
                    grp_ev[pcount] = vv
                    n_ev += 1
                    pcount += 1
                tr_v.append(grp_tr[pcount - 1])
                ev_v.append(grp_ev[pcount - 1])
                wc.wait(ge, pVE, ev_v[tix])
                v = pGE.inc(ge.dma_start(
                    bass.AP(scratch, oc * 128 * cols, [[cols, 128], [1, cols]]),
                    wstage[par][:, 0:cols]), 16)
                st_v.append(v)
                tix += 1
        barrier()

    with ExitStack() as esX:
        xt = [esX.enter_context(nc.sbuf_tensor(f"xt{i}", [128, DM], F32)) for i in range(2)]
        sqv = [esX.enter_context(nc.sbuf_tensor(f"sqv{i}", [128, DM], F32)) for i in range(2)]
        xnt = [esX.enter_context(nc.sbuf_tensor(f"xnt{i}", [128, DM], BF16)) for i in range(2)]
        vt_ = [esX.enter_context(nc.sbuf_tensor(f"vt{i}", [128, DM], BF16)) for i in range(2)]
        fmst = [esX.enter_context(nc.sbuf_tensor(f"fmst{i}", [128, NC_D * 512], BF16)) for i in range(2)]
        stat = esX.enter_context(nc.sbuf_tensor("stat", [128, 12], F32))
        eps_sb = esX.enter_context(nc.sbuf_tensor("eps_sb", [128, 1], F32))
        v_eps = pGE.inc(ge.memset(eps_sb[:], 1e-5))

        NT = SEQ // 128
        ld_x = [0] * NT
        sq_x = [0] * NT
        st_stat = [0] * NT
        rs_x = [0] * NT
        xn_x = [0] * NT
        vv_x = [0] * NT
        tr_x = [0] * NT
        evq_x = [0] * NT
        stv_x = [0] * NT
        stf_x = {}

        def x_front(i):
            par = i % 2
            if i >= 2:
                wc.wait(sy, pSE, sq_x[i - 2])
                wc.wait(sy, pGE, xn_x[i - 2])
            ld_x[i] = pSY.inc(sy.dma_start(
                xt[par][:], bass.AP(x_seq, i * 128 * DM, [[DM, 128], [1, DM]])), 16)
            wc.wait(se, pSY, ld_x[i])
            if i >= 2:
                wc.wait(se, pVE, st_stat[i - 2])
            sq_x[i] = pSE.inc(se.activation(sqv[par][:], xt[par][:], AF.Square,
                                            accum_out=stat[:, 0 + par:1 + par]))
            wc.wait(ve, pSY, ld_x[i])
            if i >= 2:
                wc.wait(ve, pGE, xn_x[i - 2])
            vop(ve.reduce_sum(stat[:, 2 + par:3 + par], xt[par][:],
                              axis=mybir.AxisListType.X, negate=True))
            wc.wait(ve, pSE, sq_x[i])
            vop(ve.tensor_scalar(stat[:, 2 + par:3 + par], stat[:, 2 + par:3 + par],
                                 1.0 / DM, None, ALU.mult))
            vop(ve.tensor_tensor(stat[:, 6 + par:7 + par], stat[:, 2 + par:3 + par],
                                 stat[:, 2 + par:3 + par], ALU.mult))
            vop(ve.tensor_scalar(stat[:, 4 + par:5 + par], stat[:, 0 + par:1 + par],
                                 1.0 / DM, None, ALU.mult))
            st_stat[i] = vop(ve.tensor_tensor(stat[:, 4 + par:5 + par], stat[:, 4 + par:5 + par],
                                              stat[:, 6 + par:7 + par], ALU.subtract))

        def x_back(i):
            par = i % 2
            wc.wait(se, pVE, st_stat[i])
            if i >= 2:
                wc.wait(se, pGE, xn_x[i - 2])
            wc.wait(se, pGE, v_eps)
            vse = pSE.inc(se.activation(stat[:, 10 + par:11 + par], stat[:, 4 + par:5 + par],
                                        AF.Ln, bias=eps_sb[:, 0:1]))
            wc.wait(se, pSE, vse)
            rs_x[i] = pSE.inc(se.activation(stat[:, 8 + par:9 + par], stat[:, 10 + par:11 + par],
                                            AF.Exp, scale=-0.5))
            wc.wait(ge, pSE, rs_x[i])
            if i >= 2:
                wc.wait(ge, pTE, tr_x[i - 2])
                wc.wait(ge, pGE, stv_x[i - 2])
            xn_x[i] = pGE.inc(ge.tensor_scalar(xnt[par][:], xt[par][:],
                                               stat[:, 2 + par:3 + par], stat[:, 8 + par:9 + par],
                                               ALU.add, ALU.mult))
            wc.wait(ge, pGE, xn_x[i])
            vv_x[i] = pGE.inc(ge.tensor_tensor(vt_[par][:], xnt[par][:], vb_rep[:], ALU.mult))
            wc.wait(te, pGE, xn_x[i])
            ppar = i % 2
            if i >= 2:
                wc.wait(te, pVE, evq_x[i - 2])
            for c in range(NC_D):
                inst = te.transpose(pW[ppar][:, c * 128:(c + 1) * 128],
                                    xnt[par][:, c * 128:(c + 1) * 128], ident_bf[:])
            tr_x[i] = pTE.inc(inst)
            gpar = (i // 4) % 2
            wc.wait(ve, pTE, tr_x[i])
            if i % 4 == 0 and i >= 8:
                wc.wait(ve, pGE, stf_x[i - 8])
            evq_x[i] = pVE.inc(ve.tensor_copy(
                bass.AP(fmst[gpar], (i % 4) * 128,
                        [[NC_D * 512, 128], [512, NC_D], [1, 128]]),
                pW[ppar][:, 0:NC_D * 128]))
            wc.wait(ge, pGE, vv_x[i])
            stv_x[i] = pGE.inc(ge.dma_start(
                bass.AP(vs, i * 128 * DM, [[DM, 128], [1, DM]]), vt_[par][:]), 16)
            if i % 4 == 3:
                g4 = i // 4
                wc.wait(ge, pVE, evq_x[i])
                stf_x[g4 * 4] = pGE.inc(ge.dma_start(
                    bass.AP(xnfm, g4 * 512, [[SEQ, 128], [128 * SEQ, NC_D], [1, 512]]),
                    fmst[g4 % 2][:]), 16)

        for i in range(NT + 1):
            if i < NT:
                x_front(i)
            if i >= 1:
                x_back(i - 1)
        barrier()

    esWX.close()
    esSX.close()

    esP = ExitStack()
    pb = [esP.enter_context(nc.psum_tensor(f"pb{i}", [128, 512], F32)) for i in range(8)]

    with ExitStack() as esA:
        q_sb = esA.enter_context(nc.sbuf_tensor("q_sb", [128, NC_D * QT], BF16))
        kt_b = [esA.enter_context(nc.sbuf_tensor(f"kt{i}", [128, 8 * 1024], BF16)) for i in range(2)]
        v_b = [esA.enter_context(nc.sbuf_tensor(f"v_b{i}", [128, 8 * 1024], BF16)) for i in range(2)]
        at_all = esA.enter_context(nc.sbuf_tensor("at_all", [128, 64 * QT], BF16))
        st_st = [esA.enter_context(nc.sbuf_tensor(f"st_st{i}", [128, 8 * QT], BF16)) for i in range(2)]

        kt_free = [0, 0]
        v_free = [0, 0]
        ev_done = pVE.n
        st_done = {}
        hix = 0
        for g in range(NG):
            trips = 8 * (g + 1)
            if g == 0:
                sy.reg_alu(rg_sy, rj_sy, 0, ALU.add)
            else:
                sy.reg_alu(rg_sy, rg_sy, c1024, ALU.add)
            wc.wait(sy, pTE, pTE.n)
            vq = pSY.inc(sy.dma_start(
                q_sb[:],
                bass.AP(xnfm, rg_sy, [[SEQ, 128], [128 * SEQ, NC_D], [1, QT]])), 16)
            wc.wait(ge, pSY, vq)
            for c in range(NC_D):
                inst = ge.tensor_scalar(q_sb[:, c * QT:(c + 1) * QT], q_sb[:, c * QT:(c + 1) * QT],
                                        m_col[:, c:c + 1], None, ALU.mult)
            qm_done = pGE.inc(inst)

            for h in range(NH):
                sg_v = {}
                mk_v = {}
                kt_ld = {}
                for k in range(trips):
                    grp, st_ = k // 8, k % 8
                    if st_ == 0:
                        wc.wait(sy, pTE, kt_free[grp % 2])
                        kt_ld[grp] = pSY.inc(sy.dma_start(
                            kt_b[grp % 2][:],
                            bass.AP(xnfm, 8 * h * 128 * SEQ + grp * 1024,
                                    [[SEQ, 128], [128 * SEQ, 8], [1, 1024]])), 16)
                    pk = pb[6 + (k % 2)]
                    wc.wait(te, pSY, kt_ld[grp])
                    wc.wait(te, pGE, qm_done)
                    if k >= 2:
                        wc.wait(te, pSE, sg_v[k - 2])
                    else:
                        wc.wait(te, pVE, ev_done)
                    for dc in range(8):
                        inst = te.matmul(pk[:, 0:QT],
                                         kt_b[grp % 2][:, dc * 1024 + st_ * 128: dc * 1024 + (st_ + 1) * 128],
                                         q_sb[:, (8 * h + dc) * QT:(8 * h + dc + 1) * QT],
                                         start=(dc == 0), stop=(dc == 7))
                    sc_v = pTE.inc(inst)
                    kt_free[grp % 2] = sc_v
                    wc.wait(se, pTE, sc_v)
                    sg_v[k] = pSE.inc(se.activation(at_all[:, k * QT:(k + 1) * QT], pk[:, 0:QT],
                                                    AF.Sigmoid, scale=0.125))
                    wc.wait(ve, pSE, sg_v[k])
                    mk_v[k] = pVE.inc(ve.scalar_tensor_tensor(
                        at_all[:, k * QT:(k + 1) * QT], dcmp[:, :],
                        float(128 * k - 1024 * g),
                        at_all[:, k * QT:(k + 1) * QT],
                        ALU.is_ge, ALU.mult))
                b_last = 0
                v_ld = {}
                for k in range(trips):
                    grp, st_ = k // 8, k % 8
                    if st_ == 0:
                        wc.wait(sy, pTE, v_free[grp % 2])
                        v_ld[grp] = pSY.inc(sy.dma_start(
                            v_b[grp % 2][:],
                            bass.AP(vs, grp * 1024 * DM + h * 1024,
                                    [[DM, 128], [128 * DM, 8], [1, 1024]])), 16)
                    wc.wait(te, pSY, v_ld[grp])
                    if k == 0:
                        wc.wait(te, pVE, mk_v[trips - 1])
                        wc.wait(te, pVE, ev_done)
                    for dc in range(8):
                        inst = te.matmul(pb[dc][:, 0:QT],
                                         v_b[grp % 2][:, st_ * 1024 + dc * 128: st_ * 1024 + (dc + 1) * 128],
                                         at_all[:, k * QT:(k + 1) * QT],
                                         start=(k == 0), stop=(k == trips - 1))
                    b_last = pTE.inc(inst)
                    v_free[grp % 2] = b_last
                hpar = hix % 2
                wc.wait(ve, pTE, b_last)
                if hix >= 2:
                    wc.wait(ve, pGE, st_done[hix - 2])
                for dc in range(8):
                    inst = ve.tensor_copy(st_st[hpar][:, dc * QT:(dc + 1) * QT], pb[dc][:, 0:QT])
                ev_done = pVE.inc(inst)
                wc.wait(ge, pVE, ev_done)
                st_done[hix] = pGE.inc(ge.dma_start(
                    bass.AP(ss, 8 * h * 128 * TLOC + g * QT,
                            [[TLOC, 128], [128 * TLOC, 8], [1, QT]]),
                    st_st[hpar][:]), 16)
                hix += 1
        barrier()

    with ExitStack() as esG:
        xn_res = esG.enter_context(nc.sbuf_tensor("xn_res", [128, NC_D * TLOC], BF16))
        wg_b = [esG.enter_context(nc.sbuf_tensor(f"wg_b{i}", [128, DM], BF16)) for i in range(2)]
        wu_b = [esG.enter_context(nc.sbuf_tensor(f"wu_b{i}", [128, DM], BF16)) for i in range(2)]
        gb_sl = [esG.enter_context(nc.sbuf_tensor(f"gb_sl{i}", [R, 128], BF16)) for i in range(2)]
        ub_sl = [esG.enter_context(nc.sbuf_tensor(f"ub_sl{i}", [R, 128], BF16)) for i in range(2)]
        zg_sb = esG.enter_context(nc.sbuf_tensor("zg_sb", [R, TLOC], BF16))
        zu_sb = esG.enter_context(nc.sbuf_tensor("zu_sb", [R, TLOC], BF16))
        hg = [esG.enter_context(nc.sbuf_tensor(f"hg{i}", [128, 512], BF16)) for i in range(2)]
        hu = [esG.enter_context(nc.sbuf_tensor(f"hu{i}", [128, 512], BF16)) for i in range(2)]
        hst = [esG.enter_context(nc.sbuf_tensor(f"hst{i}", [128, TLOC], BF16)) for i in range(2)]

        xn_ld = 0
        for g in range(NG):
            if g == 0:
                sy.reg_alu(rg_sy, rj_sy, 0, ALU.add)
            else:
                sy.reg_alu(rg_sy, rg_sy, c1024, ALU.add)
            xn_ld = pSY.inc(sy.dma_start(
                bass.AP(xn_res, g * QT, [[NC_D * TLOC, 128], [TLOC, NC_D], [1, QT]]),
                bass.AP(xnfm, rg_sy, [[SEQ, 128], [128 * SEQ, NC_D], [1, QT]])), 16)
        z_ev = 0
        for (asb, zsb) in ((ag_sb, zg_sb), (ua_sb, zu_sb)):
            for sub in range(4):
                pz = pb[6 + sub % 2]
                wc.wait(te, pSY, xn_ld)
                wc.wait(te, pSE, z_ev)
                for c in range(NC_D):
                    inst = te.matmul(pz[0:R, :], asb[:, c * R:(c + 1) * R],
                                     xn_res[:, c * TLOC + sub * 512: c * TLOC + (sub + 1) * 512],
                                     start=(c == 0), stop=(c == NC_D - 1))
                vz = pTE.inc(inst)
                wc.wait(se, pTE, vz)
                z_ev = pSE.inc(se.activation(zsb[0:R, sub * 512:(sub + 1) * 512], pz[0:R, :],
                                             AF.Identity, scale=0.02))

        ld_w = [0] * NC_I
        mm_oc = [0] * NC_I
        sl_v = {}
        ue_v = {}
        hm_v = {}
        hs_v = {}
        for oc in range(NC_I):
            par = oc % 2
            if oc >= 2:
                wc.wait(sy, pTE, mm_oc[oc - 2])
            v = pSY.inc(sy.dma_start(wg_b[par][:], bass.AP(wgT, oc * 128 * DM, [[DM, 128], [1, DM]])), 16)
            v = pSY.inc(sy.dma_start(wu_b[par][:], bass.AP(wuT, oc * 128 * DM, [[DM, 128], [1, DM]])), 16)
            v = pSY.inc(sy.dma_start(gb_sl[par][0:R, :], bass.AP(gb_bf, oc * 128, [[II, R], [1, 128]])), 16)
            v = pSY.inc(sy.dma_start(ub_sl[par][0:R, :], bass.AP(ub_bf, oc * 128, [[II, R], [1, 128]])), 16)
            ld_w[oc] = v
            last = 0
            for pr in range(2):
                subs = (2 * pr, 2 * pr + 1)
                pg_ = [pb[4 * pr + 0], pb[4 * pr + 1]]
                pu_ = [pb[4 * pr + 2], pb[4 * pr + 3]]
                wc.wait(te, pSY, ld_w[oc])
                wc.wait(te, pSE, z_ev)
                if oc >= 1:
                    for s2 in subs:
                        wc.wait(te, pSE, sl_v[(oc - 1, s2)])
                        wc.wait(te, pVE, ue_v[(oc - 1, s2)])
                for c in range(NC_D):
                    for i2, s2 in enumerate(subs):
                        te.matmul(pg_[i2][:, :], wg_b[par][:, c * 128:(c + 1) * 128],
                                  xn_res[:, c * TLOC + s2 * 512: c * TLOC + (s2 + 1) * 512],
                                  start=(c == 0), stop=False)
                for i2, s2 in enumerate(subs):
                    te.matmul(pg_[i2][:, :], gb_sl[par][0:R, :], zg_sb[0:R, s2 * 512:(s2 + 1) * 512],
                              start=False, stop=True)
                for c in range(NC_D):
                    for i2, s2 in enumerate(subs):
                        te.matmul(pu_[i2][:, :], wu_b[par][:, c * 128:(c + 1) * 128],
                                  xn_res[:, c * TLOC + s2 * 512: c * TLOC + (s2 + 1) * 512],
                                  start=(c == 0), stop=False)
                for i2, s2 in enumerate(subs):
                    inst = te.matmul(pu_[i2][:, :], ub_sl[par][0:R, :], zu_sb[0:R, s2 * 512:(s2 + 1) * 512],
                                     start=False, stop=True)
                last = pTE.inc(inst)
                for i2, s2 in enumerate(subs):
                    wc.wait(se, pTE, last)
                    if oc >= 1 or pr >= 1:
                        wc.wait(se, pGE, hm_v.get((oc, s2 - 2), hm_v.get((oc - 1, s2 + 2), 0)))
                    sl_v[(oc, s2)] = pSE.inc(se.activation(hg[i2][:], pg_[i2][:, :], AF.Silu))
                    wc.wait(ve, pTE, last)
                    if oc >= 1 or pr >= 1:
                        wc.wait(ve, pGE, hm_v.get((oc, s2 - 2), hm_v.get((oc - 1, s2 + 2), 0)))
                    ue_v[(oc, s2)] = pVE.inc(ve.tensor_copy(hu[i2][:], pu_[i2][:, :]))
                    wc.wait(ge, pSE, sl_v[(oc, s2)])
                    wc.wait(ge, pVE, ue_v[(oc, s2)])
                    if oc >= 2 and s2 == 0:
                        wc.wait(ge, pGE, hs_v[oc - 2])
                    hm_v[(oc, s2)] = pGE.inc(ge.tensor_tensor(
                        hst[par][:, s2 * 512:(s2 + 1) * 512], hg[i2][:], hu[i2][:], ALU.mult))
            mm_oc[oc] = last
            wc.wait(ge, pGE, hm_v[(oc, 3)])
            hs_v[oc] = pGE.inc(ge.dma_start(
                bass.AP(hs, oc * 128 * TLOC, [[TLOC, 128], [1, TLOC]]), hst[par][:]), 16)
        barrier()

    zd_dbg = nc.dram_tensor("d_zd", [2, R, 1024], BF16, kind="ExternalOutput") if dbg else None
    with ExitStack() as esD:
        h_res = esD.enter_context(nc.sbuf_tensor("h_res", [128, NC_I * 1024], BF16))
        wd_b = [esD.enter_context(nc.sbuf_tensor(f"wd_b{i}", [128, 29 * 128], BF16)) for i in range(2)]
        db_sl = [esD.enter_context(nc.sbuf_tensor(f"db_sl{i}", [R, 128], BF16)) for i in range(2)]
        zd_sb = esD.enter_context(nc.sbuf_tensor("zd_sb", [R, 1024], BF16))
        sd_sb = [esD.enter_context(nc.sbuf_tensor(f"sd_sb{i}", [128, 512], BF16)) for i in range(2)]
        s_sl = [esD.enter_context(nc.sbuf_tensor(f"s_sl{i}", [128, 512], BF16)) for i in range(2)]

        for half in range(2):
            h_ld = 0
            wc.wait(sy, pTE, pTE.n)
            for q4 in range(4):
                c0 = (NC_I * q4) // 4
                c1 = (NC_I * (q4 + 1)) // 4
                h_ld = pSY.inc(sy.dma_start(
                    bass.AP(h_res, c0 * 1024, [[NC_I * 1024, 128], [1024, c1 - c0], [1, 1024]]),
                    bass.AP(hs, c0 * 128 * TLOC + half * 1024,
                            [[TLOC, 128], [128 * TLOC, c1 - c0], [1, 1024]])), 16)
            z_ev = 0
            for sub in range(2):
                pz = pb[6 + sub]
                wc.wait(te, pSY, h_ld)
                wc.wait(te, pSE, z_ev)
                for c in range(NC_I):
                    inst = te.matmul(pz[0:R, :], da_sb[:, c * R:(c + 1) * R],
                                     h_res[:, c * 1024 + sub * 512: c * 1024 + (sub + 1) * 512],
                                     start=(c == 0), stop=(c == NC_I - 1))
                vz = pTE.inc(inst)
                wc.wait(se, pTE, vz)
                z_ev = pSE.inc(se.activation(zd_sb[0:R, sub * 512:(sub + 1) * 512], pz[0:R, :],
                                             AF.Identity, scale=0.02))
            if dbg:
                wc.wait(ge, pSE, z_ev)
                pGE.inc(ge.dma_start(
                    bass.AP(zd_dbg, half * R * 1024, [[1024, R], [1, 1024]]),
                    zd_sb[0:R, :]), 16)

            ld_seg = {}
            seg_used = {}
            ld_db = {}
            ld_s = {}
            sil_v = {}
            pm_v = {}
            pst_v = {}
            SEGS = [(0, 29), (29, 58), (58, 86)]
            NSEG = 3
            def load_seg(oc, s_):
                pseg = (oc * NSEG + s_) % 2
                wc.wait(sy, pTE, seg_used.get(oc * NSEG + s_ - 2, 0))
                c0, c1 = SEGS[s_]
                ld_seg[oc * NSEG + s_] = pSY.inc(sy.dma_start(
                    wd_b[pseg][:, 0:(c1 - c0) * 128],
                    bass.AP(wdT, oc * 128 * II + c0 * 128, [[II, 128], [1, (c1 - c0) * 128]])), 16)

            for oc in range(NC_D):
                load_seg(oc, 0)
                load_seg(oc, 1)
                wc.wait(sy, pTE, seg_used.get((oc - 2) * NSEG + NSEG - 1, 0) if oc >= 2 else 0)
                ld_db[oc] = pSY.inc(sy.dma_start(
                    db_sl[oc % 2][0:R, :], bass.AP(db_bf, oc * 128, [[DM, R], [1, 128]])), 16)
                for sub in range(2):
                    if oc >= 1:
                        wc.wait(sy, pGE, pm_v.get((oc - 1, sub), 0))
                    ld_s[(oc, sub)] = pSY.inc(sy.dma_start(
                        s_sl[sub][:],
                        bass.AP(ss, oc * 128 * TLOC + half * 1024 + sub * 512,
                                [[TLOC, 128], [1, 512]])), 16)
                psd = [pb[(oc % 2) * 2], pb[(oc % 2) * 2 + 1]]
                lastmm = 0
                for s_ in range(NSEG):
                    pseg = (oc * NSEG + s_) % 2
                    c0, c1 = SEGS[s_]
                    wc.wait(te, pSY, ld_seg[oc * NSEG + s_])
                    if s_ == NSEG - 1:
                        wc.wait(te, pSY, ld_db[oc])
                    wc.wait(te, pSE, z_ev)
                    inst = None
                    if s_ == 0 and oc >= 2:
                        for sub in range(2):
                            wc.wait(te, pSE, sil_v[(oc - 2, sub)])
                    for ci in range(c1 - c0):
                        c = c0 + ci
                        for sub in range(2):
                            inst = te.matmul(psd[sub][:, :], wd_b[pseg][:, ci * 128:(ci + 1) * 128],
                                             h_res[:, c * 1024 + sub * 512: c * 1024 + (sub + 1) * 512],
                                             start=(s_ == 0 and ci == 0), stop=False)
                    if s_ == NSEG - 1:
                        for sub in range(2):
                            inst = te.matmul(psd[sub][:, :], db_sl[oc % 2][0:R, :],
                                             zd_sb[0:R, sub * 512:(sub + 1) * 512],
                                             start=False, stop=True)
                        lastmm = pTE.inc(inst)
                    if s_ < NSEG - 1:
                        seg_used[oc * NSEG + s_] = pTE.inc(inst)
                        if s_ == 0:
                            load_seg(oc, 2)
                    else:
                        seg_used[oc * NSEG + s_] = lastmm
                for sub in range(2):
                    wc.wait(se, pTE, lastmm)
                    if oc >= 1:
                        wc.wait(se, pGE, pst_v[(oc - 1, sub)])
                    sil_v[(oc, sub)] = pSE.inc(se.activation(sd_sb[sub][:], psd[sub][:, :], AF.Silu))
                    wc.wait(ge, pSE, sil_v[(oc, sub)])
                    wc.wait(ge, pSY, ld_s[(oc, sub)])
                    pm_v[(oc, sub)] = pGE.inc(ge.tensor_tensor(sd_sb[sub][:], sd_sb[sub][:],
                                                               s_sl[sub][:], ALU.mult))
                    wc.wait(ge, pGE, pm_v[(oc, sub)])
                    pst_v[(oc, sub)] = pGE.inc(ge.dma_start(
                        bass.AP(ps_, oc * 128 * TLOC + half * 1024 + sub * 512,
                                [[TLOC, 128], [1, 512]]),
                        sd_sb[sub][:]), 16)
        barrier()

    with ExitStack() as esE:
        ptm = [esE.enter_context(nc.sbuf_tensor(f"ptm{i}", [128, DM], BF16)) for i in range(2)]
        xep = [esE.enter_context(nc.sbuf_tensor(f"xep{i}", [128, DM], F32)) for i in range(2)]
        ot = [esE.enter_context(nc.sbuf_tensor(f"ot{i}", [128, DM], F32)) for i in range(2)]

        NE = TLOC // 128
        ld_p = [0] * NE
        ld_xe = [0] * NE
        ad_v = [0] * NE
        so_v = [0] * NE
        for i in range(NE):
            par = i % 2
            if i >= 2:
                wc.wait(sy, pVE, ad_v[i - 2])
            ld_p[i] = pSY.inc(sy.dma_start(
                ptm[par][:], bass.AP(ps_, i * 128, [[TLOC, DM], [1, 128]]), transpose=True), 16)
            ld_xe[i] = pSY.inc(sy.dma_start(
                xep[par][:], bass.AP(x_q, i * 128 * DM, [[DM, 128], [1, DM]])), 16)
            wc.wait(ve, pSY, ld_xe[i])
            if i >= 2:
                wc.wait(ve, pGE, so_v[i - 2])
            ad_v[i] = pVE.inc(ve.tensor_tensor(ot[par][:], xep[par][:], ptm[par][:], ALU.add))
            wc.wait(ge, pVE, ad_v[i])
            so_v[i] = pGE.inc(ge.dma_start(
                bass.AP(out_q, i * 128 * DM, [[DM, 128], [1, DM]]), ot[par][:]), 16)
        wc.wait(ge, pGE, pGE.n)
    esP.close()
    if dbg:
        dbgmap = {
            "d_xnfm": (xnfm, [2, 128, SEQ]),
            "d_vs": (vs, [256, DM]),
            "d_wg": (wgT, [1, 128, DM]),
            "d_wd": (wdT, [1, 128, II]),
            "d_ss": (ss, [NC_D, 128, TLOC]),
            "d_hs": (hs, [NC_I, 128, TLOC]),
            "d_ps": (ps_, [DM, TLOC]),
        }
        vstart = pGE.n
        for name, (srct, shp) in dbgmap.items():
            dt_ = nc.dram_tensor(name, shp, BF16, kind="ExternalOutput")
            n_el = int(np.prod(shp))
            pGE.inc(ge.dma_start(bass.AP(dt_, 0, [[n_el, 1], [1, n_el]]),
                                 bass.AP(srct, 0, [[n_el, 1], [1, n_el]])), 16)
        dt_ = nc.dram_tensor("d_dcmp", [128, QT], F32, kind="ExternalOutput")
        pGE.inc(ge.dma_start(dt_.ap(), dcmp[:, :]), 16)
        dt_ = nc.dram_tensor("d_mcol", [128, R], F32, kind="ExternalOutput")
        pGE.inc(ge.dma_start(dt_.ap(), m_col[:, :]), 16)
        wc.wait(ge, pGE, pGE.n)
    es0.close()
    return nc


_NC_CACHE = {}


def kernel(**inputs):
    x = np.asarray(inputs["L0_attn_x"])
    n_cores = 8
    in_maps = []
    rowsets = []
    common = {
        "gq": np.ascontiguousarray(inputs["L0_ffn_gate_q"], dtype=np.int32),
        "uq": np.ascontiguousarray(inputs["L0_ffn_up_q"], dtype=np.int32),
        "dq": np.ascontiguousarray(inputs["L0_ffn_down_q"], dtype=np.int32),
        "gsc": np.ascontiguousarray(np.asarray(inputs["L0_ffn_gate_scale"]).reshape(NC_I, 128)),
        "usc": np.ascontiguousarray(np.asarray(inputs["L0_ffn_up_scale"]).reshape(NC_I, 128)),
        "dsc": np.ascontiguousarray(np.asarray(inputs["L0_ffn_down_scale"]).reshape(NC_D, 128)),
        "ga": np.ascontiguousarray(inputs["L0_ffn_gate_A"], dtype=np.float32),
        "gb": np.ascontiguousarray(inputs["L0_ffn_gate_B"], dtype=np.float32),
        "ua": np.ascontiguousarray(inputs["L0_ffn_up_A"], dtype=np.float32),
        "ub": np.ascontiguousarray(inputs["L0_ffn_up_B"], dtype=np.float32),
        "da": np.ascontiguousarray(inputs["L0_ffn_down_A"], dtype=np.float32),
        "db": np.ascontiguousarray(inputs["L0_ffn_down_B"], dtype=np.float32),
        "bvq": np.ascontiguousarray(inputs["L0_attn_bv_q"], dtype=np.float32),
        "bvk": np.ascontiguousarray(inputs["L0_attn_bv_k"], dtype=np.float32),
        "bvv": np.ascontiguousarray(inputs["L0_attn_bv_v"], dtype=np.float32),
    }
    for c in range(n_cores):
        s, j = c // 4, c % 4
        rows = np.concatenate([np.arange(g * 1024 + j * QT, g * 1024 + (j + 1) * QT)
                               for g in range(NG)])
        rowsets.append((s, rows))
        m = dict(common)
        m["x_seq"] = np.ascontiguousarray(x[s], dtype=np.float32)
        m["x_q"] = np.ascontiguousarray(x[s][rows], dtype=np.float32)
        in_maps.append(m)

    if "nc" not in _NC_CACHE:
        _NC_CACHE["nc"] = emit(bass.Bass("TRN2", target_bir_lowering=False))
    nc = _NC_CACHE["nc"]
    res = run_bass_kernel_spmd(nc, in_maps, core_ids=list(range(n_cores)))
    out = np.empty((2, SEQ, DM), dtype=np.float32)
    for c in range(n_cores):
        s, rows = rowsets[c]
        out[s][rows] = res.results[c]["out_q"]
    return out


# version 07
# speedup vs baseline: 1.0505x; 1.0505x over previous
import numpy as np
import ml_dtypes
import concourse.bass as bass
import concourse.mybir as mybir
from concourse.bass_utils import run_bass_kernel_spmd

F32 = mybir.dt.float32
BF16 = mybir.dt.bfloat16
I32 = mybir.dt.int32
U32 = mybir.dt.uint32
AF = mybir.ActivationFunctionType
ALU = mybir.AluOpType

SEQ = 8192
DM = 4096
NH = 4
HD = 1024
II = 11008
NC_I = II // 128
NC_D = DM // 128
TLOC = 2048
NG = 8
QT = 256
R = 32


class P:
    def __init__(self, sem):
        self.sem = sem
        self.n = 0

    def inc(self, inst, k=1):
        inst.then_inc(self.sem, k)
        self.n += k
        return self.n


class W:
    def __init__(self):
        self.seen = {}

    def wait(self, eng, p, v):
        if v <= 0:
            return
        key = (id(eng), id(p))
        if self.seen.get(key, -1) >= v:
            return
        eng.wait_ge(p.sem, v)
        self.seen[key] = v


def emit(nc, dbg=()):
    x_seq = nc.dram_tensor("x_seq", [SEQ, DM], F32, kind="ExternalInput")
    x_q = nc.dram_tensor("x_q", [TLOC, DM], F32, kind="ExternalInput")
    gq = nc.dram_tensor("gq", [II, DM], I32, kind="ExternalInput")
    uq = nc.dram_tensor("uq", [II, DM], I32, kind="ExternalInput")
    dq = nc.dram_tensor("dq", [DM, II], I32, kind="ExternalInput")
    gsc = nc.dram_tensor("gsc", [NC_I, 128], F32, kind="ExternalInput")
    usc = nc.dram_tensor("usc", [NC_I, 128], F32, kind="ExternalInput")
    dsc = nc.dram_tensor("dsc", [NC_D, 128], F32, kind="ExternalInput")
    ga = nc.dram_tensor("ga", [DM, R], F32, kind="ExternalInput")
    gb = nc.dram_tensor("gb", [R, II], F32, kind="ExternalInput")
    ua = nc.dram_tensor("ua", [DM, R], F32, kind="ExternalInput")
    ub = nc.dram_tensor("ub", [R, II], F32, kind="ExternalInput")
    da = nc.dram_tensor("da", [II, R], F32, kind="ExternalInput")
    db = nc.dram_tensor("db", [R, DM], F32, kind="ExternalInput")
    bvq = nc.dram_tensor("bvq", [NH, HD], F32, kind="ExternalInput")
    bvk = nc.dram_tensor("bvk", [NH, HD], F32, kind="ExternalInput")
    bvv = nc.dram_tensor("bvv", [NH, HD], F32, kind="ExternalInput")
    out_q = nc.dram_tensor("out_q", [TLOC, DM], F32, kind="ExternalOutput")

    wgT = nc.dram_tensor("wgT", [NC_I, 128, DM], BF16, kind="Internal")
    wuT = nc.dram_tensor("wuT", [NC_I, 128, DM], BF16, kind="Internal")
    wdT = nc.dram_tensor("wdT", [NC_D, 128, II], BF16, kind="Internal")
    xnfm = nc.dram_tensor("xnfm", [NC_D, 128, SEQ], BF16, kind="Internal")
    vs = nc.dram_tensor("vs", [SEQ, DM], BF16, kind="Internal")
    hs = nc.dram_tensor("hs", [NC_I, 128, TLOC], BF16, kind="Internal")
    ss = nc.dram_tensor("ss", [NC_D, 128, TLOC], BF16, kind="Internal")
    ps_ = nc.dram_tensor("ps", [DM, TLOC], BF16, kind="Internal")
    mvec = nc.dram_tensor("mvec", [NH, HD], F32, kind="Internal")
    vbvec = nc.dram_tensor("vbvec", [NH, HD], F32, kind="Internal")
    gb_bf = nc.dram_tensor("gb_bf", [R, II], BF16, kind="Internal")
    ub_bf = nc.dram_tensor("ub_bf", [R, II], BF16, kind="Internal")
    db_bf = nc.dram_tensor("db_bf", [R, DM], BF16, kind="Internal")

    sy, te, ve, se, ge = nc.sync, nc.tensor, nc.vector, nc.scalar, nc.gpsimd
    engines = [sy, te, ve, se, ge]

    from contextlib import ExitStack
    es0 = ExitStack()

    all_sems = []

    def newP(name):
        p = P(es0.enter_context(nc.semaphore(name)))
        all_sems.append(p)
        return p

    pSY = newP("pSY")
    pTE = newP("pTE")
    pVE = newP("pVE")
    pSE = newP("pSE")
    pGE = newP("pGE")
    pGD = newP("pGD")
    sSU = newP("sSU")
    sGU = newP("sGU")
    wLD = [newP(f"wLD{i}") for i in range(2)]
    wST = [newP(f"wST{i}") for i in range(2)]
    xLD = [newP(f"xLD{i}") for i in range(2)]
    xVS = [newP(f"xVS{i}") for i in range(2)]
    xFS = [newP(f"xFS{i}") for i in range(2)]
    aQ = newP("aQ")
    aKT = [newP(f"aKT{i}") for i in range(2)]
    aV = [newP(f"aV{i}") for i in range(2)]
    aST = [newP(f"aST{i}") for i in range(2)]
    gXN = newP("gXN")
    gWG = [newP(f"gWG{i}") for i in range(2)]
    gWU = [newP(f"gWU{i}") for i in range(2)]
    gGB = [newP(f"gGB{i}") for i in range(2)]
    gUB = [newP(f"gUB{i}") for i in range(2)]
    gHS = [newP(f"gHS{i}") for i in range(2)]
    dH = newP("dH")
    dSEG = [newP(f"dSEG{i}") for i in range(2)]
    dDB = [newP(f"dDB{i}") for i in range(2)]
    dSL = [newP(f"dSL{i}") for i in range(2)]
    dPS = [newP(f"dPS{i}") for i in range(2)]
    eP = [newP(f"eP{i}") for i in range(2)]
    eX = [newP(f"eX{i}") for i in range(2)]
    eO = [newP(f"eO{i}") for i in range(2)]
    sDBG = newP("sDBG")
    wc = W()

    def ser(p, eng, inst):
        wc.wait(eng, p, p.n)
        return p.inc(inst, 16)

    def vop(inst):
        v = pVE.inc(inst)
        wc.wait(ve, pVE, v)
        return v

    def barrier():
        for eng in engines:
            for p in all_sems:
                wc.wait(eng, p, p.n)

    ident_bf = es0.enter_context(nc.sbuf_tensor("ident_bf", [128, 128], BF16))
    dcmp = es0.enter_context(nc.sbuf_tensor("dcmp", [128, QT], F32))
    m_col = es0.enter_context(nc.sbuf_tensor("m_col", [128, R], F32))
    pid_sb = es0.enter_context(nc.sbuf_tensor("pid_sb", [1, 1], U32))
    ag_sb = es0.enter_context(nc.sbuf_tensor("ag_sb", [128, NC_D * R], BF16))
    ua_sb = es0.enter_context(nc.sbuf_tensor("ua_sb", [128, NC_D * R], BF16))
    da_sb = es0.enter_context(nc.sbuf_tensor("da_sb", [128, NC_I * R], BF16))
    esW = ExitStack()
    qraw = [esW.enter_context(nc.sbuf_tensor(f"qraw{i}", [128, 5504], I32)) for i in range(2)]
    wcvt = [esW.enter_context(nc.sbuf_tensor(f"wcvt{i}", [128, 5504], BF16)) for i in range(2)]
    wstage = [esW.enter_context(nc.sbuf_tensor(f"wstage{i}", [128, 5504], BF16)) for i in range(2)]
    scol = esW.enter_context(nc.sbuf_tensor("scol", [128, 2 * NC_I + NC_D], F32))
    bcl = esW.enter_context(nc.sbuf_tensor("bcl", [128, 2 * NC_I + NC_D], F32))
    esSX = ExitStack()
    vb_rep = esSX.enter_context(nc.sbuf_tensor("vb_rep", [128, DM], BF16))

    with ExitStack() as esS:
        ident_f = esS.enter_context(nc.sbuf_tensor("ident_f", [128, 128], F32))
        bv_sb = esS.enter_context(nc.sbuf_tensor("bv_sb", [NH, 3 * HD], F32))
        sgn = esS.enter_context(nc.sbuf_tensor("sgn", [NH, 3 * HD], F32))
        tmp1 = esS.enter_context(nc.sbuf_tensor("tmp1", [NH, HD], F32))
        alpha = esS.enter_context(nc.sbuf_tensor("alpha", [NH, 4], F32))
        mnat = esS.enter_context(nc.sbuf_tensor("mnat", [NH, HD], F32))
        pad = esS.enter_context(nc.sbuf_tensor("pad", [128, 128], F32))
        bcol = esS.enter_context(nc.sbuf_tensor("bcol", [R, 5504], F32))
        bcol_b = esS.enter_context(nc.sbuf_tensor("bcol_b", [R, 5504], BF16))
        acol = esS.enter_context(nc.sbuf_tensor("acol", [128, NC_I * R // 2], F32))
        ones1 = esS.enter_context(nc.sbuf_tensor("ones1", [1, 128], F32))
        vbrow = esS.enter_context(nc.sbuf_tensor("vbrow", [1, DM], F32))
        pS1 = esS.enter_context(nc.psum_tensor("pS1", [128, 512], F32))
        pS2 = esS.enter_context(nc.psum_tensor("pS2", [128, 512], F32))

        v0 = ser(sSU, sy, sy.dma_start(pid_sb[:], nc.partition_id_tensor.ap()))
        vg = pGE.inc(ge.iota(dcmp[:, :], pattern=[[1, QT]], base=0, channel_multiplier=-1,
                             allow_small_or_imprecise_dtypes=True))
        wc.wait(ge, pGE, vg)
        vg = pGE.inc(ge.memset(ident_bf[:], 0.0))
        wc.wait(ge, pGE, vg)
        vg = pGE.inc(ge.affine_select(
            out=ident_bf[:], in_=ident_bf[:], compare_op=ALU.not_equal,
            fill=1.0, base=0, pattern=[[-1, 128]], channel_multiplier=1))
        vg2 = pGE.inc(ge.memset(ident_f[:], 0.0))
        wc.wait(ge, pGE, vg2)
        vg2 = pGE.inc(ge.affine_select(
            out=ident_f[:], in_=ident_f[:], compare_op=ALU.not_equal,
            fill=1.0, base=0, pattern=[[-1, 128]], channel_multiplier=1))
        vg3 = pGE.inc(ge.memset(ones1[:], 1.0))
        vg4 = pGE.inc(ge.memset(pad[:], 0.0))

        vq = ser(sSU, sy, sy.dma_start(bv_sb[:, 0:HD], bvq.ap()))
        vq = ser(sSU, sy, sy.dma_start(bv_sb[:, HD:2 * HD], bvk.ap()))
        vq = ser(sSU, sy, sy.dma_start(bv_sb[:, 2 * HD:3 * HD], bvv.ap()))
        wc.wait(se, sSU, vq)
        vs1 = pSE.inc(se.sign(sgn[:, :], bv_sb[:, :]))
        wc.wait(ve, sSU, vq)
        vv = vop(ve.reduce_sum(alpha[:, 0:1], bv_sb[:, 0:HD],
                                   axis=mybir.AxisListType.X, apply_absolute_value=True))
        vv = vop(ve.reduce_sum(alpha[:, 1:2], bv_sb[:, HD:2 * HD],
                                   axis=mybir.AxisListType.X, apply_absolute_value=True))
        vv = vop(ve.reduce_sum(alpha[:, 2:3], bv_sb[:, 2 * HD:3 * HD],
                                   axis=mybir.AxisListType.X, apply_absolute_value=True))
        wc.wait(ve, pSE, vs1)
        vv = vop(ve.tensor_tensor(tmp1[:, :], sgn[:, 0:HD], sgn[:, 0:HD], ALU.mult))
        vv = vop(ve.tensor_scalar(tmp1[:, :], tmp1[:, :], -1.0, 1.0, ALU.mult, ALU.add))
        vv = vop(ve.tensor_tensor(sgn[:, 0:HD], sgn[:, 0:HD], tmp1[:, :], ALU.add))
        vv = vop(ve.tensor_tensor(tmp1[:, :], sgn[:, HD:2 * HD], sgn[:, HD:2 * HD], ALU.mult))
        vv = vop(ve.tensor_scalar(tmp1[:, :], tmp1[:, :], -1.0, 1.0, ALU.mult, ALU.add))
        vv = vop(ve.tensor_tensor(sgn[:, HD:2 * HD], sgn[:, HD:2 * HD], tmp1[:, :], ALU.add))
        vv = vop(ve.tensor_tensor(tmp1[:, :], sgn[:, 2 * HD:3 * HD], sgn[:, 2 * HD:3 * HD], ALU.mult))
        vv = vop(ve.tensor_scalar(tmp1[:, :], tmp1[:, :], -1.0, 1.0, ALU.mult, ALU.add))
        vv = vop(ve.tensor_tensor(sgn[:, 2 * HD:3 * HD], sgn[:, 2 * HD:3 * HD], tmp1[:, :], ALU.add))
        vv = vop(ve.tensor_tensor(mnat[:, :], sgn[:, 0:HD], sgn[:, HD:2 * HD], ALU.mult))
        vv = vop(ve.tensor_tensor(alpha[:, 3:4], alpha[:, 0:1], alpha[:, 1:2], ALU.mult))
        vv = vop(ve.tensor_scalar(alpha[:, 3:4], alpha[:, 3:4], 1.0 / (HD * HD), None, ALU.mult))
        vv = vop(ve.tensor_scalar(mnat[:, :], mnat[:, :], alpha[:, 3:4], None, ALU.mult))
        vv = vop(ve.tensor_scalar(alpha[:, 2:3], alpha[:, 2:3], 1.0 / HD, None, ALU.mult))
        vv = vop(ve.tensor_scalar(sgn[:, 2 * HD:3 * HD], sgn[:, 2 * HD:3 * HD],
                                      alpha[:, 2:3], None, ALU.mult))
        wc.wait(ge, pVE, vv)
        vg5 = ser(sGU, ge, ge.dma_start(mvec.ap(), mnat[:, :]))
        vg5 = ser(sGU, ge, ge.dma_start(vbvec.ap(), sgn[:, 2 * HD:3 * HD]))
        wc.wait(sy, sGU, vg5)
        wc.wait(sy, pGE, vg4)
        v1 = ser(sSU, sy, sy.dma_start(pad[0:R, :], bass.AP(mvec, 0, [[128, R], [1, 128]])))
        wc.wait(te, sSU, v1)
        wc.wait(te, pGE, vg4)
        wc.wait(te, pGE, vg2)
        vt = pTE.inc(te.transpose(pS1[:, 0:128], pad[:, :], ident_f[:]))
        wc.wait(ve, pTE, vt)
        vv = pVE.inc(ve.tensor_copy(m_col[:, :], pS1[:, 0:R]))
        v2 = ser(sSU, sy, sy.dma_start(vbrow[0:1, :], bass.AP(vbvec, 0, [[DM, 1], [1, DM]])))
        wc.wait(te, sSU, v2)
        wc.wait(te, pGE, vg3)
        vt = pTE.inc(te.matmul(pS1[:, :], ones1[0:1, :], vbrow[0:1, 0:512], start=True, stop=True))
        vt = pTE.inc(te.matmul(pS2[:, :], ones1[0:1, :], vbrow[0:1, 512:1024], start=True, stop=True))
        wc.wait(ve, pTE, vt)
        vv = pVE.inc(ve.tensor_copy(vb_rep[:, 0:512], pS1[:, :]))
        vv = pVE.inc(ve.tensor_copy(vb_rep[:, 512:1024], pS2[:, :]))
        for half in range(1, 4):
            wc.wait(te, pVE, vv)
            vt = pTE.inc(te.matmul(pS1[:, :], ones1[0:1, :],
                                   vbrow[0:1, half * 1024:half * 1024 + 512], start=True, stop=True))
            vt = pTE.inc(te.matmul(pS2[:, :], ones1[0:1, :],
                                   vbrow[0:1, half * 1024 + 512:half * 1024 + 1024], start=True, stop=True))
            wc.wait(ve, pTE, vt)
            vv = pVE.inc(ve.tensor_copy(vb_rep[:, half * 1024:half * 1024 + 512], pS1[:, :]))
            vv = pVE.inc(ve.tensor_copy(vb_rep[:, half * 1024 + 512:half * 1024 + 1024], pS2[:, :]))

        for srcb, dst in ((gb, gb_bf), (ub, ub_bf), (db, db_bf)):
            n = srcb.shape[1]
            for c0 in range(0, n, 5504):
                cw = min(5504, n - c0)
                wc.wait(sy, pVE, vv)
                v3 = ser(sSU, sy, sy.dma_start(
                    bcol[0:R, 0:cw], bass.AP(srcb, c0, [[n, R], [1, cw]])))
                wc.wait(ve, sSU, v3)
                wc.wait(ve, sGU, sGU.n)
                vv = pVE.inc(ve.tensor_copy(bcol_b[0:R, 0:cw], bcol[0:R, 0:cw]))
                wc.wait(ge, pVE, vv)
                ser(sGU, ge, ge.dma_start(bass.AP(dst, c0, [[n, R], [1, cw]]), bcol_b[0:R, 0:cw]))
        for srca, dst, nch in ((ga, ag_sb, NC_D), (ua, ua_sb, NC_D), (da, da_sb, NC_I)):
            for h0 in range(0, nch, NC_I // 2):
                hw = min(NC_I // 2, nch - h0)
                wc.wait(sy, pVE, vv)
                v4 = ser(sSU, sy, sy.dma_start(
                    acol[:, 0:hw * R],
                    bass.AP(srca, h0 * 128 * R, [[R, 128], [128 * R, hw], [1, R]])))
                wc.wait(ve, sSU, v4)
                vv = pVE.inc(ve.tensor_copy(dst[:, h0 * R:(h0 + hw) * R], acol[:, 0:hw * R]))

        ji = esS.enter_context(nc.sbuf_tensor("ji", [1, 2], U32))
        jf = esS.enter_context(nc.sbuf_tensor("jf", [1, 2], F32))
        jcol = esS.enter_context(nc.sbuf_tensor("jcol", [128, 1], F32))
        wc.wait(ve, sSU, v0)
        vv = pVE.inc(ve.tensor_scalar(ji[0:1, 0:1], pid_sb[0:1, 0:1], 3, None, ALU.bitwise_and))
        wc.wait(ve, pVE, vv)
        vv = pVE.inc(ve.tensor_copy(jf[0:1, 0:1], ji[0:1, 0:1]))
        wc.wait(ve, pVE, vv)
        vv = pVE.inc(ve.tensor_scalar(jf[0:1, 0:1], jf[0:1, 0:1], float(QT), None, ALU.mult))
        wc.wait(te, pVE, vv)
        wc.wait(te, pGE, vg3)
        vt = pTE.inc(te.matmul(pS1[:, 0:1], ones1[0:1, :], jf[0:1, 0:1], start=True, stop=True))
        wc.wait(ve, pTE, vt)
        vv = pVE.inc(ve.tensor_copy(jcol[:, 0:1], pS1[:, 0:1]))
        wc.wait(ge, pVE, vv)
        vg = pGE.inc(ge.tensor_scalar(dcmp[:, :], dcmp[:, :], jcol[:, 0:1], None, ALU.add))
        vv_sc = 0
        spad_s = pad
        for (scsrc, nrows, off) in ((gsc, NC_I, 0), (usc, NC_I, NC_I), (dsc, NC_D, 2 * NC_I)):
            wc.wait(sy, pGE, vg4)
            wc.wait(sy, pTE, pTE.n)
            v1 = ser(sSU, sy, sy.dma_start(spad_s[0:nrows, :], scsrc.ap()))
            wc.wait(te, sSU, v1)
            wc.wait(te, pVE, pVE.n)
            vt = pTE.inc(te.transpose(pS1[:, 0:128], spad_s[:, :], ident_f[:]))
            wc.wait(ve, pTE, vt)
            vv = vop(ve.tensor_copy(scol[:, off:off + nrows], pS1[:, 0:nrows]))
        vv = vop(ve.tensor_scalar(scol[:, :], scol[:, :], 0.005, None, ALU.mult))
        vv_sc = vop(ve.tensor_scalar(bcl[:, :], scol[:, :], -7.5, None, ALU.mult))
        wc.wait(sy, sSU, v0)
        rj_sy = sy.alloc_register("rj_sy")
        sy.reg_load(rj_sy, pid_sb[0:1, 0:1])
        sy.reg_alu(rj_sy, rj_sy, 3, ALU.bitwise_and)
        sy.reg_alu(rj_sy, rj_sy, QT, ALU.mult)
        rg_sy = sy.alloc_register("rg_sy")
        c1024 = sy.alloc_register("c1024")
        sy.reg_mov(c1024, 1024)
        barrier()

    esWX = ExitStack()
    pXA = esWX.enter_context(nc.psum_tensor("pXA", [128, 2048], F32))
    pSC = [esWX.enter_context(nc.psum_tensor(f"pSC{i}", [128, 512], F32)) for i in range(2)]
    pT = [esWX.enter_context(nc.psum_tensor(f"pT{i}", [128, 1024], BF16)) for i in range(2)]

    pts = {"n": 0, "tr": {}, "ev": {}}

    def pt_fill(src_ap_fn, nch, evac_fn):
        par = pts["n"] % 2
        wc.wait(te, pVE, pts["ev"].get(pts["n"] - 2, 0))
        inst = None
        for ci in range(nch):
            inst = te.transpose(pT[par][:, ci * 128:(ci + 1) * 128], src_ap_fn(ci), ident_bf[:])
        pts["tr"][pts["n"]] = pTE.inc(inst)
        wc.wait(ve, pTE, pts["tr"][pts["n"]])
        pts["ev"][pts["n"]] = pVE.inc(evac_fn(pT[par][:, 0:nch * 128]))
        pts["n"] += 1
        return pts["tr"][pts["n"] - 1], pts["ev"][pts["n"] - 1]

    W_TILES = []
    for oc in range(NC_I):
        W_TILES.append((gq, oc * 128 * DM, DM, oc, wgT, oc * 128 * DM))
    for oc in range(NC_I):
        W_TILES.append((uq, oc * 128 * DM, DM, NC_I + oc, wuT, oc * 128 * DM))
    for oc in range(NC_D):
        for half in range(2):
            W_TILES.append((dq, oc * 128 * II + half * 5504, 5504, 2 * NC_I + oc,
                            wdT, oc * 128 * II + half * 5504))
    wst = {"n": 0, "ld": {}, "cv": {}, "tr": {}, "ev": {}, "st": {}}

    def w_step():
        n = wst["n"]
        NW = len(W_TILES)
        if n >= NW + 2:
            return
        if n < NW:
            srct, soff, cols, sci, dstt, doff = W_TILES[n]
            if n >= 2:
                wc.wait(ge, pSE, wst["cv"][n - 2])
            wst["ld"][n] = wLD[n % 2].inc(ge.dma_start(
                qraw[n % 2][:, 0:cols],
                bass.AP(srct, soff, [[cols if srct is not dq else II, 128], [1, cols]])), 16)
        m = n - 1
        if 0 <= m < NW:
            srct, soff, cols, sci, dstt, doff = W_TILES[m]
            wc.wait(se, wLD[m % 2], wst["ld"][m])
            wc.wait(se, pVE, vv_sc)
            if m >= 2:
                wc.wait(se, pTE, wst["tr"][m - 2])
            wst["cv"][m] = pSE.inc(se.activation(
                wcvt[m % 2][:, 0:cols], qraw[m % 2][:, 0:cols], AF.Identity,
                bias=bcl[:, sci:sci + 1], scale=scol[:, sci:sci + 1]))
        t = n - 2
        if 0 <= t < NW:
            srct, soff, cols, sci, dstt, doff = W_TILES[t]
            nch = cols // 128
            wc.wait(te, pSE, wst["cv"][t])
            ev_last = 0
            for g0 in range(0, nch, 8):
                gn = min(8, nch - g0)
                par = t % 2
                def mksrc(ci, _g0=g0, _par=par):
                    return wcvt[_par][:, (_g0 + ci) * 128:(_g0 + ci + 1) * 128]
                def mkev(ps_ap, _g0=g0, _gn=gn, _par=par, _t=t):
                    if _g0 == 0 and _t >= 2:
                        wc.wait(ve, wST[_t % 2], wst["st"][_t - 2])
                    return ve.tensor_copy(wstage[_par][:, _g0 * 128:(_g0 + _gn) * 128], ps_ap)
                trv, ev_last = pt_fill(mksrc, gn, mkev)
            wst["tr"][t] = pts["tr"][pts["n"] - 1]
            wst["ev"][t] = ev_last
            wc.wait(ge, pVE, ev_last)
            wst["st"][t] = wST[t % 2].inc(ge.dma_start(
                bass.AP(dstt, doff, [[DM if dstt is not wdT else II, 128], [1, cols]]),
                wstage[t % 2][:, 0:cols]), 16)
        wst["n"] += 1

    def w_drain():
        while wst["n"] < len(W_TILES) + 2:
            w_step()

    with ExitStack() as esX:
        xt = [esX.enter_context(nc.sbuf_tensor(f"xt{i}", [128, DM], F32)) for i in range(2)]
        xnt = [esX.enter_context(nc.sbuf_tensor(f"xnt{i}", [128, DM], BF16)) for i in range(2)]
        vt_ = [esX.enter_context(nc.sbuf_tensor(f"vt{i}", [128, DM], BF16)) for i in range(2)]
        fmst = [esX.enter_context(nc.sbuf_tensor(f"fmst{i}", [128, NC_D * 256], BF16)) for i in range(2)]
        stat = esX.enter_context(nc.sbuf_tensor("stat", [128, 16], F32))
        eps_sb = esX.enter_context(nc.sbuf_tensor("eps_sb", [128, 1], F32))
        v_eps = pGE.inc(ge.memset(eps_sb[:], 1e-5))

        NT = SEQ // 128
        ld_x = [0] * NT
        sq_x = [0] * NT
        st_stat = [0] * NT
        rs_x = [0] * NT
        xn_x = [0] * NT
        vv_x = [0] * NT
        tr_x = [0] * NT
        evq_x = [0] * NT
        stv_x = [0] * NT
        stf_x = {}

        def x_front(i):
            par = i % 2
            if i >= 2:
                wc.wait(sy, pSE, sq_x[i - 2])
                wc.wait(sy, pGE, xn_x[i - 2])
            ld_x[i] = xLD[par].inc(sy.dma_start(
                xt[par][:], bass.AP(x_seq, i * 128 * DM, [[DM, 128], [1, DM]])), 16)
            wc.wait(se, xLD[par], ld_x[i])
            if i >= 2:
                wc.wait(se, pVE, st_stat[i - 2])
            se.activation(pXA[:, :], xt[par][:, 0:2048], AF.Square,
                          accum_out=stat[:, 0 + par:1 + par])
            sq_x[i] = pSE.inc(se.activation(pXA[:, :], xt[par][:, 2048:4096], AF.Square,
                                            accum_out=stat[:, 12 + par:13 + par]))
            wc.wait(ve, xLD[par], ld_x[i])
            if i >= 2:
                wc.wait(ve, pGE, xn_x[i - 2])
            vop(ve.reduce_sum(stat[:, 2 + par:3 + par], xt[par][:],
                              axis=mybir.AxisListType.X, negate=True))
            wc.wait(ve, pSE, sq_x[i])
            vop(ve.tensor_scalar(stat[:, 2 + par:3 + par], stat[:, 2 + par:3 + par],
                                 1.0 / DM, None, ALU.mult))
            vop(ve.tensor_tensor(stat[:, 6 + par:7 + par], stat[:, 2 + par:3 + par],
                                 stat[:, 2 + par:3 + par], ALU.mult))
            vop(ve.tensor_tensor(stat[:, 4 + par:5 + par], stat[:, 0 + par:1 + par],
                                 stat[:, 12 + par:13 + par], ALU.add))
            vop(ve.tensor_scalar(stat[:, 4 + par:5 + par], stat[:, 4 + par:5 + par],
                                 1.0 / DM, None, ALU.mult))
            st_stat[i] = vop(ve.tensor_tensor(stat[:, 4 + par:5 + par], stat[:, 4 + par:5 + par],
                                              stat[:, 6 + par:7 + par], ALU.subtract))

        def x_back(i):
            par = i % 2
            wc.wait(se, pVE, st_stat[i])
            if i >= 2:
                wc.wait(se, pGE, xn_x[i - 2])
            wc.wait(se, pGE, v_eps)
            vse = pSE.inc(se.activation(stat[:, 10 + par:11 + par], stat[:, 4 + par:5 + par],
                                        AF.Ln, bias=eps_sb[:, 0:1]))
            wc.wait(se, pSE, vse)
            rs_x[i] = pSE.inc(se.activation(stat[:, 8 + par:9 + par], stat[:, 10 + par:11 + par],
                                            AF.Exp, scale=-0.5))
            wc.wait(ge, pSE, rs_x[i])
            if i >= 2:
                wc.wait(ge, pTE, tr_x[i - 2])
                wc.wait(ge, xVS[par], stv_x[i - 2])
            xn_x[i] = pGE.inc(ge.tensor_scalar(xnt[par][:], xt[par][:],
                                               stat[:, 2 + par:3 + par], stat[:, 8 + par:9 + par],
                                               ALU.add, ALU.mult))
            wc.wait(ge, pGE, xn_x[i])
            vv_x[i] = pGE.inc(ge.tensor_tensor(vt_[par][:], xnt[par][:], vb_rep[:], ALU.mult))
            wc.wait(te, pGE, xn_x[i])
            gpar = (i // 2) % 2
            ev_last = 0
            for f in range(4):
                def mksrc(ci, _f=f, _par=par):
                    return xnt[_par][:, (_f * 8 + ci) * 128:(_f * 8 + ci + 1) * 128]
                def mkev(ps_ap, _f=f, _par=par, _i=i, _gpar=gpar):
                    if _f == 0 and _i % 2 == 0 and _i >= 4:
                        wc.wait(ve, xFS[_gpar], stf_x[_i - 4])
                    return ve.tensor_copy(
                        bass.AP(fmst[_gpar], (_i % 2) * 128 + _f * 8 * 256,
                                [[NC_D * 256, 128], [256, 8], [1, 128]]), ps_ap)
                trv, ev_last = pt_fill(mksrc, 8, mkev)
            tr_x[i] = pts["tr"][pts["n"] - 1]
            evq_x[i] = ev_last
            wc.wait(ge, pGE, vv_x[i])
            stv_x[i] = xVS[par].inc(ge.dma_start(
                bass.AP(vs, i * 128 * DM, [[DM, 128], [1, DM]]), vt_[par][:]), 16)
            if i % 2 == 1:
                g2 = i // 2
                wc.wait(ge, pVE, evq_x[i])
                wc.wait(ge, pVE, evq_x[i - 1])
                stf_x[g2 * 2] = xFS[g2 % 2].inc(ge.dma_start(
                    bass.AP(xnfm, g2 * 256, [[SEQ, 128], [128 * SEQ, NC_D], [1, 256]]),
                    fmst[g2 % 2][:]), 16)

        for i in range(NT + 1):
            if i < NT:
                x_front(i)
            if i >= 1:
                x_back(i - 1)
            w_step()
        barrier()

    esSX.close()

    with ExitStack() as esA:
        q_sb = esA.enter_context(nc.sbuf_tensor("q_sb", [128, NC_D * QT], BF16))
        kt_b = [esA.enter_context(nc.sbuf_tensor(f"kt{i}", [128, 8 * 512], BF16)) for i in range(2)]
        v_b = [esA.enter_context(nc.sbuf_tensor(f"v_b{i}", [128, 8 * 512], BF16)) for i in range(2)]
        at_all = esA.enter_context(nc.sbuf_tensor("at_all", [128, 64 * QT], BF16))
        st_st = [esA.enter_context(nc.sbuf_tensor(f"st_st{i}", [128, 8 * QT], BF16)) for i in range(2)]

        GSZ = 4
        kt_free = [0, 0]
        v_free = [0, 0]
        ev_done = pVE.n
        st_done = {}
        hix = 0
        for g in range(NG):
            trips = 8 * (g + 1)
            if g == 0:
                sy.reg_alu(rg_sy, rj_sy, 0, ALU.add)
            else:
                sy.reg_alu(rg_sy, rg_sy, c1024, ALU.add)
            wc.wait(sy, pTE, pTE.n)
            vq = aQ.inc(sy.dma_start(
                q_sb[:],
                bass.AP(xnfm, rg_sy, [[SEQ, 128], [128 * SEQ, NC_D], [1, QT]])), 16)
            wc.wait(ge, aQ, vq)
            for c in range(NC_D):
                inst = ge.tensor_scalar(q_sb[:, c * QT:(c + 1) * QT], q_sb[:, c * QT:(c + 1) * QT],
                                        m_col[:, c:c + 1], None, ALU.mult)
            qm_done = pGE.inc(inst)

            for h in range(NH):
                sg_v = {}
                mk_v = {}
                kt_ld = {}
                for k in range(trips):
                    grp, st_ = k // GSZ, k % GSZ
                    if st_ == 0:
                        wc.wait(sy, pTE, kt_free[grp % 2])
                        kt_ld[grp] = aKT[grp % 2].inc(sy.dma_start(
                            kt_b[grp % 2][:],
                            bass.AP(xnfm, 8 * h * 128 * SEQ + grp * GSZ * 128,
                                    [[SEQ, 128], [128 * SEQ, 8], [1, GSZ * 128]])), 16)
                    pk = pSC[k % 2]
                    wc.wait(te, aKT[grp % 2], kt_ld[grp])
                    wc.wait(te, pGE, qm_done)
                    if k >= 2:
                        wc.wait(te, pSE, sg_v[k - 2])
                    else:
                        wc.wait(te, pVE, ev_done)
                    for dc in range(8):
                        inst = te.matmul(pk[:, 0:QT],
                                         kt_b[grp % 2][:, dc * 512 + st_ * 128: dc * 512 + (st_ + 1) * 128],
                                         q_sb[:, (8 * h + dc) * QT:(8 * h + dc + 1) * QT],
                                         start=(dc == 0), stop=(dc == 7))
                    sc_v = pTE.inc(inst)
                    kt_free[grp % 2] = sc_v
                    wc.wait(se, pTE, sc_v)
                    sg_v[k] = pSE.inc(se.activation(at_all[:, k * QT:(k + 1) * QT], pk[:, 0:QT],
                                                    AF.Sigmoid, scale=0.125))
                    wc.wait(ve, pSE, sg_v[k])
                    mk_v[k] = pVE.inc(ve.scalar_tensor_tensor(
                        at_all[:, k * QT:(k + 1) * QT], dcmp[:, :],
                        float(128 * k - 1024 * g),
                        at_all[:, k * QT:(k + 1) * QT],
                        ALU.is_ge, ALU.mult))
                    if st_ == GSZ - 1 and (k // GSZ) % 2 == 1:
                        w_step()
                b_last = 0
                v_ld = {}
                for k in range(trips):
                    grp, st_ = k // GSZ, k % GSZ
                    if st_ == 0:
                        wc.wait(sy, pTE, v_free[grp % 2])
                        v_ld[grp] = aV[grp % 2].inc(sy.dma_start(
                            v_b[grp % 2][:],
                            bass.AP(vs, grp * GSZ * 128 * DM + h * 1024,
                                    [[DM, 128], [128 * DM, GSZ], [1, 1024]])), 16)
                    wc.wait(te, aV[grp % 2], v_ld[grp])
                    if k == 0:
                        wc.wait(te, pVE, mk_v[trips - 1])
                        wc.wait(te, pVE, ev_done)
                    for dc in range(8):
                        inst = te.matmul(
                            pXA[:, (dc % 4) * 512 + (dc // 4) * QT: (dc % 4) * 512 + (dc // 4) * QT + QT],
                            v_b[grp % 2][:, st_ * 1024 + dc * 128: st_ * 1024 + (dc + 1) * 128],
                            at_all[:, k * QT:(k + 1) * QT],
                            start=(k == 0 and dc < 4), stop=(k == trips - 1))
                    b_last = pTE.inc(inst)
                    v_free[grp % 2] = b_last
                    if st_ == GSZ - 1 and (k // GSZ) % 2 == 0:
                        w_step()
                hpar = hix % 2
                wc.wait(ve, pTE, b_last)
                if hix >= 2:
                    wc.wait(ve, aST[hpar], st_done[hix - 2])
                for dc in range(8):
                    inst = ve.tensor_copy(
                        st_st[hpar][:, dc * QT:(dc + 1) * QT],
                        pXA[:, (dc % 4) * 512 + (dc // 4) * QT: (dc % 4) * 512 + (dc // 4) * QT + QT])
                ev_done = pVE.inc(inst)
                wc.wait(ge, pVE, ev_done)
                st_done[hix] = aST[hpar].inc(ge.dma_start(
                    bass.AP(ss, 8 * h * 128 * TLOC + g * QT,
                            [[TLOC, 128], [128 * TLOC, 8], [1, QT]]),
                    st_st[hpar][:]), 16)
                hix += 1
        w_drain()
        barrier()
    esWX.close()
    esW.close()

    esP = ExitStack()
    pb = [esP.enter_context(nc.psum_tensor(f"pb{i}", [128, 512], F32)) for i in range(8)]

    with ExitStack() as esG:
        xn_res = esG.enter_context(nc.sbuf_tensor("xn_res", [128, NC_D * TLOC], BF16))
        wg_b = [esG.enter_context(nc.sbuf_tensor(f"wg_b{i}", [128, DM], BF16)) for i in range(2)]
        wu_b = [esG.enter_context(nc.sbuf_tensor(f"wu_b{i}", [128, DM], BF16)) for i in range(2)]
        gb_sl = [esG.enter_context(nc.sbuf_tensor(f"gb_sl{i}", [R, 128], BF16)) for i in range(2)]
        ub_sl = [esG.enter_context(nc.sbuf_tensor(f"ub_sl{i}", [R, 128], BF16)) for i in range(2)]
        zg_sb = esG.enter_context(nc.sbuf_tensor("zg_sb", [R, TLOC], BF16))
        zu_sb = esG.enter_context(nc.sbuf_tensor("zu_sb", [R, TLOC], BF16))
        hg = [esG.enter_context(nc.sbuf_tensor(f"hg{i}", [128, 512], BF16)) for i in range(2)]
        hu = [esG.enter_context(nc.sbuf_tensor(f"hu{i}", [128, 512], BF16)) for i in range(2)]
        hst = [esG.enter_context(nc.sbuf_tensor(f"hst{i}", [128, TLOC], BF16)) for i in range(2)]

        xn_ld = 0
        for g in range(NG):
            if g == 0:
                sy.reg_alu(rg_sy, rj_sy, 0, ALU.add)
            else:
                sy.reg_alu(rg_sy, rg_sy, c1024, ALU.add)
            xn_ld = gXN.inc(sy.dma_start(
                bass.AP(xn_res, g * QT, [[NC_D * TLOC, 128], [TLOC, NC_D], [1, QT]]),
                bass.AP(xnfm, rg_sy, [[SEQ, 128], [128 * SEQ, NC_D], [1, QT]])), 16)
        z_ev = 0
        for (asb, zsb) in ((ag_sb, zg_sb), (ua_sb, zu_sb)):
            for sub in range(4):
                pz = pb[6 + sub % 2]
                wc.wait(te, gXN, xn_ld)
                wc.wait(te, pSE, z_ev)
                for c in range(NC_D):
                    inst = te.matmul(pz[0:R, :], asb[:, c * R:(c + 1) * R],
                                     xn_res[:, c * TLOC + sub * 512: c * TLOC + (sub + 1) * 512],
                                     start=(c == 0), stop=(c == NC_D - 1))
                vz = pTE.inc(inst)
                wc.wait(se, pTE, vz)
                z_ev = pSE.inc(se.activation(zsb[0:R, sub * 512:(sub + 1) * 512], pz[0:R, :],
                                             AF.Identity, scale=0.02))

        ld_w = [0] * NC_I
        mm_oc = [0] * NC_I
        sl_v = {}
        ue_v = {}
        hm_v = {}
        hs_v = {}
        for oc in range(NC_I):
            par = oc % 2
            if oc >= 2:
                wc.wait(sy, pTE, mm_oc[oc - 2])
            vg1 = gWG[par].inc(sy.dma_start(wg_b[par][:], bass.AP(wgT, oc * 128 * DM, [[DM, 128], [1, DM]])), 16)
            vu1 = gWU[par].inc(sy.dma_start(wu_b[par][:], bass.AP(wuT, oc * 128 * DM, [[DM, 128], [1, DM]])), 16)
            vb1 = gGB[par].inc(sy.dma_start(gb_sl[par][0:R, :], bass.AP(gb_bf, oc * 128, [[II, R], [1, 128]])), 16)
            vu2 = gUB[par].inc(sy.dma_start(ub_sl[par][0:R, :], bass.AP(ub_bf, oc * 128, [[II, R], [1, 128]])), 16)
            ld_w[oc] = (par, vg1, vu1, vb1, vu2)
            last = 0
            for pr in range(2):
                subs = (2 * pr, 2 * pr + 1)
                pg_ = [pb[4 * pr + 0], pb[4 * pr + 1]]
                pu_ = [pb[4 * pr + 2], pb[4 * pr + 3]]
                _p, _v1, _v2, _v3, _v4 = ld_w[oc]
                wc.wait(te, gWG[_p], _v1)
                wc.wait(te, gWU[_p], _v2)
                wc.wait(te, gGB[_p], _v3)
                wc.wait(te, gUB[_p], _v4)
                wc.wait(te, pSE, z_ev)
                if oc >= 1:
                    for s2 in subs:
                        wc.wait(te, pSE, sl_v[(oc - 1, s2)])
                        wc.wait(te, pVE, ue_v[(oc - 1, s2)])
                for c in range(NC_D):
                    for i2, s2 in enumerate(subs):
                        te.matmul(pg_[i2][:, :], wg_b[par][:, c * 128:(c + 1) * 128],
                                  xn_res[:, c * TLOC + s2 * 512: c * TLOC + (s2 + 1) * 512],
                                  start=(c == 0), stop=False)
                for i2, s2 in enumerate(subs):
                    te.matmul(pg_[i2][:, :], gb_sl[par][0:R, :], zg_sb[0:R, s2 * 512:(s2 + 1) * 512],
                              start=False, stop=True)
                for c in range(NC_D):
                    for i2, s2 in enumerate(subs):
                        te.matmul(pu_[i2][:, :], wu_b[par][:, c * 128:(c + 1) * 128],
                                  xn_res[:, c * TLOC + s2 * 512: c * TLOC + (s2 + 1) * 512],
                                  start=(c == 0), stop=False)
                for i2, s2 in enumerate(subs):
                    inst = te.matmul(pu_[i2][:, :], ub_sl[par][0:R, :], zu_sb[0:R, s2 * 512:(s2 + 1) * 512],
                                     start=False, stop=True)
                last = pTE.inc(inst)
                for i2, s2 in enumerate(subs):
                    wc.wait(se, pTE, last)
                    if oc >= 1 or pr >= 1:
                        wc.wait(se, pGE, hm_v.get((oc, s2 - 2), hm_v.get((oc - 1, s2 + 2), 0)))
                    sl_v[(oc, s2)] = pSE.inc(se.activation(hg[i2][:], pg_[i2][:, :], AF.Silu))
                    wc.wait(ve, pTE, last)
                    if oc >= 1 or pr >= 1:
                        wc.wait(ve, pGE, hm_v.get((oc, s2 - 2), hm_v.get((oc - 1, s2 + 2), 0)))
                    ue_v[(oc, s2)] = pVE.inc(ve.tensor_copy(hu[i2][:], pu_[i2][:, :]))
                    wc.wait(ge, pSE, sl_v[(oc, s2)])
                    wc.wait(ge, pVE, ue_v[(oc, s2)])
                    if oc >= 2 and s2 == 0:
                        wc.wait(ge, gHS[par], hs_v[oc - 2])
                    hm_v[(oc, s2)] = pGE.inc(ge.tensor_tensor(
                        hst[par][:, s2 * 512:(s2 + 1) * 512], hg[i2][:], hu[i2][:], ALU.mult))
            mm_oc[oc] = last
            wc.wait(ge, pGE, hm_v[(oc, 3)])
            hs_v[oc] = gHS[par].inc(ge.dma_start(
                bass.AP(hs, oc * 128 * TLOC, [[TLOC, 128], [1, TLOC]]), hst[par][:]), 16)
        barrier()

    zd_dbg = nc.dram_tensor("d_zd", [2, R, 1024], BF16, kind="ExternalOutput") if dbg else None
    with ExitStack() as esD:
        h_res = esD.enter_context(nc.sbuf_tensor("h_res", [128, NC_I * 1024], BF16))
        wd_b = [esD.enter_context(nc.sbuf_tensor(f"wd_b{i}", [128, 29 * 128], BF16)) for i in range(2)]
        db_sl = [esD.enter_context(nc.sbuf_tensor(f"db_sl{i}", [R, 128], BF16)) for i in range(2)]
        zd_sb = esD.enter_context(nc.sbuf_tensor("zd_sb", [R, 1024], BF16))
        sd_sb = [esD.enter_context(nc.sbuf_tensor(f"sd_sb{i}", [128, 512], BF16)) for i in range(2)]
        s_sl = [esD.enter_context(nc.sbuf_tensor(f"s_sl{i}", [128, 512], BF16)) for i in range(2)]

        for half in range(2):
            h_ld = 0
            wc.wait(sy, pTE, pTE.n)
            for q4 in range(4):
                c0 = (NC_I * q4) // 4
                c1 = (NC_I * (q4 + 1)) // 4
                h_ld = dH.inc(sy.dma_start(
                    bass.AP(h_res, c0 * 1024, [[NC_I * 1024, 128], [1024, c1 - c0], [1, 1024]]),
                    bass.AP(hs, c0 * 128 * TLOC + half * 1024,
                            [[TLOC, 128], [128 * TLOC, c1 - c0], [1, 1024]])), 16)
            z_ev = 0
            for sub in range(2):
                pz = pb[6 + sub]
                wc.wait(te, dH, h_ld)
                wc.wait(te, pSE, z_ev)
                for c in range(NC_I):
                    inst = te.matmul(pz[0:R, :], da_sb[:, c * R:(c + 1) * R],
                                     h_res[:, c * 1024 + sub * 512: c * 1024 + (sub + 1) * 512],
                                     start=(c == 0), stop=(c == NC_I - 1))
                vz = pTE.inc(inst)
                wc.wait(se, pTE, vz)
                z_ev = pSE.inc(se.activation(zd_sb[0:R, sub * 512:(sub + 1) * 512], pz[0:R, :],
                                             AF.Identity, scale=0.02))
            if dbg:
                wc.wait(ge, pSE, z_ev)
                sDBG.inc(ge.dma_start(
                    bass.AP(zd_dbg, half * R * 1024, [[1024, R], [1, 1024]]),
                    zd_sb[0:R, :]), 16)

            ld_seg = {}
            seg_used = {}
            ld_db = {}
            ld_s = {}
            sil_v = {}
            pm_v = {}
            pst_v = {}
            SEGS = [(0, 29), (29, 58), (58, 86)]
            NSEG = 3
            def load_seg(oc, s_):
                pseg = (oc * NSEG + s_) % 2
                wc.wait(sy, pTE, seg_used.get(oc * NSEG + s_ - 2, 0))
                c0, c1 = SEGS[s_]
                ld_seg[oc * NSEG + s_] = dSEG[pseg].inc(sy.dma_start(
                    wd_b[pseg][:, 0:(c1 - c0) * 128],
                    bass.AP(wdT, oc * 128 * II + c0 * 128, [[II, 128], [1, (c1 - c0) * 128]])), 16)

            for oc in range(NC_D):
                load_seg(oc, 0)
                load_seg(oc, 1)
                wc.wait(sy, pTE, seg_used.get((oc - 2) * NSEG + NSEG - 1, 0) if oc >= 2 else 0)
                ld_db[oc] = dDB[oc % 2].inc(sy.dma_start(
                    db_sl[oc % 2][0:R, :], bass.AP(db_bf, oc * 128, [[DM, R], [1, 128]])), 16)
                for sub in range(2):
                    if oc >= 1:
                        wc.wait(sy, pGE, pm_v.get((oc - 1, sub), 0))
                    ld_s[(oc, sub)] = dSL[sub].inc(sy.dma_start(
                        s_sl[sub][:],
                        bass.AP(ss, oc * 128 * TLOC + half * 1024 + sub * 512,
                                [[TLOC, 128], [1, 512]])), 16)
                psd = [pb[(oc % 2) * 2], pb[(oc % 2) * 2 + 1]]
                lastmm = 0
                for s_ in range(NSEG):
                    pseg = (oc * NSEG + s_) % 2
                    c0, c1 = SEGS[s_]
                    wc.wait(te, dSEG[pseg], ld_seg[oc * NSEG + s_])
                    if s_ == NSEG - 1:
                        wc.wait(te, dDB[oc % 2], ld_db[oc])
                    wc.wait(te, pSE, z_ev)
                    inst = None
                    if s_ == 0 and oc >= 2:
                        for sub in range(2):
                            wc.wait(te, pSE, sil_v[(oc - 2, sub)])
                    for ci in range(c1 - c0):
                        c = c0 + ci
                        for sub in range(2):
                            inst = te.matmul(psd[sub][:, :], wd_b[pseg][:, ci * 128:(ci + 1) * 128],
                                             h_res[:, c * 1024 + sub * 512: c * 1024 + (sub + 1) * 512],
                                             start=(s_ == 0 and ci == 0), stop=False)
                    if s_ == NSEG - 1:
                        for sub in range(2):
                            inst = te.matmul(psd[sub][:, :], db_sl[oc % 2][0:R, :],
                                             zd_sb[0:R, sub * 512:(sub + 1) * 512],
                                             start=False, stop=True)
                        lastmm = pTE.inc(inst)
                    if s_ < NSEG - 1:
                        seg_used[oc * NSEG + s_] = pTE.inc(inst)
                        if s_ == 0:
                            load_seg(oc, 2)
                    else:
                        seg_used[oc * NSEG + s_] = lastmm
                for sub in range(2):
                    wc.wait(se, pTE, lastmm)
                    if oc >= 1:
                        wc.wait(se, dPS[sub], pst_v[(oc - 1, sub)])
                    sil_v[(oc, sub)] = pSE.inc(se.activation(sd_sb[sub][:], psd[sub][:, :], AF.Silu))
                    wc.wait(ge, pSE, sil_v[(oc, sub)])
                    wc.wait(ge, dSL[sub], ld_s[(oc, sub)])
                    pm_v[(oc, sub)] = pGE.inc(ge.tensor_tensor(sd_sb[sub][:], sd_sb[sub][:],
                                                               s_sl[sub][:], ALU.mult))
                    wc.wait(ge, pGE, pm_v[(oc, sub)])
                    pst_v[(oc, sub)] = dPS[sub].inc(ge.dma_start(
                        bass.AP(ps_, oc * 128 * TLOC + half * 1024 + sub * 512,
                                [[TLOC, 128], [1, 512]]),
                        sd_sb[sub][:]), 16)
        barrier()

    with ExitStack() as esE:
        ptm = [esE.enter_context(nc.sbuf_tensor(f"ptm{i}", [128, DM], BF16)) for i in range(2)]
        xep = [esE.enter_context(nc.sbuf_tensor(f"xep{i}", [128, DM], F32)) for i in range(2)]
        ot = [esE.enter_context(nc.sbuf_tensor(f"ot{i}", [128, DM], F32)) for i in range(2)]

        NE = TLOC // 128
        ld_p = [0] * NE
        ld_xe = [0] * NE
        ad_v = [0] * NE
        so_v = [0] * NE
        for i in range(NE):
            par = i % 2
            if i >= 2:
                wc.wait(sy, pVE, ad_v[i - 2])
            ld_p[i] = eP[par].inc(sy.dma_start(
                ptm[par][:], bass.AP(ps_, i * 128, [[TLOC, DM], [1, 128]]), transpose=True), 16)
            ld_xe[i] = eX[par].inc(sy.dma_start(
                xep[par][:], bass.AP(x_q, i * 128 * DM, [[DM, 128], [1, DM]])), 16)
            wc.wait(ve, eP[par], ld_p[i])
            wc.wait(ve, eX[par], ld_xe[i])
            if i >= 2:
                wc.wait(ve, eO[par], so_v[i - 2])
            ad_v[i] = pVE.inc(ve.tensor_tensor(ot[par][:], xep[par][:], ptm[par][:], ALU.add))
            wc.wait(ge, pVE, ad_v[i])
            so_v[i] = eO[par].inc(ge.dma_start(
                bass.AP(out_q, i * 128 * DM, [[DM, 128], [1, DM]]), ot[par][:]), 16)
        wc.wait(ge, eO[0], eO[0].n)
        wc.wait(ge, eO[1], eO[1].n)
    esP.close()
    if dbg:
        dbgmap = {
            "d_xnfm": (xnfm, [2, 128, SEQ]),
            "d_vs": (vs, [256, DM]),
            "d_wg": (wgT, [NC_I, 128, DM]),
            "d_wu": (wuT, [NC_I, 128, DM]),
            "d_wd": (wdT, [NC_D, 128, II]),
            "d_ss": (ss, [NC_D, 128, TLOC]),
            "d_hs": (hs, [NC_I, 128, TLOC]),
            "d_ps": (ps_, [DM, TLOC]),
        }
        vstart = pGE.n
        for name, (srct, shp) in dbgmap.items():
            dt_ = nc.dram_tensor(name, shp, BF16, kind="ExternalOutput")
            n_el = int(np.prod(shp))
            ser(sDBG, ge, ge.dma_start(bass.AP(dt_, 0, [[n_el, 1], [1, n_el]]),
                                       bass.AP(srct, 0, [[n_el, 1], [1, n_el]])))
        dt_ = nc.dram_tensor("d_dcmp", [128, QT], F32, kind="ExternalOutput")
        ser(sDBG, ge, ge.dma_start(dt_.ap(), dcmp[:, :]))
        dt_ = nc.dram_tensor("d_mcol", [128, R], F32, kind="ExternalOutput")
        ser(sDBG, ge, ge.dma_start(dt_.ap(), m_col[:, :]))
        wc.wait(ge, sDBG, sDBG.n)
    es0.close()
    return nc


_NC_CACHE = {}


def kernel(**inputs):
    x = np.asarray(inputs["L0_attn_x"])
    n_cores = 8
    in_maps = []
    rowsets = []
    common = {
        "gq": np.ascontiguousarray(inputs["L0_ffn_gate_q"], dtype=np.int32),
        "uq": np.ascontiguousarray(inputs["L0_ffn_up_q"], dtype=np.int32),
        "dq": np.ascontiguousarray(inputs["L0_ffn_down_q"], dtype=np.int32),
        "gsc": np.ascontiguousarray(np.asarray(inputs["L0_ffn_gate_scale"]).reshape(NC_I, 128)),
        "usc": np.ascontiguousarray(np.asarray(inputs["L0_ffn_up_scale"]).reshape(NC_I, 128)),
        "dsc": np.ascontiguousarray(np.asarray(inputs["L0_ffn_down_scale"]).reshape(NC_D, 128)),
        "ga": np.ascontiguousarray(inputs["L0_ffn_gate_A"], dtype=np.float32),
        "gb": np.ascontiguousarray(inputs["L0_ffn_gate_B"], dtype=np.float32),
        "ua": np.ascontiguousarray(inputs["L0_ffn_up_A"], dtype=np.float32),
        "ub": np.ascontiguousarray(inputs["L0_ffn_up_B"], dtype=np.float32),
        "da": np.ascontiguousarray(inputs["L0_ffn_down_A"], dtype=np.float32),
        "db": np.ascontiguousarray(inputs["L0_ffn_down_B"], dtype=np.float32),
        "bvq": np.ascontiguousarray(inputs["L0_attn_bv_q"], dtype=np.float32),
        "bvk": np.ascontiguousarray(inputs["L0_attn_bv_k"], dtype=np.float32),
        "bvv": np.ascontiguousarray(inputs["L0_attn_bv_v"], dtype=np.float32),
    }
    for c in range(n_cores):
        s, j = c // 4, c % 4
        rows = np.concatenate([np.arange(g * 1024 + j * QT, g * 1024 + (j + 1) * QT)
                               for g in range(NG)])
        rowsets.append((s, rows))
        m = dict(common)
        m["x_seq"] = np.ascontiguousarray(x[s], dtype=np.float32)
        m["x_q"] = np.ascontiguousarray(x[s][rows], dtype=np.float32)
        in_maps.append(m)

    if "nc" not in _NC_CACHE:
        _NC_CACHE["nc"] = emit(bass.Bass("TRN2", target_bir_lowering=False))
    nc = _NC_CACHE["nc"]
    res = run_bass_kernel_spmd(nc, in_maps, core_ids=list(range(n_cores)))
    out = np.empty((2, SEQ, DM), dtype=np.float32)
    for c in range(n_cores):
        s, rows = rowsets[c]
        out[s][rows] = res.results[c]["out_q"]
    return out


# version 11
# speedup vs baseline: 1.0711x; 1.0196x over previous
import numpy as np
import ml_dtypes
import concourse.bass as bass
import concourse.mybir as mybir
from concourse.bass_utils import run_bass_kernel_spmd

F32 = mybir.dt.float32
BF16 = mybir.dt.bfloat16
I32 = mybir.dt.int32
U32 = mybir.dt.uint32
AF = mybir.ActivationFunctionType
ALU = mybir.AluOpType

SEQ = 8192
DM = 4096
NH = 4
HD = 1024
II = 11008
NC_I = II // 128
NC_D = DM // 128
TLOC = 2048
NG = 8
QT = 256
R = 32


class P:
    def __init__(self, sem):
        self.sem = sem
        self.n = 0

    def inc(self, inst, k=1):
        inst.then_inc(self.sem, k)
        self.n += k
        return self.n


class W:
    def __init__(self):
        self.seen = {}

    def wait(self, eng, p, v):
        if v <= 0:
            return
        key = (id(eng), id(p))
        if self.seen.get(key, -1) >= v:
            return
        eng.wait_ge(p.sem, v)
        self.seen[key] = v


def emit(nc, dbg=()):
    x_seq = nc.dram_tensor("x_seq", [SEQ, DM], F32, kind="ExternalInput")
    x_q = nc.dram_tensor("x_q", [TLOC, DM], F32, kind="ExternalInput")
    gq = nc.dram_tensor("gq", [II, DM], I32, kind="ExternalInput")
    uq = nc.dram_tensor("uq", [II, DM], I32, kind="ExternalInput")
    dq = nc.dram_tensor("dq", [DM, II], I32, kind="ExternalInput")
    gsc = nc.dram_tensor("gsc", [NC_I, 128], F32, kind="ExternalInput")
    usc = nc.dram_tensor("usc", [NC_I, 128], F32, kind="ExternalInput")
    dsc = nc.dram_tensor("dsc", [NC_D, 128], F32, kind="ExternalInput")
    ga = nc.dram_tensor("ga", [DM, R], F32, kind="ExternalInput")
    gb = nc.dram_tensor("gb", [R, II], F32, kind="ExternalInput")
    ua = nc.dram_tensor("ua", [DM, R], F32, kind="ExternalInput")
    ub = nc.dram_tensor("ub", [R, II], F32, kind="ExternalInput")
    da = nc.dram_tensor("da", [II, R], F32, kind="ExternalInput")
    db = nc.dram_tensor("db", [R, DM], F32, kind="ExternalInput")
    bvq = nc.dram_tensor("bvq", [NH, HD], F32, kind="ExternalInput")
    bvk = nc.dram_tensor("bvk", [NH, HD], F32, kind="ExternalInput")
    bvv = nc.dram_tensor("bvv", [NH, HD], F32, kind="ExternalInput")
    out_q = nc.dram_tensor("out_q", [TLOC, DM], F32, kind="ExternalOutput")

    wgT = nc.dram_tensor("wgT", [NC_I, 128, DM], BF16, kind="Internal")
    wuT = nc.dram_tensor("wuT", [NC_I, 128, DM], BF16, kind="Internal")
    wdT = nc.dram_tensor("wdT", [NC_D, 128, II], BF16, kind="Internal")
    xnfm = nc.dram_tensor("xnfm", [NC_D, 128, SEQ], BF16, kind="Internal")
    vs = nc.dram_tensor("vs", [SEQ, DM], BF16, kind="Internal")
    hs = nc.dram_tensor("hs", [NC_I, 128, TLOC], BF16, kind="Internal")
    ss = nc.dram_tensor("ss", [NC_D, 128, TLOC], BF16, kind="Internal")
    ps_ = nc.dram_tensor("ps", [DM, TLOC], BF16, kind="Internal")
    mvec = nc.dram_tensor("mvec", [NH, HD], F32, kind="Internal")
    vbvec = nc.dram_tensor("vbvec", [NH, HD], F32, kind="Internal")
    gb_bf = nc.dram_tensor("gb_bf", [R, II], BF16, kind="Internal")
    ub_bf = nc.dram_tensor("ub_bf", [R, II], BF16, kind="Internal")
    db_bf = nc.dram_tensor("db_bf", [R, DM], BF16, kind="Internal")

    sy, te, ve, se, ge = nc.sync, nc.tensor, nc.vector, nc.scalar, nc.gpsimd
    engines = [sy, te, ve, se, ge]

    from contextlib import ExitStack
    es0 = ExitStack()

    all_sems = []

    def newP(name):
        p = P(es0.enter_context(nc.semaphore(name)))
        all_sems.append(p)
        return p

    pSY = newP("pSY")
    pTE = newP("pTE")
    pVE = newP("pVE")
    pSE = newP("pSE")
    pGE = newP("pGE")
    pGD = newP("pGD")
    sSU = newP("sSU")
    sGU = newP("sGU")
    wLD = [newP(f"wLD{i}") for i in range(2)]
    wST = [newP(f"wST{i}") for i in range(2)]
    xLD = [newP(f"xLD{i}") for i in range(2)]
    xVS = [newP(f"xVS{i}") for i in range(2)]
    xFS = [newP(f"xFS{i}") for i in range(2)]
    aQ = newP("aQ")
    aKT = [newP(f"aKT{i}") for i in range(2)]
    aV = [newP(f"aV{i}") for i in range(2)]
    aST = [newP(f"aST{i}") for i in range(2)]
    gXN = newP("gXN")
    gWG = [newP(f"gWG{i}") for i in range(2)]
    gWU = [newP(f"gWU{i}") for i in range(2)]
    gGB = [newP(f"gGB{i}") for i in range(2)]
    gUB = [newP(f"gUB{i}") for i in range(2)]
    gHS = [newP(f"gHS{i}") for i in range(2)]
    dH = newP("dH")
    dSEG = [newP(f"dSEG{i}") for i in range(2)]
    dDB = [newP(f"dDB{i}") for i in range(2)]
    dSL = [newP(f"dSL{i}") for i in range(2)]
    dPS = [newP(f"dPS{i}") for i in range(2)]
    eP = [newP(f"eP{i}") for i in range(2)]
    eX = [newP(f"eX{i}") for i in range(2)]
    eO = [newP(f"eO{i}") for i in range(2)]
    sDBG = newP("sDBG")
    wc = W()

    def ser(p, eng, inst):
        wc.wait(eng, p, p.n)
        return p.inc(inst, 16)

    def vop(inst):
        v = pVE.inc(inst)
        wc.wait(ve, pVE, v)
        return v

    def barrier():
        for eng in engines:
            for p in all_sems:
                wc.wait(eng, p, p.n)

    ident_bf = es0.enter_context(nc.sbuf_tensor("ident_bf", [128, 128], BF16))
    dcmp = es0.enter_context(nc.sbuf_tensor("dcmp", [128, QT], F32))
    m_col = es0.enter_context(nc.sbuf_tensor("m_col", [128, R], F32))
    pid_sb = es0.enter_context(nc.sbuf_tensor("pid_sb", [1, 1], U32))
    ag_sb = es0.enter_context(nc.sbuf_tensor("ag_sb", [128, NC_D * R], BF16))
    ua_sb = es0.enter_context(nc.sbuf_tensor("ua_sb", [128, NC_D * R], BF16))
    da_sb = es0.enter_context(nc.sbuf_tensor("da_sb", [128, NC_I * R], BF16))
    esW = ExitStack()
    qraw = [esW.enter_context(nc.sbuf_tensor(f"qraw{i}", [128, 5504], I32)) for i in range(2)]
    wcvt = [esW.enter_context(nc.sbuf_tensor(f"wcvt{i}", [128, 5504], BF16)) for i in range(2)]
    wstage = [esW.enter_context(nc.sbuf_tensor(f"wstage{i}", [128, 5504], BF16)) for i in range(2)]
    scol = esW.enter_context(nc.sbuf_tensor("scol", [128, 2 * NC_I + NC_D], F32))
    bcl = esW.enter_context(nc.sbuf_tensor("bcl", [128, 2 * NC_I + NC_D], F32))
    esSX = ExitStack()
    vb_rep = esSX.enter_context(nc.sbuf_tensor("vb_rep", [128, DM], BF16))

    with ExitStack() as esS:
        ident_f = esS.enter_context(nc.sbuf_tensor("ident_f", [128, 128], F32))
        bv_sb = esS.enter_context(nc.sbuf_tensor("bv_sb", [NH, 3 * HD], F32))
        sgn = esS.enter_context(nc.sbuf_tensor("sgn", [NH, 3 * HD], F32))
        tmp1 = esS.enter_context(nc.sbuf_tensor("tmp1", [NH, HD], F32))
        alpha = esS.enter_context(nc.sbuf_tensor("alpha", [NH, 4], F32))
        mnat = esS.enter_context(nc.sbuf_tensor("mnat", [NH, HD], F32))
        pad = esS.enter_context(nc.sbuf_tensor("pad", [128, 128], F32))
        bcol = esS.enter_context(nc.sbuf_tensor("bcol", [R, 5504], F32))
        bcol_b = esS.enter_context(nc.sbuf_tensor("bcol_b", [R, 5504], BF16))
        acol = esS.enter_context(nc.sbuf_tensor("acol", [128, NC_I * R // 2], F32))
        ones1 = esS.enter_context(nc.sbuf_tensor("ones1", [1, 128], F32))
        vbrow = esS.enter_context(nc.sbuf_tensor("vbrow", [1, DM], F32))
        pS1 = esS.enter_context(nc.psum_tensor("pS1", [128, 512], F32))
        pS2 = esS.enter_context(nc.psum_tensor("pS2", [128, 512], F32))

        v0 = ser(sSU, sy, sy.dma_start(pid_sb[:], nc.partition_id_tensor.ap()))
        vg = pGE.inc(ge.iota(dcmp[:, :], pattern=[[1, QT]], base=0, channel_multiplier=-1,
                             allow_small_or_imprecise_dtypes=True))
        wc.wait(ge, pGE, vg)
        vg = pGE.inc(ge.memset(ident_bf[:], 0.0))
        wc.wait(ge, pGE, vg)
        vg = pGE.inc(ge.affine_select(
            out=ident_bf[:], in_=ident_bf[:], compare_op=ALU.not_equal,
            fill=1.0, base=0, pattern=[[-1, 128]], channel_multiplier=1))
        vg2 = pGE.inc(ge.memset(ident_f[:], 0.0))
        wc.wait(ge, pGE, vg2)
        vg2 = pGE.inc(ge.affine_select(
            out=ident_f[:], in_=ident_f[:], compare_op=ALU.not_equal,
            fill=1.0, base=0, pattern=[[-1, 128]], channel_multiplier=1))
        vg3 = pGE.inc(ge.memset(ones1[:], 1.0))
        vg4 = pGE.inc(ge.memset(pad[:], 0.0))

        vq = ser(sSU, sy, sy.dma_start(bv_sb[:, 0:HD], bvq.ap()))
        vq = ser(sSU, sy, sy.dma_start(bv_sb[:, HD:2 * HD], bvk.ap()))
        vq = ser(sSU, sy, sy.dma_start(bv_sb[:, 2 * HD:3 * HD], bvv.ap()))
        wc.wait(se, sSU, vq)
        vs1 = pSE.inc(se.sign(sgn[:, :], bv_sb[:, :]))
        wc.wait(ve, sSU, vq)
        vv = vop(ve.reduce_sum(alpha[:, 0:1], bv_sb[:, 0:HD],
                                   axis=mybir.AxisListType.X, apply_absolute_value=True))
        vv = vop(ve.reduce_sum(alpha[:, 1:2], bv_sb[:, HD:2 * HD],
                                   axis=mybir.AxisListType.X, apply_absolute_value=True))
        vv = vop(ve.reduce_sum(alpha[:, 2:3], bv_sb[:, 2 * HD:3 * HD],
                                   axis=mybir.AxisListType.X, apply_absolute_value=True))
        wc.wait(ve, pSE, vs1)
        vv = vop(ve.tensor_tensor(tmp1[:, :], sgn[:, 0:HD], sgn[:, 0:HD], ALU.mult))
        vv = vop(ve.tensor_scalar(tmp1[:, :], tmp1[:, :], -1.0, 1.0, ALU.mult, ALU.add))
        vv = vop(ve.tensor_tensor(sgn[:, 0:HD], sgn[:, 0:HD], tmp1[:, :], ALU.add))
        vv = vop(ve.tensor_tensor(tmp1[:, :], sgn[:, HD:2 * HD], sgn[:, HD:2 * HD], ALU.mult))
        vv = vop(ve.tensor_scalar(tmp1[:, :], tmp1[:, :], -1.0, 1.0, ALU.mult, ALU.add))
        vv = vop(ve.tensor_tensor(sgn[:, HD:2 * HD], sgn[:, HD:2 * HD], tmp1[:, :], ALU.add))
        vv = vop(ve.tensor_tensor(tmp1[:, :], sgn[:, 2 * HD:3 * HD], sgn[:, 2 * HD:3 * HD], ALU.mult))
        vv = vop(ve.tensor_scalar(tmp1[:, :], tmp1[:, :], -1.0, 1.0, ALU.mult, ALU.add))
        vv = vop(ve.tensor_tensor(sgn[:, 2 * HD:3 * HD], sgn[:, 2 * HD:3 * HD], tmp1[:, :], ALU.add))
        vv = vop(ve.tensor_tensor(mnat[:, :], sgn[:, 0:HD], sgn[:, HD:2 * HD], ALU.mult))
        vv = vop(ve.tensor_tensor(alpha[:, 3:4], alpha[:, 0:1], alpha[:, 1:2], ALU.mult))
        vv = vop(ve.tensor_scalar(alpha[:, 3:4], alpha[:, 3:4], 1.0 / (HD * HD), None, ALU.mult))
        vv = vop(ve.tensor_scalar(mnat[:, :], mnat[:, :], alpha[:, 3:4], None, ALU.mult))
        vv = vop(ve.tensor_scalar(alpha[:, 2:3], alpha[:, 2:3], 1.0 / HD, None, ALU.mult))
        vv = vop(ve.tensor_scalar(sgn[:, 2 * HD:3 * HD], sgn[:, 2 * HD:3 * HD],
                                      alpha[:, 2:3], None, ALU.mult))
        wc.wait(ge, pVE, vv)
        vg5 = ser(sGU, ge, ge.dma_start(mvec.ap(), mnat[:, :]))
        vg5 = ser(sGU, ge, ge.dma_start(vbvec.ap(), sgn[:, 2 * HD:3 * HD]))
        wc.wait(sy, sGU, vg5)
        wc.wait(sy, pGE, vg4)
        v1 = ser(sSU, sy, sy.dma_start(pad[0:R, :], bass.AP(mvec, 0, [[128, R], [1, 128]])))
        wc.wait(te, sSU, v1)
        wc.wait(te, pGE, vg4)
        wc.wait(te, pGE, vg2)
        vt = pTE.inc(te.transpose(pS1[:, 0:128], pad[:, :], ident_f[:]))
        wc.wait(ve, pTE, vt)
        vv = pVE.inc(ve.tensor_copy(m_col[:, :], pS1[:, 0:R]))
        v2 = ser(sSU, sy, sy.dma_start(vbrow[0:1, :], bass.AP(vbvec, 0, [[DM, 1], [1, DM]])))
        wc.wait(te, sSU, v2)
        wc.wait(te, pGE, vg3)
        vt = pTE.inc(te.matmul(pS1[:, :], ones1[0:1, :], vbrow[0:1, 0:512], start=True, stop=True))
        vt = pTE.inc(te.matmul(pS2[:, :], ones1[0:1, :], vbrow[0:1, 512:1024], start=True, stop=True))
        wc.wait(ve, pTE, vt)
        vv = pVE.inc(ve.tensor_copy(vb_rep[:, 0:512], pS1[:, :]))
        vv = pVE.inc(ve.tensor_copy(vb_rep[:, 512:1024], pS2[:, :]))
        for half in range(1, 4):
            wc.wait(te, pVE, vv)
            vt = pTE.inc(te.matmul(pS1[:, :], ones1[0:1, :],
                                   vbrow[0:1, half * 1024:half * 1024 + 512], start=True, stop=True))
            vt = pTE.inc(te.matmul(pS2[:, :], ones1[0:1, :],
                                   vbrow[0:1, half * 1024 + 512:half * 1024 + 1024], start=True, stop=True))
            wc.wait(ve, pTE, vt)
            vv = pVE.inc(ve.tensor_copy(vb_rep[:, half * 1024:half * 1024 + 512], pS1[:, :]))
            vv = pVE.inc(ve.tensor_copy(vb_rep[:, half * 1024 + 512:half * 1024 + 1024], pS2[:, :]))

        for srcb, dst in ((gb, gb_bf), (ub, ub_bf), (db, db_bf)):
            n = srcb.shape[1]
            for c0 in range(0, n, 5504):
                cw = min(5504, n - c0)
                wc.wait(sy, pVE, vv)
                v3 = ser(sSU, sy, sy.dma_start(
                    bcol[0:R, 0:cw], bass.AP(srcb, c0, [[n, R], [1, cw]])))
                wc.wait(ve, sSU, v3)
                wc.wait(ve, sGU, sGU.n)
                vv = pVE.inc(ve.tensor_copy(bcol_b[0:R, 0:cw], bcol[0:R, 0:cw]))
                wc.wait(ge, pVE, vv)
                ser(sGU, ge, ge.dma_start(bass.AP(dst, c0, [[n, R], [1, cw]]), bcol_b[0:R, 0:cw]))
        for srca, dst, nch in ((ga, ag_sb, NC_D), (ua, ua_sb, NC_D), (da, da_sb, NC_I)):
            for h0 in range(0, nch, NC_I // 2):
                hw = min(NC_I // 2, nch - h0)
                wc.wait(sy, pVE, vv)
                v4 = ser(sSU, sy, sy.dma_start(
                    acol[:, 0:hw * R],
                    bass.AP(srca, h0 * 128 * R, [[R, 128], [128 * R, hw], [1, R]])))
                wc.wait(ve, sSU, v4)
                vv = pVE.inc(ve.tensor_copy(dst[:, h0 * R:(h0 + hw) * R], acol[:, 0:hw * R]))

        ji = esS.enter_context(nc.sbuf_tensor("ji", [1, 2], U32))
        jf = esS.enter_context(nc.sbuf_tensor("jf", [1, 2], F32))
        jcol = esS.enter_context(nc.sbuf_tensor("jcol", [128, 1], F32))
        wc.wait(ve, sSU, v0)
        vv = pVE.inc(ve.tensor_scalar(ji[0:1, 0:1], pid_sb[0:1, 0:1], 3, None, ALU.bitwise_and))
        wc.wait(ve, pVE, vv)
        vv = pVE.inc(ve.tensor_copy(jf[0:1, 0:1], ji[0:1, 0:1]))
        wc.wait(ve, pVE, vv)
        vv = pVE.inc(ve.tensor_scalar(jf[0:1, 0:1], jf[0:1, 0:1], float(QT), None, ALU.mult))
        wc.wait(te, pVE, vv)
        wc.wait(te, pGE, vg3)
        vt = pTE.inc(te.matmul(pS1[:, 0:1], ones1[0:1, :], jf[0:1, 0:1], start=True, stop=True))
        wc.wait(ve, pTE, vt)
        vv = pVE.inc(ve.tensor_copy(jcol[:, 0:1], pS1[:, 0:1]))
        wc.wait(ge, pVE, vv)
        vg = pGE.inc(ge.tensor_scalar(dcmp[:, :], dcmp[:, :], jcol[:, 0:1], None, ALU.add))
        vv_sc = 0
        spad_s = pad
        for (scsrc, nrows, off) in ((gsc, NC_I, 0), (usc, NC_I, NC_I), (dsc, NC_D, 2 * NC_I)):
            wc.wait(sy, pGE, vg4)
            wc.wait(sy, pTE, pTE.n)
            v1 = ser(sSU, sy, sy.dma_start(spad_s[0:nrows, :], scsrc.ap()))
            wc.wait(te, sSU, v1)
            wc.wait(te, pVE, pVE.n)
            vt = pTE.inc(te.transpose(pS1[:, 0:128], spad_s[:, :], ident_f[:]))
            wc.wait(ve, pTE, vt)
            vv = vop(ve.tensor_copy(scol[:, off:off + nrows], pS1[:, 0:nrows]))
        vv = vop(ve.tensor_scalar(scol[:, :], scol[:, :], 0.005, None, ALU.mult))
        vv_sc = vop(ve.tensor_scalar(bcl[:, :], scol[:, :], -7.5, None, ALU.mult))
        wc.wait(sy, sSU, v0)
        rj_sy = sy.alloc_register("rj_sy")
        sy.reg_load(rj_sy, pid_sb[0:1, 0:1])
        sy.reg_alu(rj_sy, rj_sy, 3, ALU.bitwise_and)
        sy.reg_alu(rj_sy, rj_sy, QT, ALU.mult)
        rg_sy = sy.alloc_register("rg_sy")
        c1024 = sy.alloc_register("c1024")
        sy.reg_mov(c1024, 1024)
        barrier()

    esWX = ExitStack()
    pXA = esWX.enter_context(nc.psum_tensor("pXA", [128, 2048], F32))
    pSC = [esWX.enter_context(nc.psum_tensor(f"pSC{i}", [128, 512], F32)) for i in range(2)]
    pT = [esWX.enter_context(nc.psum_tensor(f"pT{i}", [128, 1024], BF16)) for i in range(2)]

    pts = {"n": 0, "tr": {}, "ev": {}}

    def pt_fill(src_ap_fn, nch, evac_fn):
        par = pts["n"] % 2
        wc.wait(te, pVE, pts["ev"].get(pts["n"] - 2, 0))
        inst = None
        for ci in range(nch):
            inst = te.transpose(pT[par][:, ci * 128:(ci + 1) * 128], src_ap_fn(ci), ident_bf[:])
        pts["tr"][pts["n"]] = pTE.inc(inst)
        wc.wait(ve, pTE, pts["tr"][pts["n"]])
        pts["ev"][pts["n"]] = pVE.inc(evac_fn(pT[par][:, 0:nch * 128]))
        pts["n"] += 1
        return pts["tr"][pts["n"] - 1], pts["ev"][pts["n"] - 1]

    W_TILES = []
    for oc in range(NC_I):
        W_TILES.append((gq, oc * 128 * DM, DM, oc, wgT, oc * 128 * DM))
    for oc in range(NC_I):
        W_TILES.append((uq, oc * 128 * DM, DM, NC_I + oc, wuT, oc * 128 * DM))
    for oc in range(NC_D):
        for half in range(2):
            W_TILES.append((dq, oc * 128 * II + half * 5504, 5504, 2 * NC_I + oc,
                            wdT, oc * 128 * II + half * 5504))
    wst = {"n": 0, "ld": {}, "cv": {}, "tr": {}, "ev": {}, "st": {}}

    def w_step():
        n = wst["n"]
        NW = len(W_TILES)
        if n >= NW + 2:
            return
        if n < NW:
            srct, soff, cols, sci, dstt, doff = W_TILES[n]
            if n >= 2:
                wc.wait(ge, pSE, wst["cv"][n - 2])
            wst["ld"][n] = wLD[n % 2].inc(ge.dma_start(
                qraw[n % 2][:, 0:cols],
                bass.AP(srct, soff, [[cols if srct is not dq else II, 128], [1, cols]])), 16)
        m = n - 1
        if 0 <= m < NW:
            srct, soff, cols, sci, dstt, doff = W_TILES[m]
            wc.wait(se, wLD[m % 2], wst["ld"][m])
            wc.wait(se, pVE, vv_sc)
            if m >= 2:
                wc.wait(se, pTE, wst["tr"][m - 2])
            wst["cv"][m] = pSE.inc(se.activation(
                wcvt[m % 2][:, 0:cols], qraw[m % 2][:, 0:cols], AF.Identity,
                bias=bcl[:, sci:sci + 1], scale=scol[:, sci:sci + 1]))
        t = n - 2
        if 0 <= t < NW:
            srct, soff, cols, sci, dstt, doff = W_TILES[t]
            nch = cols // 128
            wc.wait(te, pSE, wst["cv"][t])
            ev_last = 0
            for g0 in range(0, nch, 8):
                gn = min(8, nch - g0)
                par = t % 2
                def mksrc(ci, _g0=g0, _par=par):
                    return wcvt[_par][:, (_g0 + ci) * 128:(_g0 + ci + 1) * 128]
                def mkev(ps_ap, _g0=g0, _gn=gn, _par=par, _t=t):
                    if _g0 == 0 and _t >= 2:
                        wc.wait(ve, wST[_t % 2], wst["st"][_t - 2])
                    return ve.tensor_copy(wstage[_par][:, _g0 * 128:(_g0 + _gn) * 128], ps_ap)
                trv, ev_last = pt_fill(mksrc, gn, mkev)
            wst["tr"][t] = pts["tr"][pts["n"] - 1]
            wst["ev"][t] = ev_last
            wc.wait(ge, pVE, ev_last)
            wst["st"][t] = wST[t % 2].inc(ge.dma_start(
                bass.AP(dstt, doff, [[DM if dstt is not wdT else II, 128], [1, cols]]),
                wstage[t % 2][:, 0:cols]), 16)
        wst["n"] += 1

    def w_drain():
        while wst["n"] < len(W_TILES) + 2:
            w_step()

    with ExitStack() as esX:
        xt = [esX.enter_context(nc.sbuf_tensor(f"xt{i}", [128, DM], F32)) for i in range(2)]
        xnt = [esX.enter_context(nc.sbuf_tensor(f"xnt{i}", [128, DM], BF16)) for i in range(2)]
        vt_ = [esX.enter_context(nc.sbuf_tensor(f"vt{i}", [128, DM], BF16)) for i in range(2)]
        fmst = [esX.enter_context(nc.sbuf_tensor(f"fmst{i}", [128, NC_D * 256], BF16)) for i in range(2)]
        stat = esX.enter_context(nc.sbuf_tensor("stat", [128, 16], F32))
        eps_sb = esX.enter_context(nc.sbuf_tensor("eps_sb", [128, 1], F32))
        v_eps = pGE.inc(ge.memset(eps_sb[:], 1e-5))

        NT = SEQ // 128
        ld_x = [0] * NT
        sq_x = [0] * NT
        st_stat = [0] * NT
        rs_x = [0] * NT
        xn_x = [0] * NT
        vv_x = [0] * NT
        tr_x = [0] * NT
        evq_x = [0] * NT
        stv_x = [0] * NT
        stf_x = {}

        def x_front(i):
            par = i % 2
            if i >= 2:
                wc.wait(sy, pSE, sq_x[i - 2])
                wc.wait(sy, pGE, xn_x[i - 2])
            ld_x[i] = xLD[par].inc(sy.dma_start(
                xt[par][:], bass.AP(x_seq, i * 128 * DM, [[DM, 128], [1, DM]])), 16)
            wc.wait(se, xLD[par], ld_x[i])
            if i >= 2:
                wc.wait(se, pVE, st_stat[i - 2])
            se.activation(pXA[:, :], xt[par][:, 0:2048], AF.Square,
                          accum_out=stat[:, 0 + par:1 + par])
            sq_x[i] = pSE.inc(se.activation(pXA[:, :], xt[par][:, 2048:4096], AF.Square,
                                            accum_out=stat[:, 12 + par:13 + par]))
            wc.wait(ve, xLD[par], ld_x[i])
            if i >= 2:
                wc.wait(ve, pGE, xn_x[i - 2])
            vop(ve.reduce_sum(stat[:, 2 + par:3 + par], xt[par][:],
                              axis=mybir.AxisListType.X, negate=True))
            wc.wait(ve, pSE, sq_x[i])
            vop(ve.tensor_scalar(stat[:, 2 + par:3 + par], stat[:, 2 + par:3 + par],
                                 1.0 / DM, None, ALU.mult))
            vop(ve.tensor_tensor(stat[:, 6 + par:7 + par], stat[:, 2 + par:3 + par],
                                 stat[:, 2 + par:3 + par], ALU.mult))
            vop(ve.tensor_tensor(stat[:, 4 + par:5 + par], stat[:, 0 + par:1 + par],
                                 stat[:, 12 + par:13 + par], ALU.add))
            vop(ve.tensor_scalar(stat[:, 4 + par:5 + par], stat[:, 4 + par:5 + par],
                                 1.0 / DM, None, ALU.mult))
            st_stat[i] = vop(ve.tensor_tensor(stat[:, 4 + par:5 + par], stat[:, 4 + par:5 + par],
                                              stat[:, 6 + par:7 + par], ALU.subtract))

        def x_back(i):
            par = i % 2
            wc.wait(se, pVE, st_stat[i])
            if i >= 2:
                wc.wait(se, pGE, xn_x[i - 2])
            wc.wait(se, pGE, v_eps)
            vse = pSE.inc(se.activation(stat[:, 10 + par:11 + par], stat[:, 4 + par:5 + par],
                                        AF.Ln, bias=eps_sb[:, 0:1]))
            wc.wait(se, pSE, vse)
            rs_x[i] = pSE.inc(se.activation(stat[:, 8 + par:9 + par], stat[:, 10 + par:11 + par],
                                            AF.Exp, scale=-0.5))
            wc.wait(ge, pSE, rs_x[i])
            if i >= 2:
                wc.wait(ge, pTE, tr_x[i - 2])
                wc.wait(ge, xVS[par], stv_x[i - 2])
            xn_x[i] = pGE.inc(ge.tensor_scalar(xnt[par][:], xt[par][:],
                                               stat[:, 2 + par:3 + par], stat[:, 8 + par:9 + par],
                                               ALU.add, ALU.mult))
            wc.wait(ge, pGE, xn_x[i])
            vv_x[i] = pGE.inc(ge.tensor_tensor(vt_[par][:], xnt[par][:], vb_rep[:], ALU.mult))
            wc.wait(te, pGE, xn_x[i])
            gpar = (i // 2) % 2
            ev_last = 0
            for f in range(4):
                def mksrc(ci, _f=f, _par=par):
                    return xnt[_par][:, (_f * 8 + ci) * 128:(_f * 8 + ci + 1) * 128]
                def mkev(ps_ap, _f=f, _par=par, _i=i, _gpar=gpar):
                    if _f == 0 and _i % 2 == 0 and _i >= 4:
                        wc.wait(ve, xFS[_gpar], stf_x[_i - 4])
                    return ve.tensor_copy(
                        bass.AP(fmst[_gpar], (_i % 2) * 128 + _f * 8 * 256,
                                [[NC_D * 256, 128], [256, 8], [1, 128]]), ps_ap)
                trv, ev_last = pt_fill(mksrc, 8, mkev)
            tr_x[i] = pts["tr"][pts["n"] - 1]
            evq_x[i] = ev_last
            wc.wait(ge, pGE, vv_x[i])
            stv_x[i] = xVS[par].inc(ge.dma_start(
                bass.AP(vs, i * 128 * DM, [[DM, 128], [1, DM]]), vt_[par][:]), 16)
            if i % 2 == 1:
                g2 = i // 2
                wc.wait(ge, pVE, evq_x[i])
                wc.wait(ge, pVE, evq_x[i - 1])
                stf_x[g2 * 2] = xFS[g2 % 2].inc(ge.dma_start(
                    bass.AP(xnfm, g2 * 256, [[SEQ, 128], [128 * SEQ, NC_D], [1, 256]]),
                    fmst[g2 % 2][:]), 16)

        for i in range(NT + 1):
            if i < NT:
                x_front(i)
            if i >= 1:
                x_back(i - 1)
            if i % 3 == 0:
                w_step()
        barrier()

    esSX.close()

    with ExitStack() as esA:
        q_sb = esA.enter_context(nc.sbuf_tensor("q_sb", [128, NC_D * QT], BF16))
        kt_b = [esA.enter_context(nc.sbuf_tensor(f"kt{i}", [128, 8 * 512], BF16)) for i in range(2)]
        v_b = [esA.enter_context(nc.sbuf_tensor(f"v_b{i}", [128, 8 * 512], BF16)) for i in range(2)]
        at_all = esA.enter_context(nc.sbuf_tensor("at_all", [128, 64 * QT], BF16))
        st_st = [esA.enter_context(nc.sbuf_tensor(f"st_st{i}", [128, 8 * QT], BF16)) for i in range(2)]

        GSZ = 4
        kt_free = [0, 0]
        v_free = [0, 0]
        ev_done = pVE.n
        st_done = {}
        hix = 0
        for g in range(NG):
            trips = 8 * (g + 1)
            if g == 0:
                sy.reg_alu(rg_sy, rj_sy, 0, ALU.add)
            else:
                sy.reg_alu(rg_sy, rg_sy, c1024, ALU.add)
            wc.wait(sy, pTE, pTE.n)
            vq = aQ.inc(sy.dma_start(
                q_sb[:],
                bass.AP(xnfm, rg_sy, [[SEQ, 128], [128 * SEQ, NC_D], [1, QT]])), 16)
            wc.wait(se, aQ, vq)
            for c in range(NC_D):
                inst = se.activation(q_sb[:, c * QT:(c + 1) * QT], q_sb[:, c * QT:(c + 1) * QT],
                                     AF.Identity, scale=m_col[:, c:c + 1])
            qm_done = pSE.inc(inst)

            for h in range(NH):
                v_ld = {}

                def v_load(grp, _h=h):
                    wc.wait(se, pTE, v_free[grp % 2])
                    v_ld[grp] = aV[grp % 2].inc(se.dma_start(
                        v_b[grp % 2][:],
                        bass.AP(vs, grp * GSZ * 128 * DM + _h * 1024,
                                [[DM, 128], [128 * DM, GSZ], [1, 1024]])), 16)
                v_load(0)
                v_load(1)
                sg_v = {}
                mk_v = {}
                kt_ld = {}
                for k in range(trips):
                    grp, st_ = k // GSZ, k % GSZ
                    if st_ == 0:
                        wc.wait(sy, pTE, kt_free[grp % 2])
                        kt_ld[grp] = aKT[grp % 2].inc(sy.dma_start(
                            kt_b[grp % 2][:],
                            bass.AP(xnfm, 8 * h * 128 * SEQ + grp * GSZ * 128,
                                    [[SEQ, 128], [128 * SEQ, 8], [1, GSZ * 128]])), 16)
                    pk = pSC[k % 2]
                    wc.wait(te, aKT[grp % 2], kt_ld[grp])
                    wc.wait(te, pSE, qm_done)
                    if k >= 2:
                        wc.wait(te, pSE, sg_v[k - 2])
                    else:
                        wc.wait(te, pVE, ev_done)
                    for dc in range(8):
                        inst = te.matmul(pk[:, 0:QT],
                                         kt_b[grp % 2][:, dc * 512 + st_ * 128: dc * 512 + (st_ + 1) * 128],
                                         q_sb[:, (8 * h + dc) * QT:(8 * h + dc + 1) * QT],
                                         start=(dc == 0), stop=(dc == 7))
                    sc_v = pTE.inc(inst)
                    kt_free[grp % 2] = sc_v
                    wc.wait(se, pTE, sc_v)
                    sg_v[k] = pSE.inc(se.activation(at_all[:, k * QT:(k + 1) * QT], pk[:, 0:QT],
                                                    AF.Sigmoid, scale=0.125))
                    wc.wait(ve, pSE, sg_v[k])
                    mk_v[k] = pVE.inc(ve.scalar_tensor_tensor(
                        at_all[:, k * QT:(k + 1) * QT], dcmp[:, :],
                        float(128 * k - 1024 * g),
                        at_all[:, k * QT:(k + 1) * QT],
                        ALU.is_ge, ALU.mult))
                    if st_ == GSZ - 1 and (k // GSZ) % 2 == 1:
                        w_step()
                b_last = 0
                for k in range(trips):
                    grp, st_ = k // GSZ, k % GSZ
                    if st_ == 0 and grp >= 2:
                        v_load(grp)
                    wc.wait(te, aV[grp % 2], v_ld[grp])
                    if k == 0:
                        wc.wait(te, pVE, mk_v[trips - 1])
                        wc.wait(te, pVE, ev_done)
                    for dc in range(8):
                        inst = te.matmul(
                            pXA[:, (dc % 4) * 512 + (dc // 4) * QT: (dc % 4) * 512 + (dc // 4) * QT + QT],
                            v_b[grp % 2][:, st_ * 1024 + dc * 128: st_ * 1024 + (dc + 1) * 128],
                            at_all[:, k * QT:(k + 1) * QT],
                            start=(k == 0 and dc < 4), stop=(k == trips - 1))
                    b_last = pTE.inc(inst)
                    v_free[grp % 2] = b_last
                    if st_ == GSZ - 1 and (k // GSZ) % 2 == 0:
                        w_step()
                hpar = hix % 2
                wc.wait(ve, pTE, b_last)
                if hix >= 2:
                    wc.wait(ve, aST[hpar], st_done[hix - 2])
                for dc in range(8):
                    inst = ve.tensor_copy(
                        st_st[hpar][:, dc * QT:(dc + 1) * QT],
                        pXA[:, (dc % 4) * 512 + (dc // 4) * QT: (dc % 4) * 512 + (dc // 4) * QT + QT])
                ev_done = pVE.inc(inst)
                wc.wait(ge, pVE, ev_done)
                st_done[hix] = aST[hpar].inc(ge.dma_start(
                    bass.AP(ss, 8 * h * 128 * TLOC + g * QT,
                            [[TLOC, 128], [128 * TLOC, 8], [1, QT]]),
                    st_st[hpar][:]), 16)
                hix += 1
        w_drain()
        barrier()
    esWX.close()
    esW.close()

    esP = ExitStack()
    pb = [esP.enter_context(nc.psum_tensor(f"pb{i}", [128, 512], F32)) for i in range(8)]

    with ExitStack() as esG:
        xn_res = esG.enter_context(nc.sbuf_tensor("xn_res", [128, NC_D * TLOC], BF16))
        wg_b = [esG.enter_context(nc.sbuf_tensor(f"wg_b{i}", [128, DM], BF16)) for i in range(2)]
        wu_b = [esG.enter_context(nc.sbuf_tensor(f"wu_b{i}", [128, DM], BF16)) for i in range(2)]
        gb_sl = [esG.enter_context(nc.sbuf_tensor(f"gb_sl{i}", [R, 128], BF16)) for i in range(2)]
        ub_sl = [esG.enter_context(nc.sbuf_tensor(f"ub_sl{i}", [R, 128], BF16)) for i in range(2)]
        zg_sb = esG.enter_context(nc.sbuf_tensor("zg_sb", [R, TLOC], BF16))
        zu_sb = esG.enter_context(nc.sbuf_tensor("zu_sb", [R, TLOC], BF16))
        hg = [esG.enter_context(nc.sbuf_tensor(f"hg{i}", [128, 512], BF16)) for i in range(2)]
        hu = [esG.enter_context(nc.sbuf_tensor(f"hu{i}", [128, 512], BF16)) for i in range(2)]
        hst = [esG.enter_context(nc.sbuf_tensor(f"hst{i}", [128, TLOC], BF16)) for i in range(2)]

        xn_ld = 0
        for g in range(NG):
            if g == 0:
                sy.reg_alu(rg_sy, rj_sy, 0, ALU.add)
            else:
                sy.reg_alu(rg_sy, rg_sy, c1024, ALU.add)
            xn_ld = gXN.inc(sy.dma_start(
                bass.AP(xn_res, g * QT, [[NC_D * TLOC, 128], [TLOC, NC_D], [1, QT]]),
                bass.AP(xnfm, rg_sy, [[SEQ, 128], [128 * SEQ, NC_D], [1, QT]])), 16)
        z_ev = 0
        for (asb, zsb) in ((ag_sb, zg_sb), (ua_sb, zu_sb)):
            for sub in range(4):
                pz = pb[6 + sub % 2]
                wc.wait(te, gXN, xn_ld)
                wc.wait(te, pSE, z_ev)
                for c in range(NC_D):
                    inst = te.matmul(pz[0:R, :], asb[:, c * R:(c + 1) * R],
                                     xn_res[:, c * TLOC + sub * 512: c * TLOC + (sub + 1) * 512],
                                     start=(c == 0), stop=(c == NC_D - 1))
                vz = pTE.inc(inst)
                wc.wait(se, pTE, vz)
                z_ev = pSE.inc(se.activation(zsb[0:R, sub * 512:(sub + 1) * 512], pz[0:R, :],
                                             AF.Identity, scale=0.02))

        ld_w = [0] * NC_I
        mm_oc = [0] * NC_I
        sl_v = {}
        ue_v = {}
        hm_v = {}
        hs_v = {}
        for oc in range(NC_I):
            par = oc % 2
            if oc >= 2:
                wc.wait(sy, pTE, mm_oc[oc - 2])
            vg1 = gWG[par].inc(sy.dma_start(wg_b[par][:], bass.AP(wgT, oc * 128 * DM, [[DM, 128], [1, DM]])), 16)
            vu1 = gWU[par].inc(sy.dma_start(wu_b[par][:], bass.AP(wuT, oc * 128 * DM, [[DM, 128], [1, DM]])), 16)
            vb1 = gGB[par].inc(sy.dma_start(gb_sl[par][0:R, :], bass.AP(gb_bf, oc * 128, [[II, R], [1, 128]])), 16)
            vu2 = gUB[par].inc(sy.dma_start(ub_sl[par][0:R, :], bass.AP(ub_bf, oc * 128, [[II, R], [1, 128]])), 16)
            ld_w[oc] = (par, vg1, vu1, vb1, vu2)
            last = 0
            for pr in range(2):
                subs = (2 * pr, 2 * pr + 1)
                pg_ = [pb[4 * pr + 0], pb[4 * pr + 1]]
                pu_ = [pb[4 * pr + 2], pb[4 * pr + 3]]
                _p, _v1, _v2, _v3, _v4 = ld_w[oc]
                wc.wait(te, gWG[_p], _v1)
                wc.wait(te, gWU[_p], _v2)
                wc.wait(te, gGB[_p], _v3)
                wc.wait(te, gUB[_p], _v4)
                wc.wait(te, pSE, z_ev)
                if oc >= 1:
                    for s2 in subs:
                        wc.wait(te, pSE, sl_v[(oc - 1, s2)])
                        wc.wait(te, pVE, ue_v[(oc - 1, s2)])
                for c in range(NC_D):
                    for i2, s2 in enumerate(subs):
                        te.matmul(pg_[i2][:, :], wg_b[par][:, c * 128:(c + 1) * 128],
                                  xn_res[:, c * TLOC + s2 * 512: c * TLOC + (s2 + 1) * 512],
                                  start=(c == 0), stop=False)
                for i2, s2 in enumerate(subs):
                    te.matmul(pg_[i2][:, :], gb_sl[par][0:R, :], zg_sb[0:R, s2 * 512:(s2 + 1) * 512],
                              start=False, stop=True)
                for c in range(NC_D):
                    for i2, s2 in enumerate(subs):
                        te.matmul(pu_[i2][:, :], wu_b[par][:, c * 128:(c + 1) * 128],
                                  xn_res[:, c * TLOC + s2 * 512: c * TLOC + (s2 + 1) * 512],
                                  start=(c == 0), stop=False)
                for i2, s2 in enumerate(subs):
                    inst = te.matmul(pu_[i2][:, :], ub_sl[par][0:R, :], zu_sb[0:R, s2 * 512:(s2 + 1) * 512],
                                     start=False, stop=True)
                last = pTE.inc(inst)
                for i2, s2 in enumerate(subs):
                    wc.wait(se, pTE, last)
                    if oc >= 1 or pr >= 1:
                        wc.wait(se, pGE, hm_v.get((oc, s2 - 2), hm_v.get((oc - 1, s2 + 2), 0)))
                    sl_v[(oc, s2)] = pSE.inc(se.activation(hg[i2][:], pg_[i2][:, :], AF.Silu))
                    wc.wait(ve, pTE, last)
                    if oc >= 1 or pr >= 1:
                        wc.wait(ve, pGE, hm_v.get((oc, s2 - 2), hm_v.get((oc - 1, s2 + 2), 0)))
                    ue_v[(oc, s2)] = pVE.inc(ve.tensor_copy(hu[i2][:], pu_[i2][:, :]))
                    wc.wait(ge, pSE, sl_v[(oc, s2)])
                    wc.wait(ge, pVE, ue_v[(oc, s2)])
                    if oc >= 2 and s2 == 0:
                        wc.wait(ge, gHS[par], hs_v[oc - 2])
                    hm_v[(oc, s2)] = pGE.inc(ge.tensor_tensor(
                        hst[par][:, s2 * 512:(s2 + 1) * 512], hg[i2][:], hu[i2][:], ALU.mult))
            mm_oc[oc] = last
            wc.wait(ge, pGE, hm_v[(oc, 3)])
            hs_v[oc] = gHS[par].inc(ge.dma_start(
                bass.AP(hs, oc * 128 * TLOC, [[TLOC, 128], [1, TLOC]]), hst[par][:]), 16)
        barrier()

    zd_dbg = nc.dram_tensor("d_zd", [2, R, 1024], BF16, kind="ExternalOutput") if dbg else None
    with ExitStack() as esD:
        h_res = esD.enter_context(nc.sbuf_tensor("h_res", [128, NC_I * 1024], BF16))
        wd_b = [esD.enter_context(nc.sbuf_tensor(f"wd_b{i}", [128, 29 * 128], BF16)) for i in range(2)]
        db_sl = [esD.enter_context(nc.sbuf_tensor(f"db_sl{i}", [R, 128], BF16)) for i in range(2)]
        zd_sb = esD.enter_context(nc.sbuf_tensor("zd_sb", [R, 1024], BF16))
        sd_sb = [esD.enter_context(nc.sbuf_tensor(f"sd_sb{i}", [128, 512], BF16)) for i in range(2)]
        s_sl = [esD.enter_context(nc.sbuf_tensor(f"s_sl{i}", [128, 512], BF16)) for i in range(2)]

        for half in range(2):
            h_ld = 0
            wc.wait(sy, pTE, pTE.n)
            for q4 in range(4):
                c0 = (NC_I * q4) // 4
                c1 = (NC_I * (q4 + 1)) // 4
                h_ld = dH.inc(sy.dma_start(
                    bass.AP(h_res, c0 * 1024, [[NC_I * 1024, 128], [1024, c1 - c0], [1, 1024]]),
                    bass.AP(hs, c0 * 128 * TLOC + half * 1024,
                            [[TLOC, 128], [128 * TLOC, c1 - c0], [1, 1024]])), 16)
            z_ev = 0
            for sub in range(2):
                pz = pb[6 + sub]
                wc.wait(te, dH, h_ld)
                wc.wait(te, pSE, z_ev)
                for c in range(NC_I):
                    inst = te.matmul(pz[0:R, :], da_sb[:, c * R:(c + 1) * R],
                                     h_res[:, c * 1024 + sub * 512: c * 1024 + (sub + 1) * 512],
                                     start=(c == 0), stop=(c == NC_I - 1))
                vz = pTE.inc(inst)
                wc.wait(se, pTE, vz)
                z_ev = pSE.inc(se.activation(zd_sb[0:R, sub * 512:(sub + 1) * 512], pz[0:R, :],
                                             AF.Identity, scale=0.02))
            if dbg:
                wc.wait(ge, pSE, z_ev)
                sDBG.inc(ge.dma_start(
                    bass.AP(zd_dbg, half * R * 1024, [[1024, R], [1, 1024]]),
                    zd_sb[0:R, :]), 16)

            ld_seg = {}
            seg_used = {}
            ld_db = {}
            ld_s = {}
            sil_v = {}
            pm_v = {}
            pst_v = {}
            SEGS = [(0, 29), (29, 58), (58, 86)]
            NSEG = 3
            def load_seg(oc, s_):
                pseg = (oc * NSEG + s_) % 2
                wc.wait(sy, pTE, seg_used.get(oc * NSEG + s_ - 2, 0))
                c0, c1 = SEGS[s_]
                ld_seg[oc * NSEG + s_] = dSEG[pseg].inc(sy.dma_start(
                    wd_b[pseg][:, 0:(c1 - c0) * 128],
                    bass.AP(wdT, oc * 128 * II + c0 * 128, [[II, 128], [1, (c1 - c0) * 128]])), 16)

            for oc in range(NC_D):
                load_seg(oc, 0)
                load_seg(oc, 1)
                wc.wait(sy, pTE, seg_used.get((oc - 2) * NSEG + NSEG - 1, 0) if oc >= 2 else 0)
                ld_db[oc] = dDB[oc % 2].inc(sy.dma_start(
                    db_sl[oc % 2][0:R, :], bass.AP(db_bf, oc * 128, [[DM, R], [1, 128]])), 16)
                for sub in range(2):
                    if oc >= 1:
                        wc.wait(sy, pGE, pm_v.get((oc - 1, sub), 0))
                    ld_s[(oc, sub)] = dSL[sub].inc(sy.dma_start(
                        s_sl[sub][:],
                        bass.AP(ss, oc * 128 * TLOC + half * 1024 + sub * 512,
                                [[TLOC, 128], [1, 512]])), 16)
                psd = [pb[(oc % 2) * 2], pb[(oc % 2) * 2 + 1]]
                lastmm = 0
                for s_ in range(NSEG):
                    pseg = (oc * NSEG + s_) % 2
                    c0, c1 = SEGS[s_]
                    wc.wait(te, dSEG[pseg], ld_seg[oc * NSEG + s_])
                    if s_ == NSEG - 1:
                        wc.wait(te, dDB[oc % 2], ld_db[oc])
                    wc.wait(te, pSE, z_ev)
                    inst = None
                    if s_ == 0 and oc >= 2:
                        for sub in range(2):
                            wc.wait(te, pSE, sil_v[(oc - 2, sub)])
                    for ci in range(c1 - c0):
                        c = c0 + ci
                        for sub in range(2):
                            inst = te.matmul(psd[sub][:, :], wd_b[pseg][:, ci * 128:(ci + 1) * 128],
                                             h_res[:, c * 1024 + sub * 512: c * 1024 + (sub + 1) * 512],
                                             start=(s_ == 0 and ci == 0), stop=False)
                    if s_ == NSEG - 1:
                        for sub in range(2):
                            inst = te.matmul(psd[sub][:, :], db_sl[oc % 2][0:R, :],
                                             zd_sb[0:R, sub * 512:(sub + 1) * 512],
                                             start=False, stop=True)
                        lastmm = pTE.inc(inst)
                    if s_ < NSEG - 1:
                        seg_used[oc * NSEG + s_] = pTE.inc(inst)
                        if s_ == 0:
                            load_seg(oc, 2)
                    else:
                        seg_used[oc * NSEG + s_] = lastmm
                for sub in range(2):
                    wc.wait(se, pTE, lastmm)
                    if oc >= 1:
                        wc.wait(se, dPS[sub], pst_v[(oc - 1, sub)])
                    sil_v[(oc, sub)] = pSE.inc(se.activation(sd_sb[sub][:], psd[sub][:, :], AF.Silu))
                    wc.wait(ge, pSE, sil_v[(oc, sub)])
                    wc.wait(ge, dSL[sub], ld_s[(oc, sub)])
                    pm_v[(oc, sub)] = pGE.inc(ge.tensor_tensor(sd_sb[sub][:], sd_sb[sub][:],
                                                               s_sl[sub][:], ALU.mult))
                    wc.wait(ge, pGE, pm_v[(oc, sub)])
                    pst_v[(oc, sub)] = dPS[sub].inc(ge.dma_start(
                        bass.AP(ps_, oc * 128 * TLOC + half * 1024 + sub * 512,
                                [[TLOC, 128], [1, 512]]),
                        sd_sb[sub][:]), 16)
        barrier()

    with ExitStack() as esE:
        ptm = [esE.enter_context(nc.sbuf_tensor(f"ptm{i}", [128, DM], BF16)) for i in range(2)]
        xep = [esE.enter_context(nc.sbuf_tensor(f"xep{i}", [128, DM], F32)) for i in range(2)]
        ot = [esE.enter_context(nc.sbuf_tensor(f"ot{i}", [128, DM], F32)) for i in range(2)]

        NE = TLOC // 128
        ld_p = [0] * NE
        ld_xe = [0] * NE
        ad_v = [0] * NE
        so_v = [0] * NE
        for i in range(NE):
            par = i % 2
            if i >= 2:
                wc.wait(sy, pVE, ad_v[i - 2])
            ld_p[i] = eP[par].inc(sy.dma_start(
                ptm[par][:], bass.AP(ps_, i * 128, [[TLOC, DM], [1, 128]]), transpose=True), 16)
            ld_xe[i] = eX[par].inc(sy.dma_start(
                xep[par][:], bass.AP(x_q, i * 128 * DM, [[DM, 128], [1, DM]])), 16)
            wc.wait(ve, eP[par], ld_p[i])
            wc.wait(ve, eX[par], ld_xe[i])
            if i >= 2:
                wc.wait(ve, eO[par], so_v[i - 2])
            ad_v[i] = pVE.inc(ve.tensor_tensor(ot[par][:], xep[par][:], ptm[par][:], ALU.add))
            wc.wait(ge, pVE, ad_v[i])
            so_v[i] = eO[par].inc(ge.dma_start(
                bass.AP(out_q, i * 128 * DM, [[DM, 128], [1, DM]]), ot[par][:]), 16)
        wc.wait(ge, eO[0], eO[0].n)
        wc.wait(ge, eO[1], eO[1].n)
    esP.close()
    if dbg:
        dbgmap = {
            "d_xnfm": (xnfm, [2, 128, SEQ]),
            "d_vs": (vs, [256, DM]),
            "d_wg": (wgT, [NC_I, 128, DM]),
            "d_wu": (wuT, [NC_I, 128, DM]),
            "d_wd": (wdT, [NC_D, 128, II]),
            "d_ss": (ss, [NC_D, 128, TLOC]),
            "d_hs": (hs, [NC_I, 128, TLOC]),
            "d_ps": (ps_, [DM, TLOC]),
        }
        vstart = pGE.n
        for name, (srct, shp) in dbgmap.items():
            dt_ = nc.dram_tensor(name, shp, BF16, kind="ExternalOutput")
            n_el = int(np.prod(shp))
            ser(sDBG, ge, ge.dma_start(bass.AP(dt_, 0, [[n_el, 1], [1, n_el]]),
                                       bass.AP(srct, 0, [[n_el, 1], [1, n_el]])))
        dt_ = nc.dram_tensor("d_dcmp", [128, QT], F32, kind="ExternalOutput")
        ser(sDBG, ge, ge.dma_start(dt_.ap(), dcmp[:, :]))
        dt_ = nc.dram_tensor("d_mcol", [128, R], F32, kind="ExternalOutput")
        ser(sDBG, ge, ge.dma_start(dt_.ap(), m_col[:, :]))
        wc.wait(ge, sDBG, sDBG.n)
    es0.close()
    return nc


_NC_CACHE = {}


def kernel(**inputs):
    x = np.asarray(inputs["L0_attn_x"])
    n_cores = 8
    in_maps = []
    rowsets = []
    common = {
        "gq": np.ascontiguousarray(inputs["L0_ffn_gate_q"], dtype=np.int32),
        "uq": np.ascontiguousarray(inputs["L0_ffn_up_q"], dtype=np.int32),
        "dq": np.ascontiguousarray(inputs["L0_ffn_down_q"], dtype=np.int32),
        "gsc": np.ascontiguousarray(np.asarray(inputs["L0_ffn_gate_scale"]).reshape(NC_I, 128)),
        "usc": np.ascontiguousarray(np.asarray(inputs["L0_ffn_up_scale"]).reshape(NC_I, 128)),
        "dsc": np.ascontiguousarray(np.asarray(inputs["L0_ffn_down_scale"]).reshape(NC_D, 128)),
        "ga": np.ascontiguousarray(inputs["L0_ffn_gate_A"], dtype=np.float32),
        "gb": np.ascontiguousarray(inputs["L0_ffn_gate_B"], dtype=np.float32),
        "ua": np.ascontiguousarray(inputs["L0_ffn_up_A"], dtype=np.float32),
        "ub": np.ascontiguousarray(inputs["L0_ffn_up_B"], dtype=np.float32),
        "da": np.ascontiguousarray(inputs["L0_ffn_down_A"], dtype=np.float32),
        "db": np.ascontiguousarray(inputs["L0_ffn_down_B"], dtype=np.float32),
        "bvq": np.ascontiguousarray(inputs["L0_attn_bv_q"], dtype=np.float32),
        "bvk": np.ascontiguousarray(inputs["L0_attn_bv_k"], dtype=np.float32),
        "bvv": np.ascontiguousarray(inputs["L0_attn_bv_v"], dtype=np.float32),
    }
    for c in range(n_cores):
        s, j = c // 4, c % 4
        rows = np.concatenate([np.arange(g * 1024 + j * QT, g * 1024 + (j + 1) * QT)
                               for g in range(NG)])
        rowsets.append((s, rows))
        m = dict(common)
        m["x_seq"] = np.ascontiguousarray(x[s], dtype=np.float32)
        m["x_q"] = np.ascontiguousarray(x[s][rows], dtype=np.float32)
        in_maps.append(m)

    if "nc" not in _NC_CACHE:
        _NC_CACHE["nc"] = emit(bass.Bass("TRN2", target_bir_lowering=False))
    nc = _NC_CACHE["nc"]
    res = run_bass_kernel_spmd(nc, in_maps, core_ids=list(range(n_cores)))
    out = np.empty((2, SEQ, DM), dtype=np.float32)
    for c in range(n_cores):
        s, rows = rowsets[c]
        out[s][rows] = res.results[c]["out_q"]
    return out


# version 12
# speedup vs baseline: 1.1348x; 1.0594x over previous
import numpy as np
import ml_dtypes
import concourse.bass as bass
import concourse.mybir as mybir
from concourse.bass_utils import run_bass_kernel_spmd

F32 = mybir.dt.float32
BF16 = mybir.dt.bfloat16
I32 = mybir.dt.int32
U32 = mybir.dt.uint32
AF = mybir.ActivationFunctionType
ALU = mybir.AluOpType

SEQ = 8192
DM = 4096
NH = 4
HD = 1024
II = 11008
NC_I = II // 128
NC_D = DM // 128
TLOC = 2048
NG = 8
QT = 256
R = 32


class P:
    def __init__(self, sem):
        self.sem = sem
        self.n = 0

    def inc(self, inst, k=1):
        inst.then_inc(self.sem, k)
        self.n += k
        return self.n


class W:
    def __init__(self):
        self.seen = {}

    def wait(self, eng, p, v):
        if v <= 0:
            return
        key = (id(eng), id(p))
        if self.seen.get(key, -1) >= v:
            return
        eng.wait_ge(p.sem, v)
        self.seen[key] = v


def emit(nc, dbg=()):
    x_seq = nc.dram_tensor("x_seq", [SEQ, DM], F32, kind="ExternalInput")
    x_q = nc.dram_tensor("x_q", [TLOC, DM], F32, kind="ExternalInput")
    gq = nc.dram_tensor("gq", [II, DM], I32, kind="ExternalInput")
    uq = nc.dram_tensor("uq", [II, DM], I32, kind="ExternalInput")
    dq = nc.dram_tensor("dq", [DM, II], I32, kind="ExternalInput")
    gsc = nc.dram_tensor("gsc", [NC_I, 128], F32, kind="ExternalInput")
    usc = nc.dram_tensor("usc", [NC_I, 128], F32, kind="ExternalInput")
    dsc = nc.dram_tensor("dsc", [NC_D, 128], F32, kind="ExternalInput")
    ga = nc.dram_tensor("ga", [DM, R], F32, kind="ExternalInput")
    gb = nc.dram_tensor("gb", [R, II], F32, kind="ExternalInput")
    ua = nc.dram_tensor("ua", [DM, R], F32, kind="ExternalInput")
    ub = nc.dram_tensor("ub", [R, II], F32, kind="ExternalInput")
    da = nc.dram_tensor("da", [II, R], F32, kind="ExternalInput")
    db = nc.dram_tensor("db", [R, DM], F32, kind="ExternalInput")
    bvq = nc.dram_tensor("bvq", [NH, HD], F32, kind="ExternalInput")
    bvk = nc.dram_tensor("bvk", [NH, HD], F32, kind="ExternalInput")
    bvv = nc.dram_tensor("bvv", [NH, HD], F32, kind="ExternalInput")
    out_q = nc.dram_tensor("out_q", [TLOC, DM], F32, kind="ExternalOutput")

    wgT = nc.dram_tensor("wgT", [NC_I, 128, DM], BF16, kind="Internal")
    wuT = nc.dram_tensor("wuT", [NC_I, 128, DM], BF16, kind="Internal")
    wdT = nc.dram_tensor("wdT", [NC_D, 128, II], BF16, kind="Internal")
    xnfm = nc.dram_tensor("xnfm", [NC_D, 128, SEQ], BF16, kind="Internal")
    vs = nc.dram_tensor("vs", [SEQ, DM], BF16, kind="Internal")
    hs = nc.dram_tensor("hs", [NC_I, 128, TLOC], BF16, kind="Internal")
    ss = nc.dram_tensor("ss", [NC_D, 128, TLOC], BF16, kind="Internal")
    ps_ = nc.dram_tensor("ps", [DM, TLOC], BF16, kind="Internal")
    mvec = nc.dram_tensor("mvec", [NH, HD], F32, kind="Internal")
    vbvec = nc.dram_tensor("vbvec", [NH, HD], F32, kind="Internal")
    gb_bf = nc.dram_tensor("gb_bf", [R, II], BF16, kind="Internal")
    ub_bf = nc.dram_tensor("ub_bf", [R, II], BF16, kind="Internal")
    db_bf = nc.dram_tensor("db_bf", [R, DM], BF16, kind="Internal")

    sy, te, ve, se, ge = nc.sync, nc.tensor, nc.vector, nc.scalar, nc.gpsimd
    engines = [sy, te, ve, se, ge]

    from contextlib import ExitStack
    es0 = ExitStack()

    all_sems = []

    def newP(name):
        p = P(es0.enter_context(nc.semaphore(name)))
        all_sems.append(p)
        return p

    pSY = newP("pSY")
    pTE = newP("pTE")
    pVE = newP("pVE")
    pSE = newP("pSE")
    pGE = newP("pGE")
    pGD = newP("pGD")
    sSU = newP("sSU")
    sGU = newP("sGU")
    wLD = [newP(f"wLD{i}") for i in range(2)]
    wST = [newP(f"wST{i}") for i in range(2)]
    xLD = [newP(f"xLD{i}") for i in range(2)]
    xVS = [newP(f"xVS{i}") for i in range(2)]
    xFS = [newP(f"xFS{i}") for i in range(2)]
    aQ = newP("aQ")
    aKT = [newP(f"aKT{i}") for i in range(2)]
    aV = [newP(f"aV{i}") for i in range(2)]
    aST = [newP(f"aST{i}") for i in range(2)]
    gXN = newP("gXN")
    gWG = [newP(f"gWG{i}") for i in range(2)]
    gWU = [newP(f"gWU{i}") for i in range(2)]
    gGB = [newP(f"gGB{i}") for i in range(2)]
    gUB = [newP(f"gUB{i}") for i in range(2)]
    gHS = [newP(f"gHS{i}") for i in range(2)]
    dH = newP("dH")
    dSEG = [newP(f"dSEG{i}") for i in range(2)]
    dDB = [newP(f"dDB{i}") for i in range(2)]
    dSL = [newP(f"dSL{i}") for i in range(2)]
    dPS = [newP(f"dPS{i}") for i in range(2)]
    eP = [newP(f"eP{i}") for i in range(2)]
    eX = [newP(f"eX{i}") for i in range(2)]
    eO = [newP(f"eO{i}") for i in range(2)]
    sDBG = newP("sDBG")
    wc = W()

    def ser(p, eng, inst):
        wc.wait(eng, p, p.n)
        return p.inc(inst, 16)

    def vop(inst):
        v = pVE.inc(inst)
        wc.wait(ve, pVE, v)
        return v

    def barrier():
        for eng in engines:
            for p in all_sems:
                wc.wait(eng, p, p.n)

    ident_bf = es0.enter_context(nc.sbuf_tensor("ident_bf", [128, 128], BF16))
    dcmp = es0.enter_context(nc.sbuf_tensor("dcmp", [128, QT], F32))
    m_col = es0.enter_context(nc.sbuf_tensor("m_col", [128, R], F32))
    pid_sb = es0.enter_context(nc.sbuf_tensor("pid_sb", [1, 1], U32))
    ag_sb = es0.enter_context(nc.sbuf_tensor("ag_sb", [128, NC_D * R], BF16))
    ua_sb = es0.enter_context(nc.sbuf_tensor("ua_sb", [128, NC_D * R], BF16))
    da_sb = es0.enter_context(nc.sbuf_tensor("da_sb", [128, NC_I * R], BF16))
    esW = ExitStack()
    qraw = [esW.enter_context(nc.sbuf_tensor(f"qraw{i}", [128, 5504], I32)) for i in range(2)]
    wcvt = [esW.enter_context(nc.sbuf_tensor(f"wcvt{i}", [128, 5504], BF16)) for i in range(2)]
    wstage = [esW.enter_context(nc.sbuf_tensor(f"wstage{i}", [128, 5504], BF16)) for i in range(2)]
    scol = esW.enter_context(nc.sbuf_tensor("scol", [128, 2 * NC_I + NC_D], F32))
    bcl = esW.enter_context(nc.sbuf_tensor("bcl", [128, 2 * NC_I + NC_D], F32))
    esSX = ExitStack()
    vb_rep = esSX.enter_context(nc.sbuf_tensor("vb_rep", [128, DM], BF16))

    with ExitStack() as esS:
        ident_f = esS.enter_context(nc.sbuf_tensor("ident_f", [128, 128], F32))
        bv_sb = esS.enter_context(nc.sbuf_tensor("bv_sb", [NH, 3 * HD], F32))
        sgn = esS.enter_context(nc.sbuf_tensor("sgn", [NH, 3 * HD], F32))
        tmp1 = esS.enter_context(nc.sbuf_tensor("tmp1", [NH, HD], F32))
        alpha = esS.enter_context(nc.sbuf_tensor("alpha", [NH, 4], F32))
        mnat = esS.enter_context(nc.sbuf_tensor("mnat", [NH, HD], F32))
        pad = esS.enter_context(nc.sbuf_tensor("pad", [128, 128], F32))
        bcol = esS.enter_context(nc.sbuf_tensor("bcol", [R, 5504], F32))
        bcol_b = esS.enter_context(nc.sbuf_tensor("bcol_b", [R, 5504], BF16))
        acol = esS.enter_context(nc.sbuf_tensor("acol", [128, NC_I * R // 2], F32))
        ones1 = esS.enter_context(nc.sbuf_tensor("ones1", [1, 128], F32))
        vbrow = esS.enter_context(nc.sbuf_tensor("vbrow", [1, DM], F32))
        pS1 = esS.enter_context(nc.psum_tensor("pS1", [128, 512], F32))
        pS2 = esS.enter_context(nc.psum_tensor("pS2", [128, 512], F32))

        v0 = ser(sSU, sy, sy.dma_start(pid_sb[:], nc.partition_id_tensor.ap()))
        vg = pGE.inc(ge.iota(dcmp[:, :], pattern=[[1, QT]], base=0, channel_multiplier=-1,
                             allow_small_or_imprecise_dtypes=True))
        wc.wait(ge, pGE, vg)
        vg = pGE.inc(ge.memset(ident_bf[:], 0.0))
        wc.wait(ge, pGE, vg)
        vg = pGE.inc(ge.affine_select(
            out=ident_bf[:], in_=ident_bf[:], compare_op=ALU.not_equal,
            fill=1.0, base=0, pattern=[[-1, 128]], channel_multiplier=1))
        vg2 = pGE.inc(ge.memset(ident_f[:], 0.0))
        wc.wait(ge, pGE, vg2)
        vg2 = pGE.inc(ge.affine_select(
            out=ident_f[:], in_=ident_f[:], compare_op=ALU.not_equal,
            fill=1.0, base=0, pattern=[[-1, 128]], channel_multiplier=1))
        vg3 = pGE.inc(ge.memset(ones1[:], 1.0))
        vg4 = pGE.inc(ge.memset(pad[:], 0.0))

        vq = ser(sSU, sy, sy.dma_start(bv_sb[:, 0:HD], bvq.ap()))
        vq = ser(sSU, sy, sy.dma_start(bv_sb[:, HD:2 * HD], bvk.ap()))
        vq = ser(sSU, sy, sy.dma_start(bv_sb[:, 2 * HD:3 * HD], bvv.ap()))
        wc.wait(se, sSU, vq)
        vs1 = pSE.inc(se.sign(sgn[:, :], bv_sb[:, :]))
        wc.wait(ve, sSU, vq)
        vv = vop(ve.reduce_sum(alpha[:, 0:1], bv_sb[:, 0:HD],
                                   axis=mybir.AxisListType.X, apply_absolute_value=True))
        vv = vop(ve.reduce_sum(alpha[:, 1:2], bv_sb[:, HD:2 * HD],
                                   axis=mybir.AxisListType.X, apply_absolute_value=True))
        vv = vop(ve.reduce_sum(alpha[:, 2:3], bv_sb[:, 2 * HD:3 * HD],
                                   axis=mybir.AxisListType.X, apply_absolute_value=True))
        wc.wait(ve, pSE, vs1)
        vv = vop(ve.tensor_tensor(tmp1[:, :], sgn[:, 0:HD], sgn[:, 0:HD], ALU.mult))
        vv = vop(ve.tensor_scalar(tmp1[:, :], tmp1[:, :], -1.0, 1.0, ALU.mult, ALU.add))
        vv = vop(ve.tensor_tensor(sgn[:, 0:HD], sgn[:, 0:HD], tmp1[:, :], ALU.add))
        vv = vop(ve.tensor_tensor(tmp1[:, :], sgn[:, HD:2 * HD], sgn[:, HD:2 * HD], ALU.mult))
        vv = vop(ve.tensor_scalar(tmp1[:, :], tmp1[:, :], -1.0, 1.0, ALU.mult, ALU.add))
        vv = vop(ve.tensor_tensor(sgn[:, HD:2 * HD], sgn[:, HD:2 * HD], tmp1[:, :], ALU.add))
        vv = vop(ve.tensor_tensor(tmp1[:, :], sgn[:, 2 * HD:3 * HD], sgn[:, 2 * HD:3 * HD], ALU.mult))
        vv = vop(ve.tensor_scalar(tmp1[:, :], tmp1[:, :], -1.0, 1.0, ALU.mult, ALU.add))
        vv = vop(ve.tensor_tensor(sgn[:, 2 * HD:3 * HD], sgn[:, 2 * HD:3 * HD], tmp1[:, :], ALU.add))
        vv = vop(ve.tensor_tensor(mnat[:, :], sgn[:, 0:HD], sgn[:, HD:2 * HD], ALU.mult))
        vv = vop(ve.tensor_tensor(alpha[:, 3:4], alpha[:, 0:1], alpha[:, 1:2], ALU.mult))
        vv = vop(ve.tensor_scalar(alpha[:, 3:4], alpha[:, 3:4], 1.0 / (HD * HD), None, ALU.mult))
        vv = vop(ve.tensor_scalar(mnat[:, :], mnat[:, :], alpha[:, 3:4], None, ALU.mult))
        vv = vop(ve.tensor_scalar(alpha[:, 2:3], alpha[:, 2:3], 1.0 / HD, None, ALU.mult))
        vv = vop(ve.tensor_scalar(sgn[:, 2 * HD:3 * HD], sgn[:, 2 * HD:3 * HD],
                                      alpha[:, 2:3], None, ALU.mult))
        wc.wait(ge, pVE, vv)
        vg5 = ser(sGU, ge, ge.dma_start(mvec.ap(), mnat[:, :]))
        vg5 = ser(sGU, ge, ge.dma_start(vbvec.ap(), sgn[:, 2 * HD:3 * HD]))
        wc.wait(sy, sGU, vg5)
        wc.wait(sy, pGE, vg4)
        v1 = ser(sSU, sy, sy.dma_start(pad[0:R, :], bass.AP(mvec, 0, [[128, R], [1, 128]])))
        wc.wait(te, sSU, v1)
        wc.wait(te, pGE, vg4)
        wc.wait(te, pGE, vg2)
        vt = pTE.inc(te.transpose(pS1[:, 0:128], pad[:, :], ident_f[:]))
        wc.wait(ve, pTE, vt)
        vv = pVE.inc(ve.tensor_copy(m_col[:, :], pS1[:, 0:R]))
        v2 = ser(sSU, sy, sy.dma_start(vbrow[0:1, :], bass.AP(vbvec, 0, [[DM, 1], [1, DM]])))
        wc.wait(te, sSU, v2)
        wc.wait(te, pGE, vg3)
        vt = pTE.inc(te.matmul(pS1[:, :], ones1[0:1, :], vbrow[0:1, 0:512], start=True, stop=True))
        vt = pTE.inc(te.matmul(pS2[:, :], ones1[0:1, :], vbrow[0:1, 512:1024], start=True, stop=True))
        wc.wait(ve, pTE, vt)
        vv = pVE.inc(ve.tensor_copy(vb_rep[:, 0:512], pS1[:, :]))
        vv = pVE.inc(ve.tensor_copy(vb_rep[:, 512:1024], pS2[:, :]))
        for half in range(1, 4):
            wc.wait(te, pVE, vv)
            vt = pTE.inc(te.matmul(pS1[:, :], ones1[0:1, :],
                                   vbrow[0:1, half * 1024:half * 1024 + 512], start=True, stop=True))
            vt = pTE.inc(te.matmul(pS2[:, :], ones1[0:1, :],
                                   vbrow[0:1, half * 1024 + 512:half * 1024 + 1024], start=True, stop=True))
            wc.wait(ve, pTE, vt)
            vv = pVE.inc(ve.tensor_copy(vb_rep[:, half * 1024:half * 1024 + 512], pS1[:, :]))
            vv = pVE.inc(ve.tensor_copy(vb_rep[:, half * 1024 + 512:half * 1024 + 1024], pS2[:, :]))

        for srcb, dst in ((gb, gb_bf), (ub, ub_bf), (db, db_bf)):
            n = srcb.shape[1]
            for c0 in range(0, n, 5504):
                cw = min(5504, n - c0)
                wc.wait(sy, pVE, vv)
                v3 = ser(sSU, sy, sy.dma_start(
                    bcol[0:R, 0:cw], bass.AP(srcb, c0, [[n, R], [1, cw]])))
                wc.wait(ve, sSU, v3)
                wc.wait(ve, sGU, sGU.n)
                vv = pVE.inc(ve.tensor_copy(bcol_b[0:R, 0:cw], bcol[0:R, 0:cw]))
                wc.wait(ge, pVE, vv)
                ser(sGU, ge, ge.dma_start(bass.AP(dst, c0, [[n, R], [1, cw]]), bcol_b[0:R, 0:cw]))
        for srca, dst, nch in ((ga, ag_sb, NC_D), (ua, ua_sb, NC_D), (da, da_sb, NC_I)):
            for h0 in range(0, nch, NC_I // 2):
                hw = min(NC_I // 2, nch - h0)
                wc.wait(sy, pVE, vv)
                v4 = ser(sSU, sy, sy.dma_start(
                    acol[:, 0:hw * R],
                    bass.AP(srca, h0 * 128 * R, [[R, 128], [128 * R, hw], [1, R]])))
                wc.wait(ve, sSU, v4)
                vv = pVE.inc(ve.tensor_copy(dst[:, h0 * R:(h0 + hw) * R], acol[:, 0:hw * R]))

        ji = esS.enter_context(nc.sbuf_tensor("ji", [1, 2], U32))
        jf = esS.enter_context(nc.sbuf_tensor("jf", [1, 2], F32))
        jcol = esS.enter_context(nc.sbuf_tensor("jcol", [128, 1], F32))
        wc.wait(ve, sSU, v0)
        vv = pVE.inc(ve.tensor_scalar(ji[0:1, 0:1], pid_sb[0:1, 0:1], 3, None, ALU.bitwise_and))
        wc.wait(ve, pVE, vv)
        vv = pVE.inc(ve.tensor_copy(jf[0:1, 0:1], ji[0:1, 0:1]))
        wc.wait(ve, pVE, vv)
        vv = pVE.inc(ve.tensor_scalar(jf[0:1, 0:1], jf[0:1, 0:1], float(QT), None, ALU.mult))
        wc.wait(te, pVE, vv)
        wc.wait(te, pGE, vg3)
        vt = pTE.inc(te.matmul(pS1[:, 0:1], ones1[0:1, :], jf[0:1, 0:1], start=True, stop=True))
        wc.wait(ve, pTE, vt)
        vv = pVE.inc(ve.tensor_copy(jcol[:, 0:1], pS1[:, 0:1]))
        wc.wait(ge, pVE, vv)
        vg = pGE.inc(ge.tensor_scalar(dcmp[:, :], dcmp[:, :], jcol[:, 0:1], None, ALU.add))
        vv_sc = 0
        spad_s = pad
        for (scsrc, nrows, off) in ((gsc, NC_I, 0), (usc, NC_I, NC_I), (dsc, NC_D, 2 * NC_I)):
            wc.wait(sy, pGE, vg4)
            wc.wait(sy, pTE, pTE.n)
            v1 = ser(sSU, sy, sy.dma_start(spad_s[0:nrows, :], scsrc.ap()))
            wc.wait(te, sSU, v1)
            wc.wait(te, pVE, pVE.n)
            vt = pTE.inc(te.transpose(pS1[:, 0:128], spad_s[:, :], ident_f[:]))
            wc.wait(ve, pTE, vt)
            vv = vop(ve.tensor_copy(scol[:, off:off + nrows], pS1[:, 0:nrows]))
        vv = vop(ve.tensor_scalar(scol[:, :], scol[:, :], 0.005, None, ALU.mult))
        vv_sc = vop(ve.tensor_scalar(bcl[:, :], scol[:, :], -7.5, None, ALU.mult))
        wc.wait(sy, sSU, v0)
        rj_sy = sy.alloc_register("rj_sy")
        sy.reg_load(rj_sy, pid_sb[0:1, 0:1])
        sy.reg_alu(rj_sy, rj_sy, 3, ALU.bitwise_and)
        sy.reg_alu(rj_sy, rj_sy, QT, ALU.mult)
        rg_sy = sy.alloc_register("rg_sy")
        c1024 = sy.alloc_register("c1024")
        sy.reg_mov(c1024, 1024)
        barrier()

    esWX = ExitStack()
    pXA = esWX.enter_context(nc.psum_tensor("pXA", [128, 2048], F32))
    pSC = [esWX.enter_context(nc.psum_tensor(f"pSC{i}", [128, 512], F32)) for i in range(2)]
    pT = [esWX.enter_context(nc.psum_tensor(f"pT{i}", [128, 1024], BF16)) for i in range(2)]

    pts = {"n": 0, "tr": {}, "ev": {}}

    def pt_fill(src_ap_fn, nch, evac_fn):
        par = pts["n"] % 2
        wc.wait(te, pVE, pts["ev"].get(pts["n"] - 2, 0))
        inst = None
        for ci in range(nch):
            inst = te.transpose(pT[par][:, ci * 128:(ci + 1) * 128], src_ap_fn(ci), ident_bf[:])
        pts["tr"][pts["n"]] = pTE.inc(inst)
        wc.wait(ve, pTE, pts["tr"][pts["n"]])
        pts["ev"][pts["n"]] = pVE.inc(evac_fn(pT[par][:, 0:nch * 128]))
        pts["n"] += 1
        return pts["tr"][pts["n"] - 1], pts["ev"][pts["n"] - 1]

    W_TILES = []
    for oc in range(NC_I):
        W_TILES.append((gq, oc * 128 * DM, DM, oc, wgT, oc * 128 * DM))
    for oc in range(NC_I):
        W_TILES.append((uq, oc * 128 * DM, DM, NC_I + oc, wuT, oc * 128 * DM))
    for oc in range(NC_D):
        for half in range(2):
            W_TILES.append((dq, oc * 128 * II + half * 5504, 5504, 2 * NC_I + oc,
                            wdT, oc * 128 * II + half * 5504))
    wst = {"n": 0, "ld": {}, "cv": {}, "tr": {}, "ev": {}, "st": {}}

    def w_step():
        n = wst["n"]
        NW = len(W_TILES)
        if n >= NW + 2:
            return
        if n < NW:
            srct, soff, cols, sci, dstt, doff = W_TILES[n]
            if n >= 2:
                wc.wait(ge, pSE, wst["cv"][n - 2])
            wst["ld"][n] = wLD[n % 2].inc(ge.dma_start(
                qraw[n % 2][:, 0:cols],
                bass.AP(srct, soff, [[cols if srct is not dq else II, 128], [1, cols]])), 16)
        m = n - 1
        if 0 <= m < NW:
            srct, soff, cols, sci, dstt, doff = W_TILES[m]
            wc.wait(se, wLD[m % 2], wst["ld"][m])
            wc.wait(se, pVE, vv_sc)
            if m >= 2:
                wc.wait(se, pTE, wst["tr"][m - 2])
            wst["cv"][m] = pSE.inc(se.activation(
                wcvt[m % 2][:, 0:cols], qraw[m % 2][:, 0:cols], AF.Identity,
                bias=bcl[:, sci:sci + 1], scale=scol[:, sci:sci + 1]))
        t = n - 2
        if 0 <= t < NW:
            srct, soff, cols, sci, dstt, doff = W_TILES[t]
            nch = cols // 128
            wc.wait(te, pSE, wst["cv"][t])
            ev_last = 0
            for g0 in range(0, nch, 8):
                gn = min(8, nch - g0)
                par = t % 2
                def mksrc(ci, _g0=g0, _par=par):
                    return wcvt[_par][:, (_g0 + ci) * 128:(_g0 + ci + 1) * 128]
                def mkev(ps_ap, _g0=g0, _gn=gn, _par=par, _t=t):
                    if _g0 == 0 and _t >= 2:
                        wc.wait(ve, wST[_t % 2], wst["st"][_t - 2])
                    return ve.tensor_copy(wstage[_par][:, _g0 * 128:(_g0 + _gn) * 128], ps_ap)
                trv, ev_last = pt_fill(mksrc, gn, mkev)
            wst["tr"][t] = pts["tr"][pts["n"] - 1]
            wst["ev"][t] = ev_last
            wc.wait(ge, pVE, ev_last)
            wst["st"][t] = wST[t % 2].inc(ge.dma_start(
                bass.AP(dstt, doff, [[DM if dstt is not wdT else II, 128], [1, cols]]),
                wstage[t % 2][:, 0:cols]), 16)
        wst["n"] += 1

    def w_drain():
        while wst["n"] < len(W_TILES) + 2:
            w_step()

    with ExitStack() as esX:
        xt = [esX.enter_context(nc.sbuf_tensor(f"xt{i}", [128, DM], F32)) for i in range(2)]
        xnt = [esX.enter_context(nc.sbuf_tensor(f"xnt{i}", [128, DM], BF16)) for i in range(2)]
        vt_ = [esX.enter_context(nc.sbuf_tensor(f"vt{i}", [128, DM], BF16)) for i in range(2)]
        fmst = [esX.enter_context(nc.sbuf_tensor(f"fmst{i}", [128, NC_D * 256], BF16)) for i in range(2)]
        stat = esX.enter_context(nc.sbuf_tensor("stat", [128, 16], F32))
        eps_sb = esX.enter_context(nc.sbuf_tensor("eps_sb", [128, 1], F32))
        v_eps = pGE.inc(ge.memset(eps_sb[:], 1e-5))

        NT = SEQ // 128
        ld_x = [0] * NT
        sq_x = [0] * NT
        st_stat = [0] * NT
        rs_x = [0] * NT
        xn_x = [0] * NT
        vv_x = [0] * NT
        tr_x = [0] * NT
        evq_x = [0] * NT
        stv_x = [0] * NT
        stf_x = {}

        def x_front(i):
            par = i % 2
            if i >= 2:
                wc.wait(sy, pSE, sq_x[i - 2])
                wc.wait(sy, pGE, xn_x[i - 2])
            ld_x[i] = xLD[par].inc(sy.dma_start(
                xt[par][:], bass.AP(x_seq, i * 128 * DM, [[DM, 128], [1, DM]])), 16)
            wc.wait(se, xLD[par], ld_x[i])
            se.activation(pXA[:, :], xt[par][:, 0:2048], AF.Square,
                          accum_out=stat[:, 0 + par:1 + par])
            sq_x[i] = pSE.inc(se.activation(pXA[:, :], xt[par][:, 2048:4096], AF.Square,
                                            accum_out=stat[:, 12 + par:13 + par]))
            wc.wait(ve, xLD[par], ld_x[i])
            if i >= 2:
                wc.wait(ve, pGE, xn_x[i - 2])
                wc.wait(ve, pSE, rs_x[i - 2])
            st_stat[i] = vop(ve.reduce_sum(stat[:, 2 + par:3 + par], xt[par][:],
                                           axis=mybir.AxisListType.X, negate=True))

        def x_back(i):
            par = i % 2
            wc.wait(se, pVE, st_stat[i])
            if i >= 2:
                wc.wait(se, pGE, xn_x[i - 2])
            wc.wait(se, pGE, v_eps)
            se.activation(stat[:, 4 + par:5 + par], stat[:, 0 + par:1 + par],
                          AF.Identity, bias=stat[:, 12 + par:13 + par])
            se.activation(stat[:, 2 + par:3 + par], stat[:, 2 + par:3 + par],
                          AF.Identity, scale=1.0 / DM)
            se.activation(stat[:, 6 + par:7 + par], stat[:, 2 + par:3 + par], AF.Square)
            se.activation(stat[:, 6 + par:7 + par], stat[:, 6 + par:7 + par],
                          AF.Identity, scale=-1.0, bias=eps_sb[:, 0:1])
            vse = pSE.inc(se.activation(stat[:, 10 + par:11 + par], stat[:, 4 + par:5 + par],
                                        AF.Ln, scale=1.0 / DM, bias=stat[:, 6 + par:7 + par]))
            wc.wait(se, pSE, vse)
            rs_x[i] = pSE.inc(se.activation(stat[:, 8 + par:9 + par], stat[:, 10 + par:11 + par],
                                            AF.Exp, scale=-0.5))
            wc.wait(ge, pSE, rs_x[i])
            if i >= 2:
                wc.wait(ge, pTE, tr_x[i - 2])
                wc.wait(ge, xVS[par], stv_x[i - 2])
            xn_x[i] = pGE.inc(ge.tensor_scalar(xnt[par][:], xt[par][:],
                                               stat[:, 2 + par:3 + par], stat[:, 8 + par:9 + par],
                                               ALU.add, ALU.mult))
            wc.wait(ge, pGE, xn_x[i])
            vv_x[i] = pGE.inc(ge.tensor_tensor(vt_[par][:], xnt[par][:], vb_rep[:], ALU.mult))
            wc.wait(te, pGE, xn_x[i])
            gpar = (i // 2) % 2
            ev_last = 0
            for f in range(4):
                def mksrc(ci, _f=f, _par=par):
                    return xnt[_par][:, (_f * 8 + ci) * 128:(_f * 8 + ci + 1) * 128]
                def mkev(ps_ap, _f=f, _par=par, _i=i, _gpar=gpar):
                    if _f == 0 and _i % 2 == 0 and _i >= 4:
                        wc.wait(ve, xFS[_gpar], stf_x[_i - 4])
                    return ve.tensor_copy(
                        bass.AP(fmst[_gpar], (_i % 2) * 128 + _f * 8 * 256,
                                [[NC_D * 256, 128], [256, 8], [1, 128]]), ps_ap)
                trv, ev_last = pt_fill(mksrc, 8, mkev)
            tr_x[i] = pts["tr"][pts["n"] - 1]
            evq_x[i] = ev_last
            wc.wait(ge, pGE, vv_x[i])
            stv_x[i] = xVS[par].inc(ge.dma_start(
                bass.AP(vs, i * 128 * DM, [[DM, 128], [1, DM]]), vt_[par][:]), 16)
            if i % 2 == 1:
                g2 = i // 2
                wc.wait(ge, pVE, evq_x[i])
                wc.wait(ge, pVE, evq_x[i - 1])
                stf_x[g2 * 2] = xFS[g2 % 2].inc(ge.dma_start(
                    bass.AP(xnfm, g2 * 256, [[SEQ, 128], [128 * SEQ, NC_D], [1, 256]]),
                    fmst[g2 % 2][:]), 16)

        for i in range(NT + 1):
            if i < NT:
                x_front(i)
            if i >= 1:
                x_back(i - 1)
            if i % 3 == 0:
                w_step()
        barrier()

    esSX.close()

    with ExitStack() as esA:
        q_sb = esA.enter_context(nc.sbuf_tensor("q_sb", [128, NC_D * QT], BF16))
        kt_b = [esA.enter_context(nc.sbuf_tensor(f"kt{i}", [128, 8 * 512], BF16)) for i in range(2)]
        v_b = [esA.enter_context(nc.sbuf_tensor(f"v_b{i}", [128, 8 * 512], BF16)) for i in range(2)]
        at_all = esA.enter_context(nc.sbuf_tensor("at_all", [128, 64 * QT], BF16))
        st_st = [esA.enter_context(nc.sbuf_tensor(f"st_st{i}", [128, 8 * QT], BF16)) for i in range(2)]

        GSZ = 4
        kt_free = [0, 0]
        v_free = [0, 0]
        ev_done = pVE.n
        st_done = {}
        hix = 0
        for g in range(NG):
            trips = 8 * (g + 1)
            if g == 0:
                sy.reg_alu(rg_sy, rj_sy, 0, ALU.add)
            else:
                sy.reg_alu(rg_sy, rg_sy, c1024, ALU.add)
            wc.wait(sy, pTE, pTE.n)
            vq = aQ.inc(sy.dma_start(
                q_sb[:],
                bass.AP(xnfm, rg_sy, [[SEQ, 128], [128 * SEQ, NC_D], [1, QT]])), 16)
            wc.wait(se, aQ, vq)
            for c in range(NC_D):
                inst = se.activation(q_sb[:, c * QT:(c + 1) * QT], q_sb[:, c * QT:(c + 1) * QT],
                                     AF.Identity, scale=m_col[:, c:c + 1])
            qm_done = pSE.inc(inst)

            for h in range(NH):
                v_ld = {}

                def v_load(grp, _h=h):
                    wc.wait(se, pTE, v_free[grp % 2])
                    v_ld[grp] = aV[grp % 2].inc(se.dma_start(
                        v_b[grp % 2][:],
                        bass.AP(vs, grp * GSZ * 128 * DM + _h * 1024,
                                [[DM, 128], [128 * DM, GSZ], [1, 1024]])), 16)
                v_load(0)
                v_load(1)
                sg_v = {}
                mk_v = {}
                kt_ld = {}
                for k in range(trips):
                    grp, st_ = k // GSZ, k % GSZ
                    if st_ == 0:
                        wc.wait(sy, pTE, kt_free[grp % 2])
                        kt_ld[grp] = aKT[grp % 2].inc(sy.dma_start(
                            kt_b[grp % 2][:],
                            bass.AP(xnfm, 8 * h * 128 * SEQ + grp * GSZ * 128,
                                    [[SEQ, 128], [128 * SEQ, 8], [1, GSZ * 128]])), 16)
                    pk = pSC[k % 2]
                    wc.wait(te, aKT[grp % 2], kt_ld[grp])
                    wc.wait(te, pSE, qm_done)
                    if k >= 2:
                        wc.wait(te, pSE, sg_v[k - 2])
                    else:
                        wc.wait(te, pVE, ev_done)
                    for dc in range(8):
                        inst = te.matmul(pk[:, 0:QT],
                                         kt_b[grp % 2][:, dc * 512 + st_ * 128: dc * 512 + (st_ + 1) * 128],
                                         q_sb[:, (8 * h + dc) * QT:(8 * h + dc + 1) * QT],
                                         start=(dc == 0), stop=(dc == 7))
                    sc_v = pTE.inc(inst)
                    kt_free[grp % 2] = sc_v
                    wc.wait(se, pTE, sc_v)
                    sg_v[k] = pSE.inc(se.activation(at_all[:, k * QT:(k + 1) * QT], pk[:, 0:QT],
                                                    AF.Sigmoid, scale=0.125))
                    wc.wait(ve, pSE, sg_v[k])
                    mk_v[k] = pVE.inc(ve.scalar_tensor_tensor(
                        at_all[:, k * QT:(k + 1) * QT], dcmp[:, :],
                        float(128 * k - 1024 * g),
                        at_all[:, k * QT:(k + 1) * QT],
                        ALU.is_ge, ALU.mult))
                    if st_ == GSZ - 1 and (k // GSZ) % 2 == 1:
                        w_step()
                b_last = 0
                for k in range(trips):
                    grp, st_ = k // GSZ, k % GSZ
                    if st_ == 0 and grp >= 2:
                        v_load(grp)
                    wc.wait(te, aV[grp % 2], v_ld[grp])
                    if k == 0:
                        wc.wait(te, pVE, mk_v[trips - 1])
                        wc.wait(te, pVE, ev_done)
                    for dc in range(8):
                        inst = te.matmul(
                            pXA[:, (dc % 4) * 512 + (dc // 4) * QT: (dc % 4) * 512 + (dc // 4) * QT + QT],
                            v_b[grp % 2][:, st_ * 1024 + dc * 128: st_ * 1024 + (dc + 1) * 128],
                            at_all[:, k * QT:(k + 1) * QT],
                            start=(k == 0 and dc < 4), stop=(k == trips - 1))
                    b_last = pTE.inc(inst)
                    v_free[grp % 2] = b_last
                    if st_ == GSZ - 1 and (k // GSZ) % 2 == 0:
                        w_step()
                hpar = hix % 2
                wc.wait(ve, pTE, b_last)
                if hix >= 2:
                    wc.wait(ve, aST[hpar], st_done[hix - 2])
                for dc in range(8):
                    inst = ve.tensor_copy(
                        st_st[hpar][:, dc * QT:(dc + 1) * QT],
                        pXA[:, (dc % 4) * 512 + (dc // 4) * QT: (dc % 4) * 512 + (dc // 4) * QT + QT])
                ev_done = pVE.inc(inst)
                wc.wait(ge, pVE, ev_done)
                st_done[hix] = aST[hpar].inc(ge.dma_start(
                    bass.AP(ss, 8 * h * 128 * TLOC + g * QT,
                            [[TLOC, 128], [128 * TLOC, 8], [1, QT]]),
                    st_st[hpar][:]), 16)
                hix += 1
        w_drain()
        barrier()
    esWX.close()
    esW.close()

    esP = ExitStack()
    pb = [esP.enter_context(nc.psum_tensor(f"pb{i}", [128, 512], F32)) for i in range(8)]

    with ExitStack() as esG:
        xn_res = esG.enter_context(nc.sbuf_tensor("xn_res", [128, NC_D * TLOC], BF16))
        wg_b = [esG.enter_context(nc.sbuf_tensor(f"wg_b{i}", [128, DM], BF16)) for i in range(2)]
        wu_b = [esG.enter_context(nc.sbuf_tensor(f"wu_b{i}", [128, DM], BF16)) for i in range(2)]
        gb_sl = [esG.enter_context(nc.sbuf_tensor(f"gb_sl{i}", [R, 128], BF16)) for i in range(2)]
        ub_sl = [esG.enter_context(nc.sbuf_tensor(f"ub_sl{i}", [R, 128], BF16)) for i in range(2)]
        zg_sb = esG.enter_context(nc.sbuf_tensor("zg_sb", [R, TLOC], BF16))
        zu_sb = esG.enter_context(nc.sbuf_tensor("zu_sb", [R, TLOC], BF16))
        hg = [esG.enter_context(nc.sbuf_tensor(f"hg{i}", [128, 512], BF16)) for i in range(2)]
        hu = [esG.enter_context(nc.sbuf_tensor(f"hu{i}", [128, 512], BF16)) for i in range(2)]
        hst = [esG.enter_context(nc.sbuf_tensor(f"hst{i}", [128, TLOC], BF16)) for i in range(2)]

        xn_ld = 0
        for g in range(NG):
            if g == 0:
                sy.reg_alu(rg_sy, rj_sy, 0, ALU.add)
            else:
                sy.reg_alu(rg_sy, rg_sy, c1024, ALU.add)
            xn_ld = gXN.inc(sy.dma_start(
                bass.AP(xn_res, g * QT, [[NC_D * TLOC, 128], [TLOC, NC_D], [1, QT]]),
                bass.AP(xnfm, rg_sy, [[SEQ, 128], [128 * SEQ, NC_D], [1, QT]])), 16)
        z_ev = 0
        for (asb, zsb) in ((ag_sb, zg_sb), (ua_sb, zu_sb)):
            for sub in range(4):
                pz = pb[6 + sub % 2]
                wc.wait(te, gXN, xn_ld)
                wc.wait(te, pSE, z_ev)
                for c in range(NC_D):
                    inst = te.matmul(pz[0:R, :], asb[:, c * R:(c + 1) * R],
                                     xn_res[:, c * TLOC + sub * 512: c * TLOC + (sub + 1) * 512],
                                     start=(c == 0), stop=(c == NC_D - 1))
                vz = pTE.inc(inst)
                wc.wait(se, pTE, vz)
                z_ev = pSE.inc(se.activation(zsb[0:R, sub * 512:(sub + 1) * 512], pz[0:R, :],
                                             AF.Identity, scale=0.02))

        ld_w = [0] * NC_I
        mm_oc = [0] * NC_I
        sl_v = {}
        ue_v = {}
        hm_v = {}
        hs_v = {}
        for oc in range(NC_I):
            par = oc % 2
            if oc >= 2:
                wc.wait(sy, pTE, mm_oc[oc - 2])
            vg1 = gWG[par].inc(sy.dma_start(wg_b[par][:], bass.AP(wgT, oc * 128 * DM, [[DM, 128], [1, DM]])), 16)
            vu1 = gWU[par].inc(sy.dma_start(wu_b[par][:], bass.AP(wuT, oc * 128 * DM, [[DM, 128], [1, DM]])), 16)
            vb1 = gGB[par].inc(sy.dma_start(gb_sl[par][0:R, :], bass.AP(gb_bf, oc * 128, [[II, R], [1, 128]])), 16)
            vu2 = gUB[par].inc(sy.dma_start(ub_sl[par][0:R, :], bass.AP(ub_bf, oc * 128, [[II, R], [1, 128]])), 16)
            ld_w[oc] = (par, vg1, vu1, vb1, vu2)
            last = 0
            for pr in range(2):
                subs = (2 * pr, 2 * pr + 1)
                pg_ = [pb[4 * pr + 0], pb[4 * pr + 1]]
                pu_ = [pb[4 * pr + 2], pb[4 * pr + 3]]
                _p, _v1, _v2, _v3, _v4 = ld_w[oc]
                wc.wait(te, gWG[_p], _v1)
                wc.wait(te, gWU[_p], _v2)
                wc.wait(te, gGB[_p], _v3)
                wc.wait(te, gUB[_p], _v4)
                wc.wait(te, pSE, z_ev)
                if oc >= 1:
                    for s2 in subs:
                        wc.wait(te, pSE, sl_v[(oc - 1, s2)])
                        wc.wait(te, pVE, ue_v[(oc - 1, s2)])
                for c in range(NC_D):
                    for i2, s2 in enumerate(subs):
                        te.matmul(pg_[i2][:, :], wg_b[par][:, c * 128:(c + 1) * 128],
                                  xn_res[:, c * TLOC + s2 * 512: c * TLOC + (s2 + 1) * 512],
                                  start=(c == 0), stop=False)
                for i2, s2 in enumerate(subs):
                    te.matmul(pg_[i2][:, :], gb_sl[par][0:R, :], zg_sb[0:R, s2 * 512:(s2 + 1) * 512],
                              start=False, stop=True)
                for c in range(NC_D):
                    for i2, s2 in enumerate(subs):
                        te.matmul(pu_[i2][:, :], wu_b[par][:, c * 128:(c + 1) * 128],
                                  xn_res[:, c * TLOC + s2 * 512: c * TLOC + (s2 + 1) * 512],
                                  start=(c == 0), stop=False)
                for i2, s2 in enumerate(subs):
                    inst = te.matmul(pu_[i2][:, :], ub_sl[par][0:R, :], zu_sb[0:R, s2 * 512:(s2 + 1) * 512],
                                     start=False, stop=True)
                last = pTE.inc(inst)
                for i2, s2 in enumerate(subs):
                    wc.wait(se, pTE, last)
                    if oc >= 1 or pr >= 1:
                        wc.wait(se, pGE, hm_v.get((oc, s2 - 2), hm_v.get((oc - 1, s2 + 2), 0)))
                    sl_v[(oc, s2)] = pSE.inc(se.activation(hg[i2][:], pg_[i2][:, :], AF.Silu))
                    wc.wait(ve, pTE, last)
                    if oc >= 1 or pr >= 1:
                        wc.wait(ve, pGE, hm_v.get((oc, s2 - 2), hm_v.get((oc - 1, s2 + 2), 0)))
                    ue_v[(oc, s2)] = pVE.inc(ve.tensor_copy(hu[i2][:], pu_[i2][:, :]))
                    wc.wait(ge, pSE, sl_v[(oc, s2)])
                    wc.wait(ge, pVE, ue_v[(oc, s2)])
                    if oc >= 2 and s2 == 0:
                        wc.wait(ge, gHS[par], hs_v[oc - 2])
                    hm_v[(oc, s2)] = pGE.inc(ge.tensor_tensor(
                        hst[par][:, s2 * 512:(s2 + 1) * 512], hg[i2][:], hu[i2][:], ALU.mult))
            mm_oc[oc] = last
            wc.wait(ge, pGE, hm_v[(oc, 3)])
            hs_v[oc] = gHS[par].inc(ge.dma_start(
                bass.AP(hs, oc * 128 * TLOC, [[TLOC, 128], [1, TLOC]]), hst[par][:]), 16)
        barrier()

    zd_dbg = nc.dram_tensor("d_zd", [2, R, 1024], BF16, kind="ExternalOutput") if dbg else None
    with ExitStack() as esD:
        h_res = esD.enter_context(nc.sbuf_tensor("h_res", [128, NC_I * 1024], BF16))
        wd_b = [esD.enter_context(nc.sbuf_tensor(f"wd_b{i}", [128, 29 * 128], BF16)) for i in range(2)]
        db_sl = [esD.enter_context(nc.sbuf_tensor(f"db_sl{i}", [R, 128], BF16)) for i in range(2)]
        zd_sb = esD.enter_context(nc.sbuf_tensor("zd_sb", [R, 1024], BF16))
        sd_sb = [esD.enter_context(nc.sbuf_tensor(f"sd_sb{i}", [128, 512], BF16)) for i in range(2)]
        s_sl = [esD.enter_context(nc.sbuf_tensor(f"s_sl{i}", [128, 512], BF16)) for i in range(2)]

        for half in range(2):
            h_ld = 0
            wc.wait(sy, pTE, pTE.n)
            for q4 in range(4):
                c0 = (NC_I * q4) // 4
                c1 = (NC_I * (q4 + 1)) // 4
                h_ld = dH.inc(sy.dma_start(
                    bass.AP(h_res, c0 * 1024, [[NC_I * 1024, 128], [1024, c1 - c0], [1, 1024]]),
                    bass.AP(hs, c0 * 128 * TLOC + half * 1024,
                            [[TLOC, 128], [128 * TLOC, c1 - c0], [1, 1024]])), 16)
            z_ev = 0
            for sub in range(2):
                pz = pb[6 + sub]
                wc.wait(te, dH, h_ld)
                wc.wait(te, pSE, z_ev)
                for c in range(NC_I):
                    inst = te.matmul(pz[0:R, :], da_sb[:, c * R:(c + 1) * R],
                                     h_res[:, c * 1024 + sub * 512: c * 1024 + (sub + 1) * 512],
                                     start=(c == 0), stop=(c == NC_I - 1))
                vz = pTE.inc(inst)
                wc.wait(se, pTE, vz)
                z_ev = pSE.inc(se.activation(zd_sb[0:R, sub * 512:(sub + 1) * 512], pz[0:R, :],
                                             AF.Identity, scale=0.02))
            if dbg:
                wc.wait(ge, pSE, z_ev)
                sDBG.inc(ge.dma_start(
                    bass.AP(zd_dbg, half * R * 1024, [[1024, R], [1, 1024]]),
                    zd_sb[0:R, :]), 16)

            ld_seg = {}
            seg_used = {}
            ld_db = {}
            ld_s = {}
            sil_v = {}
            pm_v = {}
            pst_v = {}
            SEGS = [(0, 29), (29, 58), (58, 86)]
            NSEG = 3
            def load_seg(oc, s_):
                pseg = (oc * NSEG + s_) % 2
                wc.wait(sy, pTE, seg_used.get(oc * NSEG + s_ - 2, 0))
                c0, c1 = SEGS[s_]
                ld_seg[oc * NSEG + s_] = dSEG[pseg].inc(sy.dma_start(
                    wd_b[pseg][:, 0:(c1 - c0) * 128],
                    bass.AP(wdT, oc * 128 * II + c0 * 128, [[II, 128], [1, (c1 - c0) * 128]])), 16)

            for oc in range(NC_D):
                load_seg(oc, 0)
                load_seg(oc, 1)
                wc.wait(sy, pTE, seg_used.get((oc - 2) * NSEG + NSEG - 1, 0) if oc >= 2 else 0)
                ld_db[oc] = dDB[oc % 2].inc(sy.dma_start(
                    db_sl[oc % 2][0:R, :], bass.AP(db_bf, oc * 128, [[DM, R], [1, 128]])), 16)
                for sub in range(2):
                    if oc >= 1:
                        wc.wait(sy, pGE, pm_v.get((oc - 1, sub), 0))
                    ld_s[(oc, sub)] = dSL[sub].inc(sy.dma_start(
                        s_sl[sub][:],
                        bass.AP(ss, oc * 128 * TLOC + half * 1024 + sub * 512,
                                [[TLOC, 128], [1, 512]])), 16)
                psd = [pb[(oc % 2) * 2], pb[(oc % 2) * 2 + 1]]
                lastmm = 0
                for s_ in range(NSEG):
                    pseg = (oc * NSEG + s_) % 2
                    c0, c1 = SEGS[s_]
                    wc.wait(te, dSEG[pseg], ld_seg[oc * NSEG + s_])
                    if s_ == NSEG - 1:
                        wc.wait(te, dDB[oc % 2], ld_db[oc])
                    wc.wait(te, pSE, z_ev)
                    inst = None
                    if s_ == 0 and oc >= 2:
                        for sub in range(2):
                            wc.wait(te, pSE, sil_v[(oc - 2, sub)])
                    for ci in range(c1 - c0):
                        c = c0 + ci
                        for sub in range(2):
                            inst = te.matmul(psd[sub][:, :], wd_b[pseg][:, ci * 128:(ci + 1) * 128],
                                             h_res[:, c * 1024 + sub * 512: c * 1024 + (sub + 1) * 512],
                                             start=(s_ == 0 and ci == 0), stop=False)
                    if s_ == NSEG - 1:
                        for sub in range(2):
                            inst = te.matmul(psd[sub][:, :], db_sl[oc % 2][0:R, :],
                                             zd_sb[0:R, sub * 512:(sub + 1) * 512],
                                             start=False, stop=True)
                        lastmm = pTE.inc(inst)
                    if s_ < NSEG - 1:
                        seg_used[oc * NSEG + s_] = pTE.inc(inst)
                        if s_ == 0:
                            load_seg(oc, 2)
                    else:
                        seg_used[oc * NSEG + s_] = lastmm
                for sub in range(2):
                    wc.wait(se, pTE, lastmm)
                    if oc >= 1:
                        wc.wait(se, dPS[sub], pst_v[(oc - 1, sub)])
                    sil_v[(oc, sub)] = pSE.inc(se.activation(sd_sb[sub][:], psd[sub][:, :], AF.Silu))
                    wc.wait(ge, pSE, sil_v[(oc, sub)])
                    wc.wait(ge, dSL[sub], ld_s[(oc, sub)])
                    pm_v[(oc, sub)] = pGE.inc(ge.tensor_tensor(sd_sb[sub][:], sd_sb[sub][:],
                                                               s_sl[sub][:], ALU.mult))
                    wc.wait(ge, pGE, pm_v[(oc, sub)])
                    pst_v[(oc, sub)] = dPS[sub].inc(ge.dma_start(
                        bass.AP(ps_, oc * 128 * TLOC + half * 1024 + sub * 512,
                                [[TLOC, 128], [1, 512]]),
                        sd_sb[sub][:]), 16)
        barrier()

    with ExitStack() as esE:
        ptm = [esE.enter_context(nc.sbuf_tensor(f"ptm{i}", [128, DM], BF16)) for i in range(2)]
        xep = [esE.enter_context(nc.sbuf_tensor(f"xep{i}", [128, DM], F32)) for i in range(2)]
        ot = [esE.enter_context(nc.sbuf_tensor(f"ot{i}", [128, DM], F32)) for i in range(2)]

        NE = TLOC // 128
        ld_p = [0] * NE
        ld_xe = [0] * NE
        ad_v = [0] * NE
        so_v = [0] * NE
        for i in range(NE):
            par = i % 2
            if i >= 2:
                wc.wait(sy, pVE, ad_v[i - 2])
            ld_p[i] = eP[par].inc(sy.dma_start(
                ptm[par][:], bass.AP(ps_, i * 128, [[TLOC, DM], [1, 128]]), transpose=True), 16)
            ld_xe[i] = eX[par].inc(sy.dma_start(
                xep[par][:], bass.AP(x_q, i * 128 * DM, [[DM, 128], [1, DM]])), 16)
            wc.wait(ve, eP[par], ld_p[i])
            wc.wait(ve, eX[par], ld_xe[i])
            if i >= 2:
                wc.wait(ve, eO[par], so_v[i - 2])
            ad_v[i] = pVE.inc(ve.tensor_tensor(ot[par][:], xep[par][:], ptm[par][:], ALU.add))
            wc.wait(ge, pVE, ad_v[i])
            so_v[i] = eO[par].inc(ge.dma_start(
                bass.AP(out_q, i * 128 * DM, [[DM, 128], [1, DM]]), ot[par][:]), 16)
        wc.wait(ge, eO[0], eO[0].n)
        wc.wait(ge, eO[1], eO[1].n)
    esP.close()
    if dbg:
        dbgmap = {
            "d_xnfm": (xnfm, [2, 128, SEQ]),
            "d_vs": (vs, [256, DM]),
            "d_wg": (wgT, [NC_I, 128, DM]),
            "d_wu": (wuT, [NC_I, 128, DM]),
            "d_wd": (wdT, [NC_D, 128, II]),
            "d_ss": (ss, [NC_D, 128, TLOC]),
            "d_hs": (hs, [NC_I, 128, TLOC]),
            "d_ps": (ps_, [DM, TLOC]),
        }
        vstart = pGE.n
        for name, (srct, shp) in dbgmap.items():
            dt_ = nc.dram_tensor(name, shp, BF16, kind="ExternalOutput")
            n_el = int(np.prod(shp))
            ser(sDBG, ge, ge.dma_start(bass.AP(dt_, 0, [[n_el, 1], [1, n_el]]),
                                       bass.AP(srct, 0, [[n_el, 1], [1, n_el]])))
        dt_ = nc.dram_tensor("d_dcmp", [128, QT], F32, kind="ExternalOutput")
        ser(sDBG, ge, ge.dma_start(dt_.ap(), dcmp[:, :]))
        dt_ = nc.dram_tensor("d_mcol", [128, R], F32, kind="ExternalOutput")
        ser(sDBG, ge, ge.dma_start(dt_.ap(), m_col[:, :]))
        wc.wait(ge, sDBG, sDBG.n)
    es0.close()
    return nc


_NC_CACHE = {}


def kernel(**inputs):
    x = np.asarray(inputs["L0_attn_x"])
    n_cores = 8
    in_maps = []
    rowsets = []
    common = {
        "gq": np.ascontiguousarray(inputs["L0_ffn_gate_q"], dtype=np.int32),
        "uq": np.ascontiguousarray(inputs["L0_ffn_up_q"], dtype=np.int32),
        "dq": np.ascontiguousarray(inputs["L0_ffn_down_q"], dtype=np.int32),
        "gsc": np.ascontiguousarray(np.asarray(inputs["L0_ffn_gate_scale"]).reshape(NC_I, 128)),
        "usc": np.ascontiguousarray(np.asarray(inputs["L0_ffn_up_scale"]).reshape(NC_I, 128)),
        "dsc": np.ascontiguousarray(np.asarray(inputs["L0_ffn_down_scale"]).reshape(NC_D, 128)),
        "ga": np.ascontiguousarray(inputs["L0_ffn_gate_A"], dtype=np.float32),
        "gb": np.ascontiguousarray(inputs["L0_ffn_gate_B"], dtype=np.float32),
        "ua": np.ascontiguousarray(inputs["L0_ffn_up_A"], dtype=np.float32),
        "ub": np.ascontiguousarray(inputs["L0_ffn_up_B"], dtype=np.float32),
        "da": np.ascontiguousarray(inputs["L0_ffn_down_A"], dtype=np.float32),
        "db": np.ascontiguousarray(inputs["L0_ffn_down_B"], dtype=np.float32),
        "bvq": np.ascontiguousarray(inputs["L0_attn_bv_q"], dtype=np.float32),
        "bvk": np.ascontiguousarray(inputs["L0_attn_bv_k"], dtype=np.float32),
        "bvv": np.ascontiguousarray(inputs["L0_attn_bv_v"], dtype=np.float32),
    }
    for c in range(n_cores):
        s, j = c // 4, c % 4
        rows = np.concatenate([np.arange(g * 1024 + j * QT, g * 1024 + (j + 1) * QT)
                               for g in range(NG)])
        rowsets.append((s, rows))
        m = dict(common)
        m["x_seq"] = np.ascontiguousarray(x[s], dtype=np.float32)
        m["x_q"] = np.ascontiguousarray(x[s][rows], dtype=np.float32)
        in_maps.append(m)

    if "nc" not in _NC_CACHE:
        _NC_CACHE["nc"] = emit(bass.Bass("TRN2", target_bir_lowering=False))
    nc = _NC_CACHE["nc"]
    res = run_bass_kernel_spmd(nc, in_maps, core_ids=list(range(n_cores)))
    out = np.empty((2, SEQ, DM), dtype=np.float32)
    for c in range(n_cores):
        s, rows = rowsets[c]
        out[s][rows] = res.results[c]["out_q"]
    return out


# version 13 (final)
# speedup vs baseline: 1.1585x; 1.0209x over previous
import numpy as np
import ml_dtypes
import concourse.bass as bass
import concourse.mybir as mybir
from concourse.bass_utils import run_bass_kernel_spmd

F32 = mybir.dt.float32
BF16 = mybir.dt.bfloat16
I32 = mybir.dt.int32
U32 = mybir.dt.uint32
AF = mybir.ActivationFunctionType
ALU = mybir.AluOpType

SEQ = 8192
DM = 4096
NH = 4
HD = 1024
II = 11008
NC_I = II // 128
NC_D = DM // 128
TLOC = 2048
NG = 8
QT = 256
R = 32


class P:
    def __init__(self, sem):
        self.sem = sem
        self.n = 0

    def inc(self, inst, k=1):
        inst.then_inc(self.sem, k)
        self.n += k
        return self.n


class W:
    def __init__(self):
        self.seen = {}

    def wait(self, eng, p, v):
        if v <= 0:
            return
        key = (id(eng), id(p))
        if self.seen.get(key, -1) >= v:
            return
        eng.wait_ge(p.sem, v)
        self.seen[key] = v


def emit(nc, dbg=()):
    x_seq = nc.dram_tensor("x_seq", [SEQ, DM], F32, kind="ExternalInput")
    x_q = nc.dram_tensor("x_q", [TLOC, DM], F32, kind="ExternalInput")
    gq = nc.dram_tensor("gq", [II, DM], I32, kind="ExternalInput")
    uq = nc.dram_tensor("uq", [II, DM], I32, kind="ExternalInput")
    dq = nc.dram_tensor("dq", [DM, II], I32, kind="ExternalInput")
    gsc = nc.dram_tensor("gsc", [NC_I, 128], F32, kind="ExternalInput")
    usc = nc.dram_tensor("usc", [NC_I, 128], F32, kind="ExternalInput")
    dsc = nc.dram_tensor("dsc", [NC_D, 128], F32, kind="ExternalInput")
    ga = nc.dram_tensor("ga", [DM, R], F32, kind="ExternalInput")
    gb = nc.dram_tensor("gb", [R, II], F32, kind="ExternalInput")
    ua = nc.dram_tensor("ua", [DM, R], F32, kind="ExternalInput")
    ub = nc.dram_tensor("ub", [R, II], F32, kind="ExternalInput")
    da = nc.dram_tensor("da", [II, R], F32, kind="ExternalInput")
    db = nc.dram_tensor("db", [R, DM], F32, kind="ExternalInput")
    bvq = nc.dram_tensor("bvq", [NH, HD], F32, kind="ExternalInput")
    bvk = nc.dram_tensor("bvk", [NH, HD], F32, kind="ExternalInput")
    bvv = nc.dram_tensor("bvv", [NH, HD], F32, kind="ExternalInput")
    out_q = nc.dram_tensor("out_q", [TLOC, DM], F32, kind="ExternalOutput")

    wgT = nc.dram_tensor("wgT", [NC_I, 128, DM], BF16, kind="Internal")
    wuT = nc.dram_tensor("wuT", [NC_I, 128, DM], BF16, kind="Internal")
    wdT = nc.dram_tensor("wdT", [NC_D, 128, II], BF16, kind="Internal")
    xnfm = nc.dram_tensor("xnfm", [NC_D, 128, SEQ], BF16, kind="Internal")
    vs = nc.dram_tensor("vs", [SEQ, DM], BF16, kind="Internal")
    hs = nc.dram_tensor("hs", [NC_I, 128, TLOC], BF16, kind="Internal")
    ss = nc.dram_tensor("ss", [NC_D, 128, TLOC], BF16, kind="Internal")
    ps_ = nc.dram_tensor("ps", [DM, TLOC], BF16, kind="Internal")
    mvec = nc.dram_tensor("mvec", [NH, HD], F32, kind="Internal")
    vbvec = nc.dram_tensor("vbvec", [NH, HD], F32, kind="Internal")
    gb_bf = nc.dram_tensor("gb_bf", [R, II], BF16, kind="Internal")
    ub_bf = nc.dram_tensor("ub_bf", [R, II], BF16, kind="Internal")
    db_bf = nc.dram_tensor("db_bf", [R, DM], BF16, kind="Internal")

    sy, te, ve, se, ge = nc.sync, nc.tensor, nc.vector, nc.scalar, nc.gpsimd
    engines = [sy, te, ve, se, ge]

    from contextlib import ExitStack
    es0 = ExitStack()

    all_sems = []

    def newP(name):
        p = P(es0.enter_context(nc.semaphore(name)))
        all_sems.append(p)
        return p

    pSY = newP("pSY")
    pTE = newP("pTE")
    pVE = newP("pVE")
    pSE = newP("pSE")
    pGE = newP("pGE")
    pGD = newP("pGD")
    sSU = newP("sSU")
    sGU = newP("sGU")
    wLD = [newP(f"wLD{i}") for i in range(2)]
    wST = [newP(f"wST{i}") for i in range(2)]
    xLD = [newP(f"xLD{i}") for i in range(2)]
    xVS = [newP(f"xVS{i}") for i in range(2)]
    xFS = [newP(f"xFS{i}") for i in range(2)]
    aQ = newP("aQ")
    aKT = [newP(f"aKT{i}") for i in range(2)]
    aV = [newP(f"aV{i}") for i in range(2)]
    aST = [newP(f"aST{i}") for i in range(2)]
    gXN = newP("gXN")
    gWG = [newP(f"gWG{i}") for i in range(2)]
    gWU = [newP(f"gWU{i}") for i in range(2)]
    gGB = [newP(f"gGB{i}") for i in range(2)]
    gUB = [newP(f"gUB{i}") for i in range(2)]
    gHS = [newP(f"gHS{i}") for i in range(2)]
    dH = newP("dH")
    dSEG = [newP(f"dSEG{i}") for i in range(2)]
    dDB = [newP(f"dDB{i}") for i in range(2)]
    dSL = [newP(f"dSL{i}") for i in range(2)]
    dPS = [newP(f"dPS{i}") for i in range(2)]
    eP = [newP(f"eP{i}") for i in range(2)]
    eX = [newP(f"eX{i}") for i in range(2)]
    eO = [newP(f"eO{i}") for i in range(2)]
    sDBG = newP("sDBG")
    wc = W()

    def ser(p, eng, inst):
        wc.wait(eng, p, p.n)
        return p.inc(inst, 16)

    def vop(inst):
        v = pVE.inc(inst)
        wc.wait(ve, pVE, v)
        return v

    def barrier():
        for eng in engines:
            for p in all_sems:
                wc.wait(eng, p, p.n)

    ident_bf = es0.enter_context(nc.sbuf_tensor("ident_bf", [128, 128], BF16))
    dcmp = es0.enter_context(nc.sbuf_tensor("dcmp", [128, QT], F32))
    m_col = es0.enter_context(nc.sbuf_tensor("m_col", [128, R], F32))
    pid_sb = es0.enter_context(nc.sbuf_tensor("pid_sb", [1, 1], U32))
    ag_sb = es0.enter_context(nc.sbuf_tensor("ag_sb", [128, NC_D * R], BF16))
    ua_sb = es0.enter_context(nc.sbuf_tensor("ua_sb", [128, NC_D * R], BF16))
    da_sb = es0.enter_context(nc.sbuf_tensor("da_sb", [128, NC_I * R], BF16))
    esW = ExitStack()
    qraw = [esW.enter_context(nc.sbuf_tensor(f"qraw{i}", [128, 5504], I32)) for i in range(2)]
    wcvt = [esW.enter_context(nc.sbuf_tensor(f"wcvt{i}", [128, 5504], BF16)) for i in range(2)]
    wstage = [esW.enter_context(nc.sbuf_tensor(f"wstage{i}", [128, 5504], BF16)) for i in range(2)]
    scol = esW.enter_context(nc.sbuf_tensor("scol", [128, 2 * NC_I + NC_D], F32))
    bcl = esW.enter_context(nc.sbuf_tensor("bcl", [128, 2 * NC_I + NC_D], F32))
    esSX = ExitStack()
    vb_rep = esSX.enter_context(nc.sbuf_tensor("vb_rep", [128, DM], BF16))

    with ExitStack() as esS:
        ident_f = esS.enter_context(nc.sbuf_tensor("ident_f", [128, 128], F32))
        bv_sb = esS.enter_context(nc.sbuf_tensor("bv_sb", [NH, 3 * HD], F32))
        sgn = esS.enter_context(nc.sbuf_tensor("sgn", [NH, 3 * HD], F32))
        tmp1 = esS.enter_context(nc.sbuf_tensor("tmp1", [NH, HD], F32))
        alpha = esS.enter_context(nc.sbuf_tensor("alpha", [NH, 4], F32))
        mnat = esS.enter_context(nc.sbuf_tensor("mnat", [NH, HD], F32))
        pad = esS.enter_context(nc.sbuf_tensor("pad", [128, 128], F32))
        bcol = esS.enter_context(nc.sbuf_tensor("bcol", [R, 5504], F32))
        bcol_b = esS.enter_context(nc.sbuf_tensor("bcol_b", [R, 5504], BF16))
        acol = esS.enter_context(nc.sbuf_tensor("acol", [128, NC_I * R // 2], F32))
        ones1 = esS.enter_context(nc.sbuf_tensor("ones1", [1, 128], F32))
        vbrow = esS.enter_context(nc.sbuf_tensor("vbrow", [1, DM], F32))
        pS1 = esS.enter_context(nc.psum_tensor("pS1", [128, 512], F32))
        pS2 = esS.enter_context(nc.psum_tensor("pS2", [128, 512], F32))

        v0 = ser(sSU, sy, sy.dma_start(pid_sb[:], nc.partition_id_tensor.ap()))
        vg = pGE.inc(ge.iota(dcmp[:, :], pattern=[[1, QT]], base=0, channel_multiplier=-1,
                             allow_small_or_imprecise_dtypes=True))
        wc.wait(ge, pGE, vg)
        vg = pGE.inc(ge.memset(ident_bf[:], 0.0))
        wc.wait(ge, pGE, vg)
        vg = pGE.inc(ge.affine_select(
            out=ident_bf[:], in_=ident_bf[:], compare_op=ALU.not_equal,
            fill=1.0, base=0, pattern=[[-1, 128]], channel_multiplier=1))
        vg2 = pGE.inc(ge.memset(ident_f[:], 0.0))
        wc.wait(ge, pGE, vg2)
        vg2 = pGE.inc(ge.affine_select(
            out=ident_f[:], in_=ident_f[:], compare_op=ALU.not_equal,
            fill=1.0, base=0, pattern=[[-1, 128]], channel_multiplier=1))
        vg3 = pGE.inc(ge.memset(ones1[:], 1.0))
        vg4 = pGE.inc(ge.memset(pad[:], 0.0))

        vq = ser(sSU, sy, sy.dma_start(bv_sb[:, 0:HD], bvq.ap()))
        vq = ser(sSU, sy, sy.dma_start(bv_sb[:, HD:2 * HD], bvk.ap()))
        vq = ser(sSU, sy, sy.dma_start(bv_sb[:, 2 * HD:3 * HD], bvv.ap()))
        wc.wait(se, sSU, vq)
        vs1 = pSE.inc(se.sign(sgn[:, :], bv_sb[:, :]))
        wc.wait(ve, sSU, vq)
        vv = vop(ve.reduce_sum(alpha[:, 0:1], bv_sb[:, 0:HD],
                                   axis=mybir.AxisListType.X, apply_absolute_value=True))
        vv = vop(ve.reduce_sum(alpha[:, 1:2], bv_sb[:, HD:2 * HD],
                                   axis=mybir.AxisListType.X, apply_absolute_value=True))
        vv = vop(ve.reduce_sum(alpha[:, 2:3], bv_sb[:, 2 * HD:3 * HD],
                                   axis=mybir.AxisListType.X, apply_absolute_value=True))
        wc.wait(ve, pSE, vs1)
        vv = vop(ve.tensor_tensor(tmp1[:, :], sgn[:, 0:HD], sgn[:, 0:HD], ALU.mult))
        vv = vop(ve.tensor_scalar(tmp1[:, :], tmp1[:, :], -1.0, 1.0, ALU.mult, ALU.add))
        vv = vop(ve.tensor_tensor(sgn[:, 0:HD], sgn[:, 0:HD], tmp1[:, :], ALU.add))
        vv = vop(ve.tensor_tensor(tmp1[:, :], sgn[:, HD:2 * HD], sgn[:, HD:2 * HD], ALU.mult))
        vv = vop(ve.tensor_scalar(tmp1[:, :], tmp1[:, :], -1.0, 1.0, ALU.mult, ALU.add))
        vv = vop(ve.tensor_tensor(sgn[:, HD:2 * HD], sgn[:, HD:2 * HD], tmp1[:, :], ALU.add))
        vv = vop(ve.tensor_tensor(tmp1[:, :], sgn[:, 2 * HD:3 * HD], sgn[:, 2 * HD:3 * HD], ALU.mult))
        vv = vop(ve.tensor_scalar(tmp1[:, :], tmp1[:, :], -1.0, 1.0, ALU.mult, ALU.add))
        vv = vop(ve.tensor_tensor(sgn[:, 2 * HD:3 * HD], sgn[:, 2 * HD:3 * HD], tmp1[:, :], ALU.add))
        vv = vop(ve.tensor_tensor(mnat[:, :], sgn[:, 0:HD], sgn[:, HD:2 * HD], ALU.mult))
        vv = vop(ve.tensor_tensor(alpha[:, 3:4], alpha[:, 0:1], alpha[:, 1:2], ALU.mult))
        vv = vop(ve.tensor_scalar(alpha[:, 3:4], alpha[:, 3:4], 1.0 / (HD * HD), None, ALU.mult))
        vv = vop(ve.tensor_scalar(mnat[:, :], mnat[:, :], alpha[:, 3:4], None, ALU.mult))
        vv = vop(ve.tensor_scalar(alpha[:, 2:3], alpha[:, 2:3], 1.0 / HD, None, ALU.mult))
        vv = vop(ve.tensor_scalar(sgn[:, 2 * HD:3 * HD], sgn[:, 2 * HD:3 * HD],
                                      alpha[:, 2:3], None, ALU.mult))
        wc.wait(ge, pVE, vv)
        vg5 = ser(sGU, ge, ge.dma_start(mvec.ap(), mnat[:, :]))
        vg5 = ser(sGU, ge, ge.dma_start(vbvec.ap(), sgn[:, 2 * HD:3 * HD]))
        wc.wait(sy, sGU, vg5)
        wc.wait(sy, pGE, vg4)
        v1 = ser(sSU, sy, sy.dma_start(pad[0:R, :], bass.AP(mvec, 0, [[128, R], [1, 128]])))
        wc.wait(te, sSU, v1)
        wc.wait(te, pGE, vg4)
        wc.wait(te, pGE, vg2)
        vt = pTE.inc(te.transpose(pS1[:, 0:128], pad[:, :], ident_f[:]))
        wc.wait(ve, pTE, vt)
        vv = pVE.inc(ve.tensor_copy(m_col[:, :], pS1[:, 0:R]))
        v2 = ser(sSU, sy, sy.dma_start(vbrow[0:1, :], bass.AP(vbvec, 0, [[DM, 1], [1, DM]])))
        wc.wait(te, sSU, v2)
        wc.wait(te, pGE, vg3)
        vt = pTE.inc(te.matmul(pS1[:, :], ones1[0:1, :], vbrow[0:1, 0:512], start=True, stop=True))
        vt = pTE.inc(te.matmul(pS2[:, :], ones1[0:1, :], vbrow[0:1, 512:1024], start=True, stop=True))
        wc.wait(ve, pTE, vt)
        vv = pVE.inc(ve.tensor_copy(vb_rep[:, 0:512], pS1[:, :]))
        vv = pVE.inc(ve.tensor_copy(vb_rep[:, 512:1024], pS2[:, :]))
        for half in range(1, 4):
            wc.wait(te, pVE, vv)
            vt = pTE.inc(te.matmul(pS1[:, :], ones1[0:1, :],
                                   vbrow[0:1, half * 1024:half * 1024 + 512], start=True, stop=True))
            vt = pTE.inc(te.matmul(pS2[:, :], ones1[0:1, :],
                                   vbrow[0:1, half * 1024 + 512:half * 1024 + 1024], start=True, stop=True))
            wc.wait(ve, pTE, vt)
            vv = pVE.inc(ve.tensor_copy(vb_rep[:, half * 1024:half * 1024 + 512], pS1[:, :]))
            vv = pVE.inc(ve.tensor_copy(vb_rep[:, half * 1024 + 512:half * 1024 + 1024], pS2[:, :]))

        for srcb, dst in ((gb, gb_bf), (ub, ub_bf), (db, db_bf)):
            n = srcb.shape[1]
            for c0 in range(0, n, 5504):
                cw = min(5504, n - c0)
                wc.wait(sy, pVE, vv)
                v3 = ser(sSU, sy, sy.dma_start(
                    bcol[0:R, 0:cw], bass.AP(srcb, c0, [[n, R], [1, cw]])))
                wc.wait(ve, sSU, v3)
                wc.wait(ve, sGU, sGU.n)
                vv = pVE.inc(ve.tensor_copy(bcol_b[0:R, 0:cw], bcol[0:R, 0:cw]))
                wc.wait(ge, pVE, vv)
                ser(sGU, ge, ge.dma_start(bass.AP(dst, c0, [[n, R], [1, cw]]), bcol_b[0:R, 0:cw]))
        for srca, dst, nch in ((ga, ag_sb, NC_D), (ua, ua_sb, NC_D), (da, da_sb, NC_I)):
            for h0 in range(0, nch, NC_I // 2):
                hw = min(NC_I // 2, nch - h0)
                wc.wait(sy, pVE, vv)
                v4 = ser(sSU, sy, sy.dma_start(
                    acol[:, 0:hw * R],
                    bass.AP(srca, h0 * 128 * R, [[R, 128], [128 * R, hw], [1, R]])))
                wc.wait(ve, sSU, v4)
                vv = pVE.inc(ve.tensor_copy(dst[:, h0 * R:(h0 + hw) * R], acol[:, 0:hw * R]))

        ji = esS.enter_context(nc.sbuf_tensor("ji", [1, 2], U32))
        jf = esS.enter_context(nc.sbuf_tensor("jf", [1, 2], F32))
        jcol = esS.enter_context(nc.sbuf_tensor("jcol", [128, 1], F32))
        wc.wait(ve, sSU, v0)
        vv = pVE.inc(ve.tensor_scalar(ji[0:1, 0:1], pid_sb[0:1, 0:1], 3, None, ALU.bitwise_and))
        wc.wait(ve, pVE, vv)
        vv = pVE.inc(ve.tensor_copy(jf[0:1, 0:1], ji[0:1, 0:1]))
        wc.wait(ve, pVE, vv)
        vv = pVE.inc(ve.tensor_scalar(jf[0:1, 0:1], jf[0:1, 0:1], float(QT), None, ALU.mult))
        wc.wait(te, pVE, vv)
        wc.wait(te, pGE, vg3)
        vt = pTE.inc(te.matmul(pS1[:, 0:1], ones1[0:1, :], jf[0:1, 0:1], start=True, stop=True))
        wc.wait(ve, pTE, vt)
        vv = pVE.inc(ve.tensor_copy(jcol[:, 0:1], pS1[:, 0:1]))
        wc.wait(ge, pVE, vv)
        vg = pGE.inc(ge.tensor_scalar(dcmp[:, :], dcmp[:, :], jcol[:, 0:1], None, ALU.add))
        vv_sc = 0
        spad_s = pad
        for (scsrc, nrows, off) in ((gsc, NC_I, 0), (usc, NC_I, NC_I), (dsc, NC_D, 2 * NC_I)):
            wc.wait(sy, pGE, vg4)
            wc.wait(sy, pTE, pTE.n)
            v1 = ser(sSU, sy, sy.dma_start(spad_s[0:nrows, :], scsrc.ap()))
            wc.wait(te, sSU, v1)
            wc.wait(te, pVE, pVE.n)
            vt = pTE.inc(te.transpose(pS1[:, 0:128], spad_s[:, :], ident_f[:]))
            wc.wait(ve, pTE, vt)
            vv = vop(ve.tensor_copy(scol[:, off:off + nrows], pS1[:, 0:nrows]))
        vv = vop(ve.tensor_scalar(scol[:, :], scol[:, :], 0.005, None, ALU.mult))
        vv_sc = vop(ve.tensor_scalar(bcl[:, :], scol[:, :], -7.5, None, ALU.mult))
        wc.wait(sy, sSU, v0)
        rj_sy = sy.alloc_register("rj_sy")
        sy.reg_load(rj_sy, pid_sb[0:1, 0:1])
        sy.reg_alu(rj_sy, rj_sy, 3, ALU.bitwise_and)
        sy.reg_alu(rj_sy, rj_sy, QT, ALU.mult)
        rg_sy = sy.alloc_register("rg_sy")
        c1024 = sy.alloc_register("c1024")
        sy.reg_mov(c1024, 1024)
        barrier()

    esWX = ExitStack()
    pXA = esWX.enter_context(nc.psum_tensor("pXA", [128, 2048], F32))
    pSC = [esWX.enter_context(nc.psum_tensor(f"pSC{i}", [128, 512], F32)) for i in range(2)]
    pT = [esWX.enter_context(nc.psum_tensor(f"pT{i}", [128, 1024], BF16)) for i in range(2)]

    pts = {"n": 0, "tr": {}, "ev": {}}

    def pt_fill(src_ap_fn, nch, evac_fn):
        par = pts["n"] % 2
        wc.wait(te, pVE, pts["ev"].get(pts["n"] - 2, 0))
        inst = None
        for ci in range(nch):
            inst = te.transpose(pT[par][:, ci * 128:(ci + 1) * 128], src_ap_fn(ci), ident_bf[:])
        pts["tr"][pts["n"]] = pTE.inc(inst)
        wc.wait(ve, pTE, pts["tr"][pts["n"]])
        pts["ev"][pts["n"]] = pVE.inc(evac_fn(pT[par][:, 0:nch * 128]))
        pts["n"] += 1
        return pts["tr"][pts["n"] - 1], pts["ev"][pts["n"] - 1]

    W_TILES = []
    for oc in range(NC_I):
        W_TILES.append((gq, oc * 128 * DM, DM, oc, wgT, oc * 128 * DM))
    for oc in range(NC_I):
        W_TILES.append((uq, oc * 128 * DM, DM, NC_I + oc, wuT, oc * 128 * DM))
    for oc in range(NC_D):
        for half in range(2):
            W_TILES.append((dq, oc * 128 * II + half * 5504, 5504, 2 * NC_I + oc,
                            wdT, oc * 128 * II + half * 5504))
    wst = {"n": 0, "ld": {}, "cv": {}, "tr": {}, "ev": {}, "st": {}}

    def w_step():
        n = wst["n"]
        NW = len(W_TILES)
        if n >= NW + 2:
            return
        if n < NW:
            srct, soff, cols, sci, dstt, doff = W_TILES[n]
            if n >= 2:
                wc.wait(ge, pSE, wst["cv"][n - 2])
            wst["ld"][n] = wLD[n % 2].inc(ge.dma_start(
                qraw[n % 2][:, 0:cols],
                bass.AP(srct, soff, [[cols if srct is not dq else II, 128], [1, cols]])), 16)
        m = n - 1
        if 0 <= m < NW:
            srct, soff, cols, sci, dstt, doff = W_TILES[m]
            wc.wait(se, wLD[m % 2], wst["ld"][m])
            wc.wait(se, pVE, vv_sc)
            if m >= 2:
                wc.wait(se, pTE, wst["tr"][m - 2])
            wst["cv"][m] = pSE.inc(se.activation(
                wcvt[m % 2][:, 0:cols], qraw[m % 2][:, 0:cols], AF.Identity,
                bias=bcl[:, sci:sci + 1], scale=scol[:, sci:sci + 1]))
        t = n - 2
        if 0 <= t < NW:
            srct, soff, cols, sci, dstt, doff = W_TILES[t]
            nch = cols // 128
            wc.wait(te, pSE, wst["cv"][t])
            ev_last = 0
            for g0 in range(0, nch, 8):
                gn = min(8, nch - g0)
                par = t % 2
                def mksrc(ci, _g0=g0, _par=par):
                    return wcvt[_par][:, (_g0 + ci) * 128:(_g0 + ci + 1) * 128]
                def mkev(ps_ap, _g0=g0, _gn=gn, _par=par, _t=t):
                    if _g0 == 0 and _t >= 2:
                        wc.wait(ve, wST[_t % 2], wst["st"][_t - 2])
                    return ve.tensor_copy(wstage[_par][:, _g0 * 128:(_g0 + _gn) * 128], ps_ap)
                trv, ev_last = pt_fill(mksrc, gn, mkev)
            wst["tr"][t] = pts["tr"][pts["n"] - 1]
            wst["ev"][t] = ev_last
            wc.wait(ge, pVE, ev_last)
            wst["st"][t] = wST[t % 2].inc(ge.dma_start(
                bass.AP(dstt, doff, [[DM if dstt is not wdT else II, 128], [1, cols]]),
                wstage[t % 2][:, 0:cols]), 16)
        wst["n"] += 1

    def w_drain():
        while wst["n"] < len(W_TILES) + 2:
            w_step()

    with ExitStack() as esX:
        xt = [esX.enter_context(nc.sbuf_tensor(f"xt{i}", [128, DM], F32)) for i in range(2)]
        xnt = [esX.enter_context(nc.sbuf_tensor(f"xnt{i}", [128, DM], BF16)) for i in range(2)]
        vt_ = [esX.enter_context(nc.sbuf_tensor(f"vt{i}", [128, DM], BF16)) for i in range(2)]
        fmst = [esX.enter_context(nc.sbuf_tensor(f"fmst{i}", [128, NC_D * 256], BF16)) for i in range(2)]
        stat = esX.enter_context(nc.sbuf_tensor("stat", [128, 16], F32))
        eps_sb = esX.enter_context(nc.sbuf_tensor("eps_sb", [128, 1], F32))
        v_eps = pGE.inc(ge.memset(eps_sb[:], 1e-5))

        NT = SEQ // 128
        ld_x = [0] * NT
        sq_x = [0] * NT
        st_stat = [0] * NT
        rs_x = [0] * NT
        xn_x = [0] * NT
        vv_x = [0] * NT
        vvv_x = [0] * NT
        tr_x = [0] * NT
        evq_x = [0] * NT
        stv_x = [0] * NT
        stf_x = {}

        def x_front(i):
            par = i % 2
            if i >= 2:
                wc.wait(sy, pSE, sq_x[i - 2])
                wc.wait(sy, pGE, xn_x[i - 2])
            ld_x[i] = xLD[par].inc(sy.dma_start(
                xt[par][:], bass.AP(x_seq, i * 128 * DM, [[DM, 128], [1, DM]])), 16)
            wc.wait(se, xLD[par], ld_x[i])
            se.activation(pXA[:, :], xt[par][:, 0:2048], AF.Square,
                          accum_out=stat[:, 0 + par:1 + par])
            sq_x[i] = pSE.inc(se.activation(pXA[:, :], xt[par][:, 2048:4096], AF.Square,
                                            accum_out=stat[:, 12 + par:13 + par]))
            wc.wait(ve, xLD[par], ld_x[i])
            if i >= 2:
                wc.wait(ve, pGE, xn_x[i - 2])
                wc.wait(ve, pSE, rs_x[i - 2])
            st_stat[i] = vop(ve.reduce_sum(stat[:, 2 + par:3 + par], xt[par][:],
                                           axis=mybir.AxisListType.X, negate=True))

        def x_back(i):
            par = i % 2
            wc.wait(se, pVE, st_stat[i])
            if i >= 2:
                wc.wait(se, pGE, xn_x[i - 2])
            wc.wait(se, pGE, v_eps)
            se.activation(stat[:, 4 + par:5 + par], stat[:, 0 + par:1 + par],
                          AF.Identity, bias=stat[:, 12 + par:13 + par])
            se.activation(stat[:, 2 + par:3 + par], stat[:, 2 + par:3 + par],
                          AF.Identity, scale=1.0 / DM)
            se.activation(stat[:, 6 + par:7 + par], stat[:, 2 + par:3 + par], AF.Square)
            se.activation(stat[:, 6 + par:7 + par], stat[:, 6 + par:7 + par],
                          AF.Identity, scale=-1.0, bias=eps_sb[:, 0:1])
            vse = pSE.inc(se.activation(stat[:, 10 + par:11 + par], stat[:, 4 + par:5 + par],
                                        AF.Ln, scale=1.0 / DM, bias=stat[:, 6 + par:7 + par]))
            wc.wait(se, pSE, vse)
            rs_x[i] = pSE.inc(se.activation(stat[:, 8 + par:9 + par], stat[:, 10 + par:11 + par],
                                            AF.Exp, scale=-0.5))
            wc.wait(ge, pSE, rs_x[i])
            if i >= 2:
                wc.wait(ge, pTE, tr_x[i - 2])
                wc.wait(ge, pVE, vvv_x[i - 2])
                wc.wait(ge, xVS[par], stv_x[i - 2])
            xn_x[i] = pGE.inc(ge.tensor_scalar(xnt[par][:], xt[par][:],
                                               stat[:, 2 + par:3 + par], stat[:, 8 + par:9 + par],
                                               ALU.add, ALU.mult))
            wc.wait(ge, pGE, xn_x[i])
            vv_x[i] = pGE.inc(ge.tensor_tensor(vt_[par][:, 0:2048], xnt[par][:, 0:2048],
                                               vb_rep[:, 0:2048], ALU.mult))
            wc.wait(ve, pGE, xn_x[i])
            if i >= 2:
                wc.wait(ve, xVS[par], stv_x[i - 2])
            vvv_x[i] = pVE.inc(ve.tensor_tensor(vt_[par][:, 2048:4096], xnt[par][:, 2048:4096],
                                                vb_rep[:, 2048:4096], ALU.mult))
            wc.wait(te, pGE, xn_x[i])
            gpar = (i // 2) % 2
            ev_last = 0
            for f in range(4):
                def mksrc(ci, _f=f, _par=par):
                    return xnt[_par][:, (_f * 8 + ci) * 128:(_f * 8 + ci + 1) * 128]
                def mkev(ps_ap, _f=f, _par=par, _i=i, _gpar=gpar):
                    if _f == 0 and _i % 2 == 0 and _i >= 4:
                        wc.wait(ve, xFS[_gpar], stf_x[_i - 4])
                    return ve.tensor_copy(
                        bass.AP(fmst[_gpar], (_i % 2) * 128 + _f * 8 * 256,
                                [[NC_D * 256, 128], [256, 8], [1, 128]]), ps_ap)
                trv, ev_last = pt_fill(mksrc, 8, mkev)
            tr_x[i] = pts["tr"][pts["n"] - 1]
            evq_x[i] = ev_last
            wc.wait(ge, pGE, vv_x[i])
            wc.wait(ge, pVE, vvv_x[i])
            stv_x[i] = xVS[par].inc(ge.dma_start(
                bass.AP(vs, i * 128 * DM, [[DM, 128], [1, DM]]), vt_[par][:]), 16)
            if i % 2 == 1:
                g2 = i // 2
                wc.wait(ge, pVE, evq_x[i])
                wc.wait(ge, pVE, evq_x[i - 1])
                stf_x[g2 * 2] = xFS[g2 % 2].inc(ge.dma_start(
                    bass.AP(xnfm, g2 * 256, [[SEQ, 128], [128 * SEQ, NC_D], [1, 256]]),
                    fmst[g2 % 2][:]), 16)

        for i in range(NT + 1):
            if i < NT:
                x_front(i)
            if i >= 1:
                x_back(i - 1)
            if i % 3 == 0:
                w_step()
        barrier()

    esSX.close()

    with ExitStack() as esA:
        q_sb = esA.enter_context(nc.sbuf_tensor("q_sb", [128, NC_D * QT], BF16))
        kt_b = [esA.enter_context(nc.sbuf_tensor(f"kt{i}", [128, 8 * 512], BF16)) for i in range(2)]
        v_b = [esA.enter_context(nc.sbuf_tensor(f"v_b{i}", [128, 8 * 512], BF16)) for i in range(2)]
        at_all = esA.enter_context(nc.sbuf_tensor("at_all", [128, 64 * QT], BF16))
        st_st = [esA.enter_context(nc.sbuf_tensor(f"st_st{i}", [128, 8 * QT], BF16)) for i in range(2)]

        GSZ = 4
        kt_free = [0, 0]
        v_free = [0, 0]
        ev_done = pVE.n
        st_done = {}
        hix = 0
        for g in range(NG):
            trips = 8 * (g + 1)
            if g == 0:
                sy.reg_alu(rg_sy, rj_sy, 0, ALU.add)
            else:
                sy.reg_alu(rg_sy, rg_sy, c1024, ALU.add)
            wc.wait(sy, pTE, pTE.n)
            vq = aQ.inc(sy.dma_start(
                q_sb[:],
                bass.AP(xnfm, rg_sy, [[SEQ, 128], [128 * SEQ, NC_D], [1, QT]])), 16)
            wc.wait(se, aQ, vq)
            for c in range(NC_D):
                inst = se.activation(q_sb[:, c * QT:(c + 1) * QT], q_sb[:, c * QT:(c + 1) * QT],
                                     AF.Identity, scale=m_col[:, c:c + 1])
            qm_done = pSE.inc(inst)

            for h in range(NH):
                v_ld = {}

                def v_load(grp, _h=h):
                    wc.wait(se, pTE, v_free[grp % 2])
                    v_ld[grp] = aV[grp % 2].inc(se.dma_start(
                        v_b[grp % 2][:],
                        bass.AP(vs, grp * GSZ * 128 * DM + _h * 1024,
                                [[DM, 128], [128 * DM, GSZ], [1, 1024]])), 16)
                v_load(0)
                v_load(1)
                sg_v = {}
                mk_v = {}
                kt_ld = {}
                for k in range(trips):
                    grp, st_ = k // GSZ, k % GSZ
                    if st_ == 0:
                        wc.wait(sy, pTE, kt_free[grp % 2])
                        kt_ld[grp] = aKT[grp % 2].inc(sy.dma_start(
                            kt_b[grp % 2][:],
                            bass.AP(xnfm, 8 * h * 128 * SEQ + grp * GSZ * 128,
                                    [[SEQ, 128], [128 * SEQ, 8], [1, GSZ * 128]])), 16)
                    pk = pSC[k % 2]
                    wc.wait(te, aKT[grp % 2], kt_ld[grp])
                    wc.wait(te, pSE, qm_done)
                    if k >= 2:
                        wc.wait(te, pSE, sg_v[k - 2])
                    else:
                        wc.wait(te, pVE, ev_done)
                    for dc in range(8):
                        inst = te.matmul(pk[:, 0:QT],
                                         kt_b[grp % 2][:, dc * 512 + st_ * 128: dc * 512 + (st_ + 1) * 128],
                                         q_sb[:, (8 * h + dc) * QT:(8 * h + dc + 1) * QT],
                                         start=(dc == 0), stop=(dc == 7))
                    sc_v = pTE.inc(inst)
                    kt_free[grp % 2] = sc_v
                    wc.wait(se, pTE, sc_v)
                    sg_v[k] = pSE.inc(se.activation(at_all[:, k * QT:(k + 1) * QT], pk[:, 0:QT],
                                                    AF.Sigmoid, scale=0.125))
                    wc.wait(ve, pSE, sg_v[k])
                    mk_v[k] = pVE.inc(ve.scalar_tensor_tensor(
                        at_all[:, k * QT:(k + 1) * QT], dcmp[:, :],
                        float(128 * k - 1024 * g),
                        at_all[:, k * QT:(k + 1) * QT],
                        ALU.is_ge, ALU.mult))
                    if st_ == GSZ - 1 and (k // GSZ) % 2 == 1:
                        w_step()
                b_last = 0
                for k in range(trips):
                    grp, st_ = k // GSZ, k % GSZ
                    if st_ == 0 and grp >= 2:
                        v_load(grp)
                    wc.wait(te, aV[grp % 2], v_ld[grp])
                    if k == 0:
                        wc.wait(te, pVE, mk_v[trips - 1])
                        wc.wait(te, pVE, ev_done)
                    for dc in range(8):
                        inst = te.matmul(
                            pXA[:, (dc % 4) * 512 + (dc // 4) * QT: (dc % 4) * 512 + (dc // 4) * QT + QT],
                            v_b[grp % 2][:, st_ * 1024 + dc * 128: st_ * 1024 + (dc + 1) * 128],
                            at_all[:, k * QT:(k + 1) * QT],
                            start=(k == 0 and dc < 4), stop=(k == trips - 1))
                    b_last = pTE.inc(inst)
                    v_free[grp % 2] = b_last
                    if st_ == GSZ - 1 and (k // GSZ) % 2 == 0:
                        w_step()
                hpar = hix % 2
                wc.wait(ve, pTE, b_last)
                if hix >= 2:
                    wc.wait(ve, aST[hpar], st_done[hix - 2])
                for dc in range(8):
                    inst = ve.tensor_copy(
                        st_st[hpar][:, dc * QT:(dc + 1) * QT],
                        pXA[:, (dc % 4) * 512 + (dc // 4) * QT: (dc % 4) * 512 + (dc // 4) * QT + QT])
                ev_done = pVE.inc(inst)
                wc.wait(ge, pVE, ev_done)
                st_done[hix] = aST[hpar].inc(ge.dma_start(
                    bass.AP(ss, 8 * h * 128 * TLOC + g * QT,
                            [[TLOC, 128], [128 * TLOC, 8], [1, QT]]),
                    st_st[hpar][:]), 16)
                hix += 1
        w_drain()
        barrier()
    esWX.close()
    esW.close()

    esP = ExitStack()
    pb = [esP.enter_context(nc.psum_tensor(f"pb{i}", [128, 512], F32)) for i in range(8)]

    with ExitStack() as esG:
        xn_res = esG.enter_context(nc.sbuf_tensor("xn_res", [128, NC_D * TLOC], BF16))
        wg_b = [esG.enter_context(nc.sbuf_tensor(f"wg_b{i}", [128, DM], BF16)) for i in range(2)]
        wu_b = [esG.enter_context(nc.sbuf_tensor(f"wu_b{i}", [128, DM], BF16)) for i in range(2)]
        gb_sl = [esG.enter_context(nc.sbuf_tensor(f"gb_sl{i}", [R, 128], BF16)) for i in range(2)]
        ub_sl = [esG.enter_context(nc.sbuf_tensor(f"ub_sl{i}", [R, 128], BF16)) for i in range(2)]
        zg_sb = esG.enter_context(nc.sbuf_tensor("zg_sb", [R, TLOC], BF16))
        zu_sb = esG.enter_context(nc.sbuf_tensor("zu_sb", [R, TLOC], BF16))
        hg = [esG.enter_context(nc.sbuf_tensor(f"hg{i}", [128, 512], BF16)) for i in range(2)]
        hu = [esG.enter_context(nc.sbuf_tensor(f"hu{i}", [128, 512], BF16)) for i in range(2)]
        hst = [esG.enter_context(nc.sbuf_tensor(f"hst{i}", [128, TLOC], BF16)) for i in range(2)]

        xn_ld = 0
        for g in range(NG):
            if g == 0:
                sy.reg_alu(rg_sy, rj_sy, 0, ALU.add)
            else:
                sy.reg_alu(rg_sy, rg_sy, c1024, ALU.add)
            xn_ld = gXN.inc(sy.dma_start(
                bass.AP(xn_res, g * QT, [[NC_D * TLOC, 128], [TLOC, NC_D], [1, QT]]),
                bass.AP(xnfm, rg_sy, [[SEQ, 128], [128 * SEQ, NC_D], [1, QT]])), 16)
        z_ev = 0
        for (asb, zsb) in ((ag_sb, zg_sb), (ua_sb, zu_sb)):
            for sub in range(4):
                pz = pb[6 + sub % 2]
                wc.wait(te, gXN, xn_ld)
                wc.wait(te, pSE, z_ev)
                for c in range(NC_D):
                    inst = te.matmul(pz[0:R, :], asb[:, c * R:(c + 1) * R],
                                     xn_res[:, c * TLOC + sub * 512: c * TLOC + (sub + 1) * 512],
                                     start=(c == 0), stop=(c == NC_D - 1))
                vz = pTE.inc(inst)
                wc.wait(se, pTE, vz)
                z_ev = pSE.inc(se.activation(zsb[0:R, sub * 512:(sub + 1) * 512], pz[0:R, :],
                                             AF.Identity, scale=0.02))

        ld_w = [0] * NC_I
        mm_oc = [0] * NC_I
        sl_v = {}
        ue_v = {}
        hm_v = {}
        hs_v = {}
        for oc in range(NC_I):
            par = oc % 2
            if oc >= 2:
                wc.wait(sy, pTE, mm_oc[oc - 2])
            vg1 = gWG[par].inc(sy.dma_start(wg_b[par][:], bass.AP(wgT, oc * 128 * DM, [[DM, 128], [1, DM]])), 16)
            vu1 = gWU[par].inc(sy.dma_start(wu_b[par][:], bass.AP(wuT, oc * 128 * DM, [[DM, 128], [1, DM]])), 16)
            vb1 = gGB[par].inc(sy.dma_start(gb_sl[par][0:R, :], bass.AP(gb_bf, oc * 128, [[II, R], [1, 128]])), 16)
            vu2 = gUB[par].inc(sy.dma_start(ub_sl[par][0:R, :], bass.AP(ub_bf, oc * 128, [[II, R], [1, 128]])), 16)
            ld_w[oc] = (par, vg1, vu1, vb1, vu2)
            last = 0
            for pr in range(2):
                subs = (2 * pr, 2 * pr + 1)
                pg_ = [pb[4 * pr + 0], pb[4 * pr + 1]]
                pu_ = [pb[4 * pr + 2], pb[4 * pr + 3]]
                _p, _v1, _v2, _v3, _v4 = ld_w[oc]
                wc.wait(te, gWG[_p], _v1)
                wc.wait(te, gWU[_p], _v2)
                wc.wait(te, gGB[_p], _v3)
                wc.wait(te, gUB[_p], _v4)
                wc.wait(te, pSE, z_ev)
                if oc >= 1:
                    for s2 in subs:
                        wc.wait(te, pSE, sl_v[(oc - 1, s2)])
                        wc.wait(te, pVE, ue_v[(oc - 1, s2)])
                for c in range(NC_D):
                    for i2, s2 in enumerate(subs):
                        te.matmul(pg_[i2][:, :], wg_b[par][:, c * 128:(c + 1) * 128],
                                  xn_res[:, c * TLOC + s2 * 512: c * TLOC + (s2 + 1) * 512],
                                  start=(c == 0), stop=False)
                for i2, s2 in enumerate(subs):
                    te.matmul(pg_[i2][:, :], gb_sl[par][0:R, :], zg_sb[0:R, s2 * 512:(s2 + 1) * 512],
                              start=False, stop=True)
                for c in range(NC_D):
                    for i2, s2 in enumerate(subs):
                        te.matmul(pu_[i2][:, :], wu_b[par][:, c * 128:(c + 1) * 128],
                                  xn_res[:, c * TLOC + s2 * 512: c * TLOC + (s2 + 1) * 512],
                                  start=(c == 0), stop=False)
                for i2, s2 in enumerate(subs):
                    inst = te.matmul(pu_[i2][:, :], ub_sl[par][0:R, :], zu_sb[0:R, s2 * 512:(s2 + 1) * 512],
                                     start=False, stop=True)
                last = pTE.inc(inst)
                for i2, s2 in enumerate(subs):
                    wc.wait(se, pTE, last)
                    if oc >= 1 or pr >= 1:
                        wc.wait(se, pGE, hm_v.get((oc, s2 - 2), hm_v.get((oc - 1, s2 + 2), 0)))
                    sl_v[(oc, s2)] = pSE.inc(se.activation(hg[i2][:], pg_[i2][:, :], AF.Silu))
                    wc.wait(ve, pTE, last)
                    if oc >= 1 or pr >= 1:
                        wc.wait(ve, pGE, hm_v.get((oc, s2 - 2), hm_v.get((oc - 1, s2 + 2), 0)))
                    ue_v[(oc, s2)] = pVE.inc(ve.tensor_copy(hu[i2][:], pu_[i2][:, :]))
                    wc.wait(ge, pSE, sl_v[(oc, s2)])
                    wc.wait(ge, pVE, ue_v[(oc, s2)])
                    if oc >= 2 and s2 == 0:
                        wc.wait(ge, gHS[par], hs_v[oc - 2])
                    hm_v[(oc, s2)] = pGE.inc(ge.tensor_tensor(
                        hst[par][:, s2 * 512:(s2 + 1) * 512], hg[i2][:], hu[i2][:], ALU.mult))
            mm_oc[oc] = last
            wc.wait(ge, pGE, hm_v[(oc, 3)])
            hs_v[oc] = gHS[par].inc(ge.dma_start(
                bass.AP(hs, oc * 128 * TLOC, [[TLOC, 128], [1, TLOC]]), hst[par][:]), 16)
        barrier()

    zd_dbg = nc.dram_tensor("d_zd", [2, R, 1024], BF16, kind="ExternalOutput") if dbg else None
    with ExitStack() as esD:
        h_res = esD.enter_context(nc.sbuf_tensor("h_res", [128, NC_I * 1024], BF16))
        wd_b = [esD.enter_context(nc.sbuf_tensor(f"wd_b{i}", [128, 29 * 128], BF16)) for i in range(2)]
        db_sl = [esD.enter_context(nc.sbuf_tensor(f"db_sl{i}", [R, 128], BF16)) for i in range(2)]
        zd_sb = esD.enter_context(nc.sbuf_tensor("zd_sb", [R, 1024], BF16))
        sd_sb = [esD.enter_context(nc.sbuf_tensor(f"sd_sb{i}", [128, 512], BF16)) for i in range(2)]
        s_sl = [esD.enter_context(nc.sbuf_tensor(f"s_sl{i}", [128, 512], BF16)) for i in range(2)]

        for half in range(2):
            h_ld = 0
            wc.wait(sy, pTE, pTE.n)
            for q4 in range(4):
                c0 = (NC_I * q4) // 4
                c1 = (NC_I * (q4 + 1)) // 4
                h_ld = dH.inc(sy.dma_start(
                    bass.AP(h_res, c0 * 1024, [[NC_I * 1024, 128], [1024, c1 - c0], [1, 1024]]),
                    bass.AP(hs, c0 * 128 * TLOC + half * 1024,
                            [[TLOC, 128], [128 * TLOC, c1 - c0], [1, 1024]])), 16)
            z_ev = 0
            for sub in range(2):
                pz = pb[6 + sub]
                wc.wait(te, dH, h_ld)
                wc.wait(te, pSE, z_ev)
                for c in range(NC_I):
                    inst = te.matmul(pz[0:R, :], da_sb[:, c * R:(c + 1) * R],
                                     h_res[:, c * 1024 + sub * 512: c * 1024 + (sub + 1) * 512],
                                     start=(c == 0), stop=(c == NC_I - 1))
                vz = pTE.inc(inst)
                wc.wait(se, pTE, vz)
                z_ev = pSE.inc(se.activation(zd_sb[0:R, sub * 512:(sub + 1) * 512], pz[0:R, :],
                                             AF.Identity, scale=0.02))
            if dbg:
                wc.wait(ge, pSE, z_ev)
                sDBG.inc(ge.dma_start(
                    bass.AP(zd_dbg, half * R * 1024, [[1024, R], [1, 1024]]),
                    zd_sb[0:R, :]), 16)

            ld_seg = {}
            seg_used = {}
            ld_db = {}
            ld_s = {}
            sil_v = {}
            pm_v = {}
            pst_v = {}
            SEGS = [(0, 29), (29, 58), (58, 86)]
            NSEG = 3
            def load_seg(oc, s_):
                pseg = (oc * NSEG + s_) % 2
                wc.wait(sy, pTE, seg_used.get(oc * NSEG + s_ - 2, 0))
                c0, c1 = SEGS[s_]
                ld_seg[oc * NSEG + s_] = dSEG[pseg].inc(sy.dma_start(
                    wd_b[pseg][:, 0:(c1 - c0) * 128],
                    bass.AP(wdT, oc * 128 * II + c0 * 128, [[II, 128], [1, (c1 - c0) * 128]])), 16)

            for oc in range(NC_D):
                load_seg(oc, 0)
                load_seg(oc, 1)
                wc.wait(sy, pTE, seg_used.get((oc - 2) * NSEG + NSEG - 1, 0) if oc >= 2 else 0)
                ld_db[oc] = dDB[oc % 2].inc(sy.dma_start(
                    db_sl[oc % 2][0:R, :], bass.AP(db_bf, oc * 128, [[DM, R], [1, 128]])), 16)
                for sub in range(2):
                    if oc >= 1:
                        wc.wait(sy, pGE, pm_v.get((oc - 1, sub), 0))
                    ld_s[(oc, sub)] = dSL[sub].inc(sy.dma_start(
                        s_sl[sub][:],
                        bass.AP(ss, oc * 128 * TLOC + half * 1024 + sub * 512,
                                [[TLOC, 128], [1, 512]])), 16)
                psd = [pb[(oc % 2) * 2], pb[(oc % 2) * 2 + 1]]
                lastmm = 0
                for s_ in range(NSEG):
                    pseg = (oc * NSEG + s_) % 2
                    c0, c1 = SEGS[s_]
                    wc.wait(te, dSEG[pseg], ld_seg[oc * NSEG + s_])
                    if s_ == NSEG - 1:
                        wc.wait(te, dDB[oc % 2], ld_db[oc])
                    wc.wait(te, pSE, z_ev)
                    inst = None
                    if s_ == 0 and oc >= 2:
                        for sub in range(2):
                            wc.wait(te, pSE, sil_v[(oc - 2, sub)])
                    for ci in range(c1 - c0):
                        c = c0 + ci
                        for sub in range(2):
                            inst = te.matmul(psd[sub][:, :], wd_b[pseg][:, ci * 128:(ci + 1) * 128],
                                             h_res[:, c * 1024 + sub * 512: c * 1024 + (sub + 1) * 512],
                                             start=(s_ == 0 and ci == 0), stop=False)
                    if s_ == NSEG - 1:
                        for sub in range(2):
                            inst = te.matmul(psd[sub][:, :], db_sl[oc % 2][0:R, :],
                                             zd_sb[0:R, sub * 512:(sub + 1) * 512],
                                             start=False, stop=True)
                        lastmm = pTE.inc(inst)
                    if s_ < NSEG - 1:
                        seg_used[oc * NSEG + s_] = pTE.inc(inst)
                        if s_ == 0:
                            load_seg(oc, 2)
                    else:
                        seg_used[oc * NSEG + s_] = lastmm
                for sub in range(2):
                    wc.wait(se, pTE, lastmm)
                    if oc >= 1:
                        wc.wait(se, dPS[sub], pst_v[(oc - 1, sub)])
                    sil_v[(oc, sub)] = pSE.inc(se.activation(sd_sb[sub][:], psd[sub][:, :], AF.Silu))
                    wc.wait(ge, pSE, sil_v[(oc, sub)])
                    wc.wait(ge, dSL[sub], ld_s[(oc, sub)])
                    pm_v[(oc, sub)] = pGE.inc(ge.tensor_tensor(sd_sb[sub][:], sd_sb[sub][:],
                                                               s_sl[sub][:], ALU.mult))
                    wc.wait(ge, pGE, pm_v[(oc, sub)])
                    pst_v[(oc, sub)] = dPS[sub].inc(ge.dma_start(
                        bass.AP(ps_, oc * 128 * TLOC + half * 1024 + sub * 512,
                                [[TLOC, 128], [1, 512]]),
                        sd_sb[sub][:]), 16)
        barrier()

    with ExitStack() as esE:
        ptm = [esE.enter_context(nc.sbuf_tensor(f"ptm{i}", [128, DM], BF16)) for i in range(2)]
        xep = [esE.enter_context(nc.sbuf_tensor(f"xep{i}", [128, DM], F32)) for i in range(2)]
        ot = [esE.enter_context(nc.sbuf_tensor(f"ot{i}", [128, DM], F32)) for i in range(2)]

        NE = TLOC // 128
        ld_p = [0] * NE
        ld_xe = [0] * NE
        ad_v = [0] * NE
        so_v = [0] * NE
        for i in range(NE):
            par = i % 2
            if i >= 2:
                wc.wait(sy, pVE, ad_v[i - 2])
            ld_p[i] = eP[par].inc(sy.dma_start(
                ptm[par][:], bass.AP(ps_, i * 128, [[TLOC, DM], [1, 128]]), transpose=True), 16)
            if i >= 2:
                wc.wait(se, pVE, ad_v[i - 2])
            ld_xe[i] = eX[par].inc(se.dma_start(
                xep[par][:], bass.AP(x_q, i * 128 * DM, [[DM, 128], [1, DM]])), 16)
            wc.wait(ve, eP[par], ld_p[i])
            wc.wait(ve, eX[par], ld_xe[i])
            if i >= 2:
                wc.wait(ve, eO[par], so_v[i - 2])
            ad_v[i] = pVE.inc(ve.tensor_tensor(ot[par][:], xep[par][:], ptm[par][:], ALU.add))
            wc.wait(ge, pVE, ad_v[i])
            so_v[i] = eO[par].inc(ge.dma_start(
                bass.AP(out_q, i * 128 * DM, [[DM, 128], [1, DM]]), ot[par][:]), 16)
        wc.wait(ge, eO[0], eO[0].n)
        wc.wait(ge, eO[1], eO[1].n)
    esP.close()
    if dbg:
        dbgmap = {
            "d_xnfm": (xnfm, [2, 128, SEQ]),
            "d_vs": (vs, [256, DM]),
            "d_wg": (wgT, [NC_I, 128, DM]),
            "d_wu": (wuT, [NC_I, 128, DM]),
            "d_wd": (wdT, [NC_D, 128, II]),
            "d_ss": (ss, [NC_D, 128, TLOC]),
            "d_hs": (hs, [NC_I, 128, TLOC]),
            "d_ps": (ps_, [DM, TLOC]),
        }
        vstart = pGE.n
        for name, (srct, shp) in dbgmap.items():
            dt_ = nc.dram_tensor(name, shp, BF16, kind="ExternalOutput")
            n_el = int(np.prod(shp))
            ser(sDBG, ge, ge.dma_start(bass.AP(dt_, 0, [[n_el, 1], [1, n_el]]),
                                       bass.AP(srct, 0, [[n_el, 1], [1, n_el]])))
        dt_ = nc.dram_tensor("d_dcmp", [128, QT], F32, kind="ExternalOutput")
        ser(sDBG, ge, ge.dma_start(dt_.ap(), dcmp[:, :]))
        dt_ = nc.dram_tensor("d_mcol", [128, R], F32, kind="ExternalOutput")
        ser(sDBG, ge, ge.dma_start(dt_.ap(), m_col[:, :]))
        wc.wait(ge, sDBG, sDBG.n)
    es0.close()
    return nc


_NC_CACHE = {}


def kernel(**inputs):
    x = np.asarray(inputs["L0_attn_x"])
    n_cores = 8
    in_maps = []
    rowsets = []
    common = {
        "gq": np.ascontiguousarray(inputs["L0_ffn_gate_q"], dtype=np.int32),
        "uq": np.ascontiguousarray(inputs["L0_ffn_up_q"], dtype=np.int32),
        "dq": np.ascontiguousarray(inputs["L0_ffn_down_q"], dtype=np.int32),
        "gsc": np.ascontiguousarray(np.asarray(inputs["L0_ffn_gate_scale"]).reshape(NC_I, 128)),
        "usc": np.ascontiguousarray(np.asarray(inputs["L0_ffn_up_scale"]).reshape(NC_I, 128)),
        "dsc": np.ascontiguousarray(np.asarray(inputs["L0_ffn_down_scale"]).reshape(NC_D, 128)),
        "ga": np.ascontiguousarray(inputs["L0_ffn_gate_A"], dtype=np.float32),
        "gb": np.ascontiguousarray(inputs["L0_ffn_gate_B"], dtype=np.float32),
        "ua": np.ascontiguousarray(inputs["L0_ffn_up_A"], dtype=np.float32),
        "ub": np.ascontiguousarray(inputs["L0_ffn_up_B"], dtype=np.float32),
        "da": np.ascontiguousarray(inputs["L0_ffn_down_A"], dtype=np.float32),
        "db": np.ascontiguousarray(inputs["L0_ffn_down_B"], dtype=np.float32),
        "bvq": np.ascontiguousarray(inputs["L0_attn_bv_q"], dtype=np.float32),
        "bvk": np.ascontiguousarray(inputs["L0_attn_bv_k"], dtype=np.float32),
        "bvv": np.ascontiguousarray(inputs["L0_attn_bv_v"], dtype=np.float32),
    }
    for c in range(n_cores):
        s, j = c // 4, c % 4
        rows = np.concatenate([np.arange(g * 1024 + j * QT, g * 1024 + (j + 1) * QT)
                               for g in range(NG)])
        rowsets.append((s, rows))
        m = dict(common)
        m["x_seq"] = np.ascontiguousarray(x[s], dtype=np.float32)
        m["x_q"] = np.ascontiguousarray(x[s][rows], dtype=np.float32)
        in_maps.append(m)

    if "nc" not in _NC_CACHE:
        _NC_CACHE["nc"] = emit(bass.Bass("TRN2", target_bir_lowering=False))
    nc = _NC_CACHE["nc"]
    res = run_bass_kernel_spmd(nc, in_maps, core_ids=list(range(n_cores)))
    out = np.empty((2, SEQ, DM), dtype=np.float32)
    for c in range(n_cores):
        s, rows = rowsets[c]
        out[s][rows] = res.results[c]["out_q"]
    return out
